# version 15
# baseline (speedup 1.0000x reference)
import contextlib
import numpy as np
import ml_dtypes
import concourse.bass as bass
import concourse.mybir as mybir
from concourse.bass_utils import run_bass_kernel_spmd
from concourse.alu_op_type import AluOpType as Op
from concourse.bass import AP

F32 = mybir.dt.float32
BF16 = mybir.dt.bfloat16
AF = mybir.ActivationFunctionType
BF = ml_dtypes.bfloat16

D = 2048
T = 2048
NH = 16
DK = DV = 128
KW = 4
CH = 64
NCH = T // CH
SBL = 8
NSB = CH // SBL
NT = D // 128
EPS = 1e-5


class Ser:

    def __init__(self, sem):
        self.sem = sem
        self.count = 0
        self.last = None

    def _wait(self, eng):
        key = eng.engine
        if not (key == self.last and key == mybir.EngineType.PE):
            eng.wait_ge(self.sem, self.count)
        self.last = key

    def dma(self, eng, out, in_, **kw):
        self._wait(eng)
        eng.dma_start(out=out, in_=in_, **kw).then_inc(self.sem, 16)
        self.count += 16

    def op(self, eng, fn, *a, **kw):
        self._wait(eng)
        fn(*a, **kw).then_inc(self.sem, 1)
        self.count += 1



class Ctx:

    def __init__(self, fused=False):
        self.nc = bass.Bass("TRN2", target_bir_lowering=False, debug=False)
        self.fused = fused
        self._dram = {}
        self._semidx = 0
        self.stack = contextlib.ExitStack()
        self.block = None
        self.q = None

    def din(self, name, shape, dt):
        if name not in self._dram:
            self._dram[name] = self.nc.dram_tensor(name, shape, dt, kind="ExternalInput").ap()
        return self._dram[name]

    def dmid(self, name, shape, dt, producer=False):
        if name not in self._dram:
            kind = "Internal" if self.fused else ("ExternalOutput" if producer else "ExternalInput")
            self._dram[name] = self.nc.dram_tensor(name, shape, dt, kind=kind).ap()
        return self._dram[name]

    def dout(self, name, shape, dt):
        if name not in self._dram:
            self._dram[name] = self.nc.dram_tensor(name, shape, dt, kind="ExternalOutput").ap()
        return self._dram[name]

    def begin(self):
        self.block = self.stack.enter_context(self.nc.Block())
        self._new_ser()
        return self

    def _new_ser(self):
        sem = self.stack.enter_context(self.nc.semaphore(f"ss{self._semidx}"))
        self._semidx += 1
        self.q = Ser(sem)

    def phase(self):
        old_sem, old_cnt = self.q.sem, self.q.count
        blk = self.block
        for name in ("sync", "tensor", "vector", "scalar", "gpsimd"):
            @getattr(blk, name)
            def _(eng):
                eng.wait_ge(old_sem, old_cnt)
        self._new_ser()

    def finish(self):
        blk, q = self.block, self.q
        @blk.gpsimd
        def _(gp):
            gp.wait_ge(q.sem, q.count)
        self.stack.close()
        return self.nc


class Stream:
    def __init__(self, sem):
        self.sem = sem
        self.count = 0
        self.last = None


class MSer:

    def __init__(self, stack, nc, names):
        self.streams = {n: Stream(stack.enter_context(nc.semaphore("ms_" + n))) for n in names}
        self.paranoid = False
        self.noskip = False
        self.peonly = False

    def mark(self, name):
        st = self.streams[name]
        return (st.sem, st.count)

    def _waits(self, st, eng, joins):
        key = eng.engine
        skip_ok = (key == st.last and not joins and key == mybir.EngineType.PE)
        if self.noskip or not skip_ok:
            eng.wait_ge(st.sem, st.count)
            for (sem, cnt) in joins:
                if cnt > 0:
                    eng.wait_ge(sem, cnt)
            st.last = key

    def op(self, name, eng, fn, *a, joins=(), **kw):
        st = self.streams[name]
        joins = list(joins)
        if self.paranoid:
            joins += [(s2.sem, s2.count) for n2, s2 in self.streams.items() if n2 != name]
        self._waits(st, eng, list(joins))
        fn(*a, **kw).then_inc(st.sem, 1)
        st.count += 1

    def dma(self, name, eng, out, in_, joins=(), **kw):
        st = self.streams[name]
        joins = list(joins)
        if self.paranoid:
            joins += [(s2.sem, s2.count) for n2, s2 in self.streams.items() if n2 != name]
        self._waits(st, eng, list(joins))
        eng.dma_start(out=out, in_=in_, **kw).then_inc(st.sem, 16)
        st.count += 16



def emit_ln0(ctx, xname="x", outname="xn", pfx="n0"):
    nc, block, q = ctx.nc, ctx.block, ctx.q
    X = ctx.din(xname, [D, T], F32)
    ONESC = ctx.din("ONESC", [128, 1], F32)
    ONESR = ctx.din("ONESR", [1, 128], F32)
    XN = ctx.dmid(outname, [D, T], BF16, producer=True)
    XN32 = ctx.dmid(outname + "32", [D, T], F32, producer=True)
    W = 512
    if True:
        st = contextlib.ExitStack()
        sb = lambda name, shape, dt=F32: st.enter_context(nc.sbuf_tensor(pfx + name, shape, dt))
        ps = lambda name, shape: st.enter_context(nc.psum_tensor(pfx + name, shape, F32))
        xt = sb("xt", [128, NT * W])
        xsq = sb("xsq", [128, W])
        onc = sb("onc", [128, 1])
        onr = sb("onr", [1, 128])
        rows = sb("rows", [1, 6 * W])
        tmp = sb("tmp", [128, W])
        xnt = sb("xnt", [128, W], BF16)
        epst = sb("epst", [1, 1])
        ps_sum = ps("ps_sum", [1, W])
        ps_sq = ps("ps_sq", [1, W])
        ps_b1 = ps("ps_b1", [128, W])
        ps_b2 = ps("ps_b2", [128, W])

        @block.sync
        def _(sync):
            q.dma(sync, onc[:], ONESC)
            q.dma(sync, onr[:], ONESR)

        @block.vector
        def _(v):
            q.op(v, v.memset, epst[:], EPS)

        for cb in range(T // W):
            csl = slice(cb * W, (cb + 1) * W)

            @block.sync
            def _(sync):
                for k in range(NT):
                    q.dma(sync, xt[:, k * W:(k + 1) * W], X[k * 128:(k + 1) * 128, csl])

            for k in range(NT):
                @block.tensor
                def _(t, k=k):
                    q.op(t, t.matmul, ps_sum[:], onc[:], xt[:, k * W:(k + 1) * W],
                         start=(k == 0), stop=(k == NT - 1))

            for k in range(NT):
                @block.scalar
                def _(s, k=k):
                    q.op(s, s.activation, out=xsq[:], in_=xt[:, k * W:(k + 1) * W], func=AF.Square)

                @block.tensor
                def _(t, k=k):
                    q.op(t, t.matmul, ps_sq[:], onc[:], xsq[:],
                         start=(k == 0), stop=(k == NT - 1))

            @block.vector
            def _(v):
                mr, msq, m2, var, rstd, mneg = (rows[0:1, i * W:(i + 1) * W] for i in range(6))
                q.op(v, v.tensor_scalar, out=mr, in0=ps_sum[:], scalar1=1.0 / D, scalar2=None, op0=Op.mult)
                q.op(v, v.tensor_scalar, out=msq, in0=ps_sq[:], scalar1=1.0 / D, scalar2=None, op0=Op.mult)
                q.op(v, v.tensor_tensor, out=m2, in0=mr, in1=mr, op=Op.mult)
                q.op(v, v.tensor_tensor, out=var, in0=msq, in1=m2, op=Op.subtract)

            @block.scalar
            def _(s):
                var = rows[0:1, 3 * W:4 * W]
                q.op(s, s.activation, out=var, in_=var, func=AF.Sqrt, bias=epst[:])

            @block.vector
            def _(v):
                mr, msq, m2, var, rstd, mneg = (rows[0:1, i * W:(i + 1) * W] for i in range(6))
                q.op(v, v.reciprocal, out=rstd, in_=var)
                q.op(v, v.tensor_tensor, out=mneg, in0=mr, in1=rstd, op=Op.mult)
                q.op(v, v.tensor_scalar, out=mneg, in0=mneg, scalar1=-1.0, scalar2=None, op0=Op.mult)

            @block.tensor
            def _(t):
                q.op(t, t.matmul, ps_b1[:], onr[:], rows[0:1, 4 * W:5 * W], start=True, stop=True)
                q.op(t, t.matmul, ps_b2[:], onr[:], rows[0:1, 5 * W:6 * W], start=True, stop=True)

            for k in range(NT):
                @block.vector
                def _(v, k=k):
                    q.op(v, v.tensor_tensor, out=tmp[:], in0=xt[:, k * W:(k + 1) * W], in1=ps_b1[:], op=Op.mult)
                    q.op(v, v.tensor_tensor, out=tmp[:], in0=tmp[:], in1=ps_b2[:], op=Op.add)
                    q.op(v, v.tensor_copy, xnt[:], tmp[:])

                @block.sync
                def _(sync, k=k):
                    q.dma(sync, XN[k * 128:(k + 1) * 128, csl], xnt[:])
                    q.dma(sync, XN32[k * 128:(k + 1) * 128, csl], tmp[:])

        st.close()


def build_ln0():
    ctx = Ctx(fused=False).begin()
    emit_ln0(ctx)
    ctx.finish()
    return ctx.nc



def emit_proj(ctx, pfx="pj"):
    nc, block, q = ctx.nc, ctx.block, ctx.q
    XN = ctx.dmid("xn", [D, T], BF16)
    Ws = {n: ctx.din(n, [D, 2048], F32) for n in ("Wq", "Wk", "Wv", "Wa", "Wg")}
    Wb = ctx.din("Wb", [D, 16], F32)
    CW = {n: ctx.din(n, [2048, 4], F32) for n in ("CQW", "CKW", "CVW")}
    CB = {n: ctx.din(n, [128, 16], F32) for n in ("CQB", "CKB", "CVB")}
    BA = ctx.din("BA", [128, 16], F32)
    BB = ctx.din("BB", [16, 1], F32)
    GAM = ctx.din("GAM", [128, T], F32)
    IDENT = ctx.din("IDENT", [128, 128], F32)

    q_s = ctx.dmid("q_s", [2048, T], BF16, producer=True)
    k_s = ctx.dmid("k_s", [2048, T], BF16, producer=True)
    p1_s = ctx.dmid("p1_s", [2048, T], F32, producer=True)
    af_s = ctx.dmid("af_s", [2048, T], F32, producer=True)
    ia_s = ctx.dmid("ia_s", [2048, T], F32, producer=True)
    g_s = ctx.dmid("g_s", [2048, T], BF16, producer=True)
    bta_s = ctx.dmid("bta_s", [16, T], F32, producer=True)
    eom_s = ctx.dmid("eom_s", [128, NCH * NH], F32, producer=True)
    elc_s = ctx.dmid("elc_s", [128, NCH * NH], F32, producer=True)
    decrows_s = ctx.dmid("decrows_s", [NCH, 2048], F32, producer=True)
    elcrows_s = ctx.dmid("elcrows_s", [NCH, 2048], F32, producer=True)

    st = contextlib.ExitStack()
    sb = lambda name, shape, dt=F32: st.enter_context(nc.sbuf_tensor(pfx + name, shape, dt))
    xnt = sb("xnt", [128, NT * T], BF16)
    wpan = sb("wpan", [128, NT * 128], BF16)
    wbpan = sb("wbpan", [128, NT * 16], BF16)
    zt = sb("zt", [128, T]); tA = sb("tA", [128, T]); tB = sb("tB", [128, T])
    Lt = sb("Lt", [128, T]); lp = sb("lp", [128, T]); p1b = sb("p1b", [128, T])
    cvt = sb("cvt", [128, T]); vst = sb("vst", [128, T]); aft = sb("aft", [128, T])
    iat = sb("iat", [128, T])
    outb = sb("outb", [128, T], BF16)
    gam = sb("gam", [128, T])
    cw = sb("cw", [128, 3 * 64]); cb = sb("cb", [128, 3 * 16])
    bac = sb("bac", [128, 16]); bbc = sb("bbc", [16, 1])
    omt = sb("omt", [128, 32]); eomv = sb("eomv", [128, 32])
    elct = sb("elct", [128, 32]); dect = sb("dect", [128, 32])
    dtr = sb("dtr", [32, 128])
    eomres = sb("eomres", [128, NCH * NH]); elcres = sb("elcres", [128, NCH * NH])
    idf = sb("idf", [128, 128])
    btile = sb("btile", [16, T])
    psm = st.enter_context(nc.psum_tensor(pfx + "psm", [128, 2048], F32))
    psx = st.enter_context(nc.psum_tensor(pfx + "psx", [128, 512], F32))
    with st:
        pass

        @block.sync
        def _(sync):
            for k in range(NT):
                q.dma(sync, xnt[:, k * T:(k + 1) * T], XN[k * 128:(k + 1) * 128, :])
            q.dma(sync, gam[:], GAM)
            q.dma(sync, idf[:], IDENT)
            for i, n in enumerate(("CQW", "CKW", "CVW")):
                q.dma(sync, cw[:, i * 64:(i + 1) * 64].rearrange("p (k j) -> p k j", k=16),
                      CW[n].rearrange("(k p) j -> p k j", p=128))
            for i, n in enumerate(("CQB", "CKB", "CVB")):
                q.dma(sync, cb[:, i * 16:(i + 1) * 16], CB[n])
            q.dma(sync, bac[:], BA)
            q.dma(sync, bbc[:], BB)

        def load_panel(W_, h):
            @block.gpsimd
            def _(gp):
                q.dma(gp, wpan[:].rearrange("p (k c) -> p k c", k=NT),
                      W_.rearrange("(k p) c -> p k c", p=128)[:, :, h * 128:(h + 1) * 128])

        def matmul_tile():
            for tb in range(4):
                for kd in range(NT):
                    @block.tensor
                    def _(t, tb=tb, kd=kd):
                        q.op(t, t.matmul, psm[:, tb * 512:(tb + 1) * 512],
                             wpan[:, kd * 128:(kd + 1) * 128],
                             xnt[:, kd * T + tb * 512: kd * T + (tb + 1) * 512],
                             start=(kd == 0), stop=(kd == NT - 1))

        def conv(ci, h):
            w_j = lambda j: cw[:, ci * 64 + h * 4 + j: ci * 64 + h * 4 + j + 1]

            @block.vector
            def _(v):
                q.op(v, v.tensor_scalar, out=cvt[:], in0=psm[:],
                     scalar1=w_j(3), scalar2=None, op0=Op.mult)
                for sft in range(1, 4):
                    q.op(v, v.scalar_tensor_tensor, out=cvt[:, sft:T],
                         in0=psm[:, 0:T - sft], scalar=w_j(3 - sft),
                         in1=cvt[:, sft:T], op0=Op.mult, op1=Op.add)

        for h in range(NH):
            hsl = slice(h * 128, (h + 1) * 128)
            load_panel(Ws["Wa"], h)
            matmul_tile()

            @block.vector
            def _(v, h=h):
                q.op(v, v.tensor_scalar, out=zt[:], in0=psm[:],
                     scalar1=bac[:, h:h + 1], scalar2=None, op0=Op.add)

            @block.scalar
            def _(s):
                q.op(s, s.activation, out=tA[:], in_=zt[:], func=AF.Abs)
                q.op(s, s.activation, out=tB[:], in_=tA[:], func=AF.Exp, scale=-1.0)
                q.op(s, s.activation, out=tA[:], in_=tB[:], func=AF.Ln, bias=1.0)

            @block.vector
            def _(v):
                q.op(v, v.scalar_tensor_tensor, out=tB[:], in0=zt[:], scalar=0.0,
                     in1=tA[:], op0=Op.min, op1=Op.subtract)
                q.op(v, v.tensor_tensor_scan, out=Lt[:], data0=gam[:], data1=tB[:],
                     initial=0.0, op0=Op.mult, op1=Op.add)
                q.op(v, v.tensor_copy, omt[:], Lt[:, 31:T:64])

            @block.scalar
            def _(s, h=h):
                q.op(s, s.activation, out=eomv[:], in_=omt[:], func=AF.Exp)

            @block.vector
            def _(v, h=h):
                q.op(v, v.tensor_copy, eomres[:, h:NCH * NH:16], eomv[:])
                q.op(v, v.tensor_copy, tA[:].rearrange("p (n t) -> p n t", n=32),
                     omt[:].broadcast_to([128, 32, 64]))
                q.op(v, v.tensor_tensor, out=tB[:], in0=Lt[:], in1=tA[:], op=Op.subtract)
                q.op(v, v.tensor_copy, lp[:, 1:T], tB[:, 0:T - 1])
                q.op(v, v.tensor_scalar, out=lp[:, 0:T:64], in0=omt[:],
                     scalar1=-1.0, scalar2=None, op0=Op.mult)

            @block.scalar
            def _(s):
                q.op(s, s.activation, out=aft[:], in_=tB[:], func=AF.Exp)
                q.op(s, s.activation, out=p1b[:], in_=lp[:], func=AF.Exp, scale=-1.0)
                q.op(s, s.activation, out=tA[:], in_=zt[:], func=AF.Exp, scale=-1.0)

            @block.vector
            def _(v, h=h):
                q.op(v, v.tensor_scalar, out=iat[:], in0=tA[:], scalar1=1.0, scalar2=None, op0=Op.add)
                q.op(v, v.tensor_copy, elct[:], aft[:, 63:T:64])
                q.op(v, v.tensor_copy, elcres[:, h:NCH * NH:16], elct[:])
                q.op(v, v.tensor_tensor, out=dect[:], in0=elct[:], in1=eomv[:], op=Op.mult)

            @block.tensor
            def _(t):
                q.op(t, t.transpose, psx[0:32, 0:128], dect[:], idf[:])

            @block.vector
            def _(v):
                q.op(v, v.tensor_copy, dtr[:], psx[0:32, 0:128])

            @block.sync
            def _(sync, h=h):
                q.dma(sync, decrows_s[:, h * 128:(h + 1) * 128], dtr[:])

            @block.tensor
            def _(t):
                q.op(t, t.transpose, psx[0:32, 0:128], elct[:], idf[:])

            @block.vector
            def _(v):
                q.op(v, v.tensor_copy, dtr[:], psx[0:32, 0:128])

            @block.sync
            def _(sync, h=h):
                q.dma(sync, af_s[hsl, :], aft[:])
                q.dma(sync, ia_s[hsl, :], iat[:])
                q.dma(sync, elcrows_s[:, h * 128:(h + 1) * 128], dtr[:])

            load_panel(Ws["Wv"], h)
            matmul_tile()
            conv(2, h)

            @block.scalar
            def _(s, h=h):
                q.op(s, s.activation, out=vst[:], in_=cvt[:], func=AF.Silu,
                     bias=cb[:, 2 * 16 + h:2 * 16 + h + 1])

            @block.vector
            def _(v):
                q.op(v, v.tensor_tensor, out=tB[:], in0=p1b[:], in1=vst[:], op=Op.mult)

            @block.sync
            def _(sync, h=h):
                q.dma(sync, p1_s[hsl, :], tB[:])

            load_panel(Ws["Wq"], h)
            matmul_tile()
            conv(0, h)

            @block.scalar
            def _(s, h=h):
                q.op(s, s.activation, out=outb[:], in_=cvt[:], func=AF.Silu,
                     bias=cb[:, h:h + 1])

            @block.sync
            def _(sync, h=h):
                q.dma(sync, q_s[hsl, :], outb[:])

            load_panel(Ws["Wk"], h)
            matmul_tile()
            conv(1, h)

            @block.scalar
            def _(s, h=h):
                q.op(s, s.activation, out=vst[:], in_=cvt[:], func=AF.Silu,
                     bias=cb[:, 16 + h:16 + h + 1])

            @block.vector
            def _(v):
                q.op(v, v.tensor_scalar, out=outb[:], in0=vst[:],
                     scalar1=float(DK) ** -0.5, scalar2=None, op0=Op.mult)

            @block.sync
            def _(sync, h=h):
                q.dma(sync, k_s[hsl, :], outb[:])

            load_panel(Ws["Wg"], h)
            matmul_tile()

            @block.scalar
            def _(s):
                q.op(s, s.activation, out=outb[:], in_=psm[:], func=AF.Sigmoid)

            @block.sync
            def _(sync, h=h):
                q.dma(sync, g_s[hsl, :], outb[:])

        @block.gpsimd
        def _(gp):
            q.dma(gp, wbpan[:, 0:NT * 16].rearrange("p (k e) -> p k e", k=NT),
                  Wb.rearrange("(k p) e -> p k e", p=128))

        for tb in range(4):
            for kd in range(NT):
                @block.tensor
                def _(t, tb=tb, kd=kd):
                    q.op(t, t.matmul, psm[0:16, tb * 512:(tb + 1) * 512],
                         wbpan[:, kd * 16:(kd + 1) * 16],
                         xnt[:, kd * T + tb * 512: kd * T + (tb + 1) * 512],
                         start=(kd == 0), stop=(kd == NT - 1))

        @block.scalar
        def _(s):
            q.op(s, s.activation, out=btile[:], in_=psm[0:16, :], func=AF.Sigmoid, bias=bbc[:])

        @block.sync
        def _(sync):
            q.dma(sync, bta_s, btile[:])
            q.dma(sync, eom_s, eomres[:])
            q.dma(sync, elc_s, elcres[:])



def build_proj():
    ctx = Ctx(fused=False).begin()
    emit_proj(ctx)
    ctx.finish()
    return ctx.nc


def emit_delta(ctx, pfx="dl"):
    nc, block, q = ctx.nc, ctx.block, ctx.q
    q_s = ctx.dmid("q_s", [2048, T], BF16)
    k_s = ctx.dmid("k_s", [2048, T], BF16)
    g_s = ctx.dmid("g_s", [2048, T], BF16)
    p1_s = ctx.dmid("p1_s", [2048, T], F32)
    af_s = ctx.dmid("af_s", [2048, T], F32)
    ia_s = ctx.dmid("ia_s", [2048, T], F32)
    bta_s = ctx.dmid("bta_s", [16, T], F32)
    eom_s = ctx.dmid("eom_s", [128, NCH * NH], F32)
    elc_s = ctx.dmid("elc_s", [128, NCH * NH], F32)
    decrows_s = ctx.dmid("decrows_s", [NCH, 2048], F32)
    MBD = ctx.din("MBD", [128, 128], BF16)
    MSL = ctx.din("MSL", [128, 128], BF16)
    MTL = ctx.din("MTL", [128, 128], BF16)
    MBD4 = ctx.din("MBD4", [128, 512], BF16)
    MSL4 = ctx.din("MSL4", [128, 512], BF16)
    MTL4 = ctx.din("MTL4", [128, 512], BF16)
    IDENTB = ctx.din("IDENTB", [128, 128], BF16)
    IDENT = ctx.din("IDENT", [128, 128], F32)
    ONESR = ctx.din("ONESR", [1, 128], F32)
    LNW = ctx.din("LNW", [128, 1], F32)
    LNB = ctx.din("LNB", [128, 1], F32)
    of_s = ctx.dmid("of_s", [2048, T], BF16, producer=True)

    st = contextlib.ExitStack()
    sb = lambda name, shape, dt=F32: st.enter_context(nc.sbuf_tensor(pfx + name, shape, dt))
    Kc = sb("Kc", [128, 1024], BF16); Qc = sb("Qc", [128, 1024], BF16)
    Gc = sb("Gc", [128, 1024], BF16)
    Kord = sb("Kord", [128, 1024], BF16); Qord = sb("Qord", [128, 1024], BF16)
    P1c = sb("P1c", [128, 1024]); Afc = sb("Afc", [128, 1024]); IAc = sb("IAc", [128, 1024])
    iaord = sb("iaord", [128, 1024]); afod = sb("afod", [128, 1024])
    iaTs = sb("iaTs", [128, 1024]); abTs = sb("abTs", [128, 1024])
    nrot = sb("nrot", [128, 1024]); dtot = sb("dtot", [128, 1024])
    tmpw = sb("tmpw", [128, 1024])
    ysb = sb("ysb", [128, 1024], BF16)
    Ytb = sb("Ytb", [128, 1024], BF16); Ytbs = sb("Ytbs", [128, 1024], BF16)
    ktr = sb("ktr", [64, 2048], BF16); yh = sb("yh", [64, 128], BF16)
    ofchunk = sb("ofchunk", [128, 1024]); ofb = sb("ofb", [128, 1024], BF16)
    S_B = sb("S_B", [128, 2048]); S_Bb = sb("S_Bb", [128, 2048], BF16)
    eomres = sb("eomres", [128, NCH * NH]); elcres = sb("elcres", [128, NCH * NH])
    mbd = sb("mbd", [128, 128], BF16); msl = sb("msl", [128, 128], BF16)
    mtl = sb("mtl", [128, 128], BF16); idb = sb("idb", [128, 128], BF16)
    idf = sb("idf", [128, 128]); onr = sb("onr", [1, 128])
    lnw = sb("lnw", [128, 1]); lnb = sb("lnb", [128, 1]); epst = sb("epst", [128, 1])
    brow = sb("brow", [1, 1024])
    decrow = sb("decrow", [1, 2048])
    gmb = sb("gmb", [128, 128], BF16); gdb = sb("gdb", [128, 128], BF16)
    besb = sb("besb", [128, 128], BF16)
    obs = sb("obs", [128, 128]); obsq = sb("obsq", [128, 128])
    ons2 = [sb(f"ons{i}", [128, 128]) for i in range(2)]
    cols = sb("cols", [128, 8])
    ps_big = st.enter_context(nc.psum_tensor(pfx + "ps_big", [128, 1024], F32))
    ps_pair = st.enter_context(nc.psum_tensor(pfx + "ps_pair", [128, 128], F32))
    ps_sol = st.enter_context(nc.psum_tensor(pfx + "ps_sol", [128, 128], F32))
    ps_o2 = st.enter_context(nc.psum_tensor(pfx + "ps_o2", [128, 128], F32))
    ps_tr = st.enter_context(nc.psum_tensor(pfx + "ps_tr", [128, 128], F32))
    ps_trb = st.enter_context(nc.psum_tensor(pfx + "ps_trb", [128, 128], BF16))
    with st:
        pass

        @block.sync
        def _(sync):
            q.dma(sync, mbd[:], MBD); q.dma(sync, msl[:], MSL); q.dma(sync, mtl[:], MTL)
            q.dma(sync, idb[:], IDENTB); q.dma(sync, idf[:], IDENT); q.dma(sync, onr[:], ONESR)
            q.dma(sync, lnw[:], LNW); q.dma(sync, lnb[:], LNB)
            q.dma(sync, eomres[:], eom_s); q.dma(sync, elcres[:], elc_s)

        @block.vector
        def _(v):
            q.op(v, v.memset, epst[:], EPS)
            q.op(v, v.memset, S_B[:], 0.0)
            q.op(v, v.memset, S_Bb[:], 0.0)

        hview = lambda ap: ap.rearrange("(h p) t -> p h t", p=128)

        for n in range(NCH):
            csl = slice(n * CH, (n + 1) * CH)

            @block.sync
            def _(sync, n=n):
                q.dma(sync, Kc[:].rearrange("p (h t) -> p h t", h=NH), hview(k_s)[:, :, csl])
                q.dma(sync, Qc[:].rearrange("p (h t) -> p h t", h=NH), hview(q_s)[:, :, csl])
                q.dma(sync, Gc[:].rearrange("p (h t) -> p h t", h=NH), hview(g_s)[:, :, csl])
                q.dma(sync, P1c[:].rearrange("p (h t) -> p h t", h=NH), hview(p1_s)[:, :, csl])
                q.dma(sync, Afc[:].rearrange("p (h t) -> p h t", h=NH), hview(af_s)[:, :, csl])
                q.dma(sync, IAc[:].rearrange("p (h t) -> p h t", h=NH), hview(ia_s)[:, :, csl])
                q.dma(sync, decrow[:], decrows_s[n:n + 1, :])
                q.dma(sync, brow[:].rearrange("o (h t) -> o h t", h=NH),
                      bta_s.rearrange("(o h) t -> o h t", o=1)[0:1, :, csl])

            @block.vector
            def _(v, n=n):
                q.op(v, v.tensor_scalar, out=brow[:], in0=brow[:], scalar1=-1.0, scalar2=None, op0=Op.mult)

            @block.tensor
            def _(t):
                q.op(t, t.matmul, ps_big[:, 0:512], onr[:], brow[0:1, 0:512], start=True, stop=True)
                q.op(t, t.matmul, ps_big[:, 512:1024], onr[:], brow[0:1, 512:1024], start=True, stop=True)

            @block.vector
            def _(v, n=n):
                q.op(v, v.tensor_tensor, out=IAc[:], in0=IAc[:], in1=ps_big[:], op=Op.mult)
                for src, dst in ((Kc, Kord), (Qc, Qord), (IAc, iaord), (Afc, afod)):
                    q.op(v, v.tensor_copy,
                         dst[:].rearrange("p (s h t) -> p h s t", s=NSB, h=NH),
                         src[:].rearrange("p (h s t) -> p h s t", h=NH, s=NSB))

            for h in range(NH):
                @block.tensor
                def _(t, h=h):
                    q.op(t, t.matmul, ps_big[:, h * 64:(h + 1) * 64],
                         S_Bb[:, h * 128:(h + 1) * 128], Kc[:, h * 64:(h + 1) * 64],
                         start=(h % 8 == 0), stop=(h % 8 == 7))

            @block.vector
            def _(v, n=n):
                for b in range(2):
                    bs = slice(b * 512, (b + 1) * 512)
                    eosl = eomres[:, n * 16 + b * 8: n * 16 + (b + 1) * 8]
                    q.op(v, v.tensor_tensor,
                         out=tmpw[:, bs].rearrange("p (h t) -> p h t", h=8),
                         in0=ps_big[:, bs].rearrange("p (h t) -> p h t", h=8),
                         in1=eosl.broadcast_to([128, 8, 64]), op=Op.mult)
                    q.op(v, v.tensor_tensor,
                         out=nrot[:].rearrange("p (s h t) -> p h s t", s=NSB, h=NH)[:, b * 8:(b + 1) * 8],
                         in0=tmpw[:, bs].rearrange("p (h s t) -> p h s t", h=8, s=NSB),
                         in1=P1c[:].rearrange("p (h s t) -> p h s t", h=NH, s=NSB)[:, b * 8:(b + 1) * 8],
                         op=Op.subtract)

            for h in range(NH):
                @block.tensor
                def _(t, h=h):
                    q.op(t, t.matmul, ps_big[:, h * 64:(h + 1) * 64],
                         S_Bb[:, h * 128:(h + 1) * 128], Qc[:, h * 64:(h + 1) * 64],
                         start=(h % 8 == 0), stop=(h % 8 == 7))

            @block.vector
            def _(v, n=n):
                for b in range(2):
                    bs = slice(b * 512, (b + 1) * 512)
                    eosl = eomres[:, n * 16 + b * 8: n * 16 + (b + 1) * 8]
                    q.op(v, v.tensor_tensor,
                         out=dtot[:].rearrange("p (s h t) -> p h s t", s=NSB, h=NH)[:, b * 8:(b + 1) * 8],
                         in0=ps_big[:, bs].rearrange("p (h s t) -> p h s t", h=8, s=NSB),
                         in1=eosl.broadcast_to([128, 8, 64]).rearrange("p h (s t) -> p h s t", s=NSB),
                         op=Op.mult)

            for s_ in range(NSB):
                for src, dst in ((iaord, iaTs), (afod, abTs)):
                    @block.tensor
                    def _(t, s_=s_, src=src):
                        q.op(t, t.transpose, ps_tr[:], src[:, s_ * 128:(s_ + 1) * 128], idf[:])

                    @block.vector
                    def _(v, s_=s_, dst=dst):
                        q.op(v, v.tensor_copy, dst[:, s_ * 128:(s_ + 1) * 128], ps_tr[:])

            for J in range(NSB):
                jsl = slice(J * 128, (J + 1) * 128)

                @block.tensor
                def _(t, J=J):
                    q.op(t, t.transpose, ps_sol[:], nrot[:, jsl], idf[:])

                for i in range(J):
                    @block.tensor
                    def _(t, i=i, J=J):
                        q.op(t, t.matmul, ps_pair[:], Kord[:, i * 128:(i + 1) * 128],
                             Kord[:, J * 128:(J + 1) * 128], start=True, stop=True)

                    @block.vector
                    def _(v):
                        q.op(v, v.tensor_tensor, out=gmb[:], in0=ps_pair[:], in1=mbd[:], op=Op.mult)

                    @block.tensor
                    def _(t, i=i):
                        q.op(t, t.matmul, ps_sol[:], gmb[:], ysb[:, i * 128:(i + 1) * 128],
                             start=False, stop=False)

                @block.tensor
                def _(t, J=J):
                    q.op(t, t.matmul, ps_pair[:], Kord[:, jsl], Kord[:, jsl], start=True, stop=True)

                @block.vector
                def _(v):
                    q.op(v, v.tensor_tensor, out=gdb[:], in0=ps_pair[:], in1=msl[:], op=Op.mult)

                @block.vector
                def _(v, J=J):
                    q.op(v, v.tensor_tensor, out=ysb[:, jsl], in0=ps_sol[:],
                         in1=iaTs[:, jsl], op=Op.mult)

                @block.scalar
                def _(s):
                    q.op(s, s.copy, out=besb[:], in_=ps_sol[:])

                for r in range(SBL - 1):
                    @block.tensor
                    def _(t, J=J):
                        q.op(t, t.matmul, ps_sol[:], idb[:], besb[:], start=True, stop=False)
                        q.op(t, t.matmul, ps_sol[:], gdb[:], ysb[:, jsl], start=False, stop=True)

                    @block.vector
                    def _(v, J=J):
                        q.op(v, v.tensor_tensor, out=ysb[:, jsl], in0=ps_sol[:],
                             in1=iaTs[:, jsl], op=Op.mult)

                @block.tensor
                def _(t, J=J):
                    q.op(t, t.transpose, ps_o2[:], dtot[:, jsl], idf[:])

                for i in range(J + 1):
                    @block.tensor
                    def _(t, i=i, J=J):
                        q.op(t, t.matmul, ps_pair[:], Kord[:, i * 128:(i + 1) * 128],
                             Qord[:, J * 128:(J + 1) * 128], start=True, stop=True)

                    @block.vector
                    def _(v, i=i, J=J):
                        q.op(v, v.tensor_tensor, out=gmb[:], in0=ps_pair[:],
                             in1=(mtl[:] if i == J else mbd[:]), op=Op.mult)

                    @block.tensor
                    def _(t, i=i, J=J):
                        q.op(t, t.matmul, ps_o2[:], gmb[:], ysb[:, i * 128:(i + 1) * 128],
                             start=False, stop=(i == J))

                @block.vector
                def _(v, J=J):
                    q.op(v, v.tensor_tensor, out=obs[:], in0=ps_o2[:], in1=abTs[:, jsl], op=Op.mult)
                    q.op(v, v.tensor_reduce, out=cols[:, 0:1], in_=obs[:], axis=mybir.AxisListType.X, op=Op.add)
                    q.op(v, v.tensor_scalar, out=cols[:, 0:1], in0=cols[:, 0:1],
                         scalar1=1.0 / DV, scalar2=None, op0=Op.mult)

                @block.scalar
                def _(s):
                    q.op(s, s.activation, out=obsq[:], in_=obs[:], func=AF.Square)

                @block.vector
                def _(v):
                    q.op(v, v.tensor_reduce, out=cols[:, 1:2], in_=obsq[:], axis=mybir.AxisListType.X, op=Op.add)
                    q.op(v, v.tensor_scalar, out=cols[:, 1:2], in0=cols[:, 1:2],
                         scalar1=1.0 / DV, scalar2=None, op0=Op.mult)
                    q.op(v, v.tensor_tensor, out=cols[:, 2:3], in0=cols[:, 0:1], in1=cols[:, 0:1], op=Op.mult)
                    q.op(v, v.tensor_tensor, out=cols[:, 3:4], in0=cols[:, 1:2], in1=cols[:, 2:3], op=Op.subtract)

                @block.scalar
                def _(s):
                    q.op(s, s.activation, out=cols[:, 4:5], in_=cols[:, 3:4], func=AF.Sqrt, bias=epst[:])

                @block.vector
                def _(v):
                    q.op(v, v.reciprocal, out=cols[:, 5:6], in_=cols[:, 4:5])
                    q.op(v, v.tensor_scalar, out=ons[:], in0=obs[:],
                         scalar1=cols[:, 0:1], scalar2=cols[:, 5:6], op0=Op.subtract, op1=Op.mult)

                @block.tensor
                def _(t):
                    q.op(t, t.transpose, ps_tr[:], ons[:], idf[:])

                @block.vector
                def _(v, J=J):
                    q.op(v, v.tensor_scalar,
                         out=ofchunk[:].rearrange("p (h t) -> p h t", h=NH)[:, :, J * 8:(J + 1) * 8],
                         in0=ps_tr[:].rearrange("p (h t) -> p h t", h=NH),
                         scalar1=lnw[:], scalar2=lnb[:], op0=Op.mult, op1=Op.add)

            @block.vector
            def _(v):
                q.op(v, v.tensor_tensor, out=ofb[:], in0=ofchunk[:], in1=Gc[:], op=Op.mult)

            @block.sync
            def _(sync, n=n):
                q.dma(sync, hview(of_s)[:, :, csl], ofb[:].rearrange("p (h t) -> p h t", h=NH))

            for h in range(NH):
                @block.tensor
                def _(t, h=h):
                    q.op(t, t.transpose, ps_trb[0:64, :], Kc[:, h * 64:(h + 1) * 64], idb[:])

                @block.vector
                def _(v, h=h):
                    q.op(v, v.tensor_copy, ktr[:, h * 128:(h + 1) * 128], ps_trb[0:64, :])

            for s_ in range(NSB):
                @block.tensor
                def _(t, s_=s_):
                    q.op(t, t.transpose, ps_trb[:], ysb[:, s_ * 128:(s_ + 1) * 128], idb[:])

                @block.vector
                def _(v, s_=s_):
                    q.op(v, v.tensor_copy,
                         Ytb[:].rearrange("p (h t) -> p h t", h=NH)[:, :, s_ * 8:(s_ + 1) * 8],
                         ps_trb[:].rearrange("p (h t) -> p h t", h=NH))

            @block.vector
            def _(v, n=n):
                q.op(v, v.tensor_tensor, out=Ytbs[:].rearrange("p (h t) -> p h t", h=NH),
                     in0=Ytb[:].rearrange("p (h t) -> p h t", h=NH),
                     in1=elcres[:, n * 16:(n + 1) * 16].broadcast_to([128, 16, 64]), op=Op.mult)

            for hg in range(4):
                for hh in range(4):
                    h = hg * 4 + hh

                    @block.tensor
                    def _(t, h=h):
                        q.op(t, t.transpose, ps_trb[0:64, :], Ytbs[:, h * 64:(h + 1) * 64], idb[:])

                    @block.vector
                    def _(v):
                        q.op(v, v.tensor_copy, yh[:], ps_trb[0:64, :])

                    @block.tensor
                    def _(t, h=h, hh=hh):
                        q.op(t, t.matmul, ps_big[:, hh * 128:(hh + 1) * 128],
                             ktr[:, h * 128:(h + 1) * 128], yh[:],
                             start=(hh == 0), stop=(hh == 3))

                @block.tensor
                def _(t, hg=hg):
                    q.op(t, t.matmul, ps_big[:, 512:1024], onr[:],
                         decrow[0:1, hg * 512:(hg + 1) * 512], start=True, stop=True)

                @block.vector
                def _(v, hg=hg):
                    ssl = slice(hg * 512, (hg + 1) * 512)
                    q.op(v, v.tensor_tensor, out=S_B[:, ssl], in0=S_B[:, ssl],
                         in1=ps_big[:, 512:1024], op=Op.mult)
                    q.op(v, v.tensor_tensor, out=S_B[:, ssl], in0=S_B[:, ssl],
                         in1=ps_big[:, 0:512], op=Op.add)

            @block.vector
            def _(v):
                q.op(v, v.tensor_copy, S_Bb[:], S_B[:])



def build_delta():
    ctx = Ctx(fused=False).begin()
    emit_delta(ctx)
    ctx.finish()
    return ctx.nc


def emit_l0mix(ctx, pfx="m0"):
    nc, block, q = ctx.nc, ctx.block, ctx.q
    X = ctx.din("x", [D, T], F32)
    XN = ctx.dmid("xn", [D, T], BF16)
    XN32 = ctx.dmid("xn32", [D, T], F32)
    OF = ctx.dmid("of_s", [2048, T], BF16)
    Wo = ctx.din("Wo", [2048, 2048], F32)
    WG8 = ctx.din("WG8", [D, 8], F32)
    WET = ctx.din("WET", [8, D, 2048], F32)
    BE = ctx.din("BE", [8, 2048], F32)
    IDENT = ctx.din("IDENT", [128, 128], F32)
    IDENTB = ctx.din("IDENTB", [128, 128], BF16)
    ONESRB = ctx.din("ONESRB", [1, 128], BF16)
    X1 = ctx.dmid("x1_s", [D, T], F32, producer=True)
    xe_s = ctx.dmid("xe_s_" + pfx, [8, D, T], BF16, producer=True) if not ctx.fused else ctx.dmid("xe_s", [8, D, T], BF16)

    st = contextlib.ExitStack()
    sb = lambda name, shape, dt=F32: st.enter_context(nc.sbuf_tensor(pfx + name, shape, dt))
    lg = sb("lg", [128, 8]); ex = sb("ex", [128, 8]); mh = sb("mh", [128, 8])
    colsb = sb("colsb", [128, 8])
    gmb = sb("gmb", [128, 8], BF16)
    gRow = sb("gRow", [1, 8 * T], BF16)
    onrb = sb("onrb", [1, 128], BF16)
    idb = sb("idb", [128, 128], BF16)
    wgp = sb("wgp", [128, NT * 8])
    x32w = sb("x32w", [128, 8 * 128])
    big = sb("big", [128, 8 * NT * 512], BF16)
    xes = big
    oft = big
    wep = sb("wep", [128, 4 * NT * 128], BF16)
    xnt = wep
    beb = sb("beb", [8, 2048], BF16)
    gT8 = sb("gT8", [8, T], BF16)
    xt = sb("xt", [128, 512], F32)
    x1t = sb("x1t", [128, 512], F32)
    outt = sb("outt", [128, 512], F32)
    ps_out = st.enter_context(nc.psum_tensor(pfx + "ps_out", [128, 512], F32))
    ps_g = st.enter_context(nc.psum_tensor(pfx + "ps_g", [128, 512], F32))
    ps_lg = st.enter_context(nc.psum_tensor(pfx + "ps_lg", [128, 8], F32))
    ps_tb = st.enter_context(nc.psum_tensor(pfx + "ps_tb", [128, 128], BF16))
    with st:
        pass

        @block.sync
        def _(sync):
            q.dma(sync, onrb[:], ONESRB)
            q.dma(sync, idb[:], IDENTB)

        @block.sync
        def _(sync):
            q.dma(sync, wgp[:].rearrange("p (k e) -> p k e", k=NT),
                  WG8.rearrange("(k p) e -> p k e", p=128))

        @block.gpsimd
        def _(gp):
            q.dma(gp, beb[:], BE)

        for tt in range(NT):
            for half in range(2):
                @block.sync
                def _(sync, tt=tt, half=half):
                    q.dma(sync, x32w[:].rearrange("p (kd t) -> p kd t", kd=8),
                          XN32.rearrange("(kd p) t -> p kd t", p=128)[:, half * 8:(half + 1) * 8,
                                                                     tt * 128:(tt + 1) * 128])

                for kd8 in range(8):
                    @block.tensor
                    def _(t, kd8=kd8, half=half):
                        kd = half * 8 + kd8
                        q.op(t, t.matmul, ps_lg[:], x32w[:, kd8 * 128:(kd8 + 1) * 128],
                             wgp[:, kd * 8:(kd + 1) * 8], start=(kd == 0), stop=(kd == NT - 1))

            @block.vector
            def _(v):
                q.op(v, v.tensor_copy, lg[:], ps_lg[:])
                q.op(v, v.tensor_reduce, out=colsb[:, 0:1], in_=lg[:], axis=mybir.AxisListType.X, op=Op.max)
                q.op(v, v.tensor_scalar, out=colsb[:, 1:2], in0=colsb[:, 0:1],
                     scalar1=-1.0, scalar2=None, op0=Op.mult)

            @block.scalar
            def _(s):
                q.op(s, s.activation, out=ex[:], in_=lg[:], func=AF.Exp, bias=colsb[:, 1:2])

            @block.vector
            def _(v):
                q.op(v, v.tensor_reduce, out=colsb[:, 2:3], in_=ex[:], axis=mybir.AxisListType.X, op=Op.add)
                q.op(v, v.reciprocal, out=colsb[:, 3:4], in_=colsb[:, 2:3])
                q.op(v, v.tensor_tensor, out=mh[:], in0=lg[:],
                     in1=colsb[:, 0:1].broadcast_to([128, 8]), op=Op.is_equal)
                q.op(v, v.tensor_copy, colsb[:, 4:5], mh[:, 0:1])
                for e in range(1, 8):
                    q.op(v, v.tensor_scalar, out=colsb[:, 5:6], in0=colsb[:, 4:5],
                         scalar1=-1.0, scalar2=1.0, op0=Op.mult, op1=Op.add)
                    q.op(v, v.tensor_tensor, out=mh[:, e:e + 1], in0=mh[:, e:e + 1],
                         in1=colsb[:, 5:6], op=Op.mult)
                    q.op(v, v.tensor_tensor, out=colsb[:, 4:5], in0=colsb[:, 4:5],
                         in1=mh[:, e:e + 1], op=Op.max)
                q.op(v, v.tensor_scalar, out=gmb[:], in0=mh[:], scalar1=colsb[:, 3:4],
                     scalar2=None, op0=Op.mult)

            for e in range(8):
                @block.tensor
                def _(t, e=e):
                    q.op(t, t.transpose, ps_tb[0:1, :], gmb[:, e:e + 1], idb[:])

                @block.vector
                def _(v, e=e, tt=tt):
                    q.op(v, v.tensor_copy, gRow[0:1, e * T + tt * 128:e * T + (tt + 1) * 128],
                         ps_tb[0:1, 0:128])

        @block.sync
        def _(sync):
            for e in range(8):
                q.dma(sync, gT8[e:e + 1, :], gRow[0:1, e * T:(e + 1) * T])

        for tb in range(4):
            @block.sync
            def _(sync, tb=tb):
                q.dma(sync, xnt[:].rearrange("p (kd t) -> p kd t", kd=NT),
                      XN.rearrange("(kd p) t -> p kd t", p=128)[:, :, tb * 512:(tb + 1) * 512])

            for e in range(8):
                @block.tensor
                def _(t, e=e, tb=tb):
                    q.op(t, t.matmul, ps_g[:], onrb[:], gRow[0:1, e * T + tb * 512:e * T + (tb + 1) * 512],
                         start=True, stop=True)

                @block.vector
                def _(v, e=e):
                    for kd in range(NT):
                        q.op(v, v.tensor_tensor, out=xes[:, (e * NT + kd) * 512:(e * NT + kd + 1) * 512],
                             in0=xnt[:, kd * 512:(kd + 1) * 512], in1=ps_g[:], op=Op.mult)

            @block.sync
            def _(sync, tb=tb):
                for e in range(8):
                    q.dma(sync, xe_s[e].rearrange("(kd p) t -> p kd t", p=128)[:, :, tb * 512:(tb + 1) * 512],
                          xes[:, e * NT * 512:(e + 1) * NT * 512].rearrange("p (kd t) -> p kd t", kd=NT))

        for tb in range(4):
            @block.sync
            def _(sync, tb=tb):
                for e in range(8):
                    q.dma(sync, xes[:, e * NT * 512:(e + 1) * NT * 512].rearrange("p (kd t) -> p kd t", kd=NT),
                          xe_s[e].rearrange("(kd p) t -> p kd t", p=128)[:, :, tb * 512:(tb + 1) * 512])

            for k in range(NT):
                for eh in range(2):
                    @block.gpsimd
                    def _(gp, k=k, eh=eh):
                        for ei in range(4):
                            e = eh * 4 + ei
                            q.dma(gp, wep[:, ei * NT * 128:(ei + 1) * NT * 128].rearrange("p (kd c) -> p kd c", kd=NT),
                                  WET[e].rearrange("(kd p) c -> p kd c", p=128)[:, :, k * 128:(k + 1) * 128])

                    for ei in range(4):
                        e = eh * 4 + ei
                        for kd in range(NT):
                            @block.tensor
                            def _(t, e=e, ei=ei, kd=kd):
                                q.op(t, t.matmul, ps_out[:],
                                     wep[:, (ei * NT + kd) * 128:(ei * NT + kd + 1) * 128],
                                     xes[:, (e * NT + kd) * 512:(e * NT + kd + 1) * 512],
                                     start=(e == 0 and kd == 0), stop=False)

                @block.tensor
                def _(t, k=k, tb=tb):
                    q.op(t, t.matmul, ps_out[:], beb[:, k * 128:(k + 1) * 128],
                         gT8[:, tb * 512:(tb + 1) * 512], start=False, stop=True)

                @block.vector
                def _(v):
                    q.op(v, v.tensor_copy, outt[:], ps_out[:])

                @block.sync
                def _(sync, k=k, tb=tb):
                    q.dma(sync, X1[k * 128:(k + 1) * 128, tb * 512:(tb + 1) * 512], outt[:])

        @block.sync
        def _(sync):
            for hv in range(NT):
                q.dma(sync, oft[:, hv * T:(hv + 1) * T], OF[hv * 128:(hv + 1) * 128, :])

        for k in range(NT):
            @block.gpsimd
            def _(gp, k=k):
                q.dma(gp, wep[:, 0:NT * 128].rearrange("p (hv c) -> p hv c", hv=NT),
                      Wo.rearrange("(hv p) c -> p hv c", p=128)[:, :, k * 128:(k + 1) * 128])

            for tb in range(4):
                for hv in range(NT):
                    @block.tensor
                    def _(t, hv=hv, tb=tb):
                        q.op(t, t.matmul, ps_out[:], wep[:, hv * 128:(hv + 1) * 128],
                             oft[:, hv * T + tb * 512: hv * T + (tb + 1) * 512],
                             start=(hv == 0), stop=(hv == NT - 1))

                @block.sync
                def _(sync, k=k, tb=tb):
                    q.dma(sync, xt[:], X[k * 128:(k + 1) * 128, tb * 512:(tb + 1) * 512])
                    q.dma(sync, x1t[:], X1[k * 128:(k + 1) * 128, tb * 512:(tb + 1) * 512])

                @block.vector
                def _(v):
                    q.op(v, v.tensor_tensor, out=outt[:], in0=xt[:], in1=ps_out[:], op=Op.add)
                    q.op(v, v.tensor_tensor, out=outt[:], in0=outt[:], in1=x1t[:], op=Op.add)

                @block.sync
                def _(sync, k=k, tb=tb):
                    q.dma(sync, X1[k * 128:(k + 1) * 128, tb * 512:(tb + 1) * 512], outt[:])



def build_l0mix():
    ctx = Ctx(fused=False).begin()
    emit_l0mix(ctx)
    ctx.finish()
    return ctx.nc


def emit_l1(ctx, pfx="l1"):
    nc, block, q = ctx.nc, ctx.block, ctx.q
    X1 = ctx.dmid("x1_s", [D, T], F32)
    WAT = ctx.din("WAT", [D, 64], F32)
    WBT = ctx.din("WBT", [D, 64], F32)
    BA1 = ctx.din("BA1", [64, 1], F32)
    BB1 = ctx.din("BB1", [64, 1], F32)
    WOT = ctx.din("WOT", [64, D], F32)
    GWT = ctx.din("GWT", [D, 8], F32)
    WDT = ctx.din("WDT", [8, D, 512], F32)
    WUT = ctx.din("WUT", [8, 512, D], F32)
    BDC = ctx.din("BDC", [128, 8 * 4], F32)
    BU = ctx.din("BU", [8, D], F32)
    WSDT = ctx.din("WSDT", [D, 512], F32)
    BSDC = ctx.din("BSDC", [128, 4], F32)
    WSUT = ctx.din("WSUT", [512, D], F32)
    BSUC = ctx.din("BSUC", [128, 16], F32)
    BOUTC = ctx.din("BOUTC", [128, 16], F32)
    ONESC = ctx.din("ONESC", [128, 1], F32)
    ONESR = ctx.din("ONESR", [1, 128], F32)
    IDENTB = ctx.din("IDENTB", [128, 128], BF16)
    ONESRB = ctx.din("ONESRB", [1, 128], BF16)
    OUT = ctx.dout("out_s", [D, T], F32)
    X132 = ctx.dmid("xn132_i", [D, T], F32) if ctx.fused else nc.dram_tensor("xn132_i", [D, T], F32, kind="Internal").ap()

    st = contextlib.ExitStack()
    sb = lambda name, shape, dt=F32: st.enter_context(nc.sbuf_tensor(pfx + name, shape, dt))
    xnt1 = sb("xnt1", [128, NT * T], BF16)
    xt = sb("xt", [128, 512]); tmp = sb("tmp", [128, 512]); outt = sb("outt", [128, 512])
    epst = sb("epst", [128, 1])
    onc = sb("onc", [128, 1]); onr = sb("onr", [1, 128]); onrb = sb("onrb", [1, 128], BF16)
    idb = sb("idb", [128, 128], BF16)
    ba1 = sb("ba1", [64, 1]); bb1 = sb("bb1", [64, 1])
    hsb = sb("hsb", [64, T], BF16)
    lg = sb("lg", [128, 8]); ex = sb("ex", [128, 8]); mh = sb("mh", [128, 8])
    mh2 = sb("mh2", [128, 8]); pp = sb("pp", [128, 8]); colsb = sb("colsb", [128, 10])
    cmb = sb("cmb", [128, 8], BF16)
    cRow = sb("cRow", [1, 8 * T], BF16)
    cT8 = sb("cT8", [8, T], BF16)
    bdc = sb("bdc", [128, 8 * 4]); bsdc = sb("bsdc", [128, 4])
    bub = sb("bub", [8, D], BF16)
    bcol = sb("bcol", [128, 16])
    x1t = sb("x1t", [128, 512])
    stA = contextlib.ExitStack()
    sba = lambda name, shape, dt=F32: stA.enter_context(nc.sbuf_tensor(pfx + name, shape, dt))
    x1w = sba("x1w", [128, NT * 512])
    xsq = sba("xsq", [128, 512])
    rows = sba("rows", [1, 6 * 512])
    watp = sba("watp", [128, NT * 64], BF16); wbtp = sba("wbtp", [128, NT * 64], BF16)
    aa = sba("aa", [64, T]); ab = sba("ab", [64, T])
    gwtp = sba("gwtp", [128, NT * 8]); x32w = sba("x32w", [128, NT * 128])
    late = {}

    def _alloc_ffn():
        stA.close()
        late["wdtp"] = sb("wdtp", [128, NT * 128], BF16)
        late["wutp"] = sb("wutp", [128, 32 * 128], BF16)
        late["wsutp"] = sb("wsutp", [128, 4 * 128], BF16)
        late["wotp"] = sb("wotp", [64, 128], BF16)
        late["gsb"] = sb("gsb", [128, 512])
        late["he"] = sb("he", [128, 8 * 4 * 512], BF16)
        late["hsh"] = sb("hsh", [128, 4 * 512], BF16)
    ps_out = st.enter_context(nc.psum_tensor(pfx + "ps_out", [128, 512], F32))
    ps_g = st.enter_context(nc.psum_tensor(pfx + "ps_g", [128, 512], F32))
    ps_a = st.enter_context(nc.psum_tensor(pfx + "ps_a", [128, 512], F32))
    ps_lg = st.enter_context(nc.psum_tensor(pfx + "ps_lg", [128, 8], F32))
    ps_sum = st.enter_context(nc.psum_tensor(pfx + "ps_sum", [1, 512], F32))
    ps_sq = st.enter_context(nc.psum_tensor(pfx + "ps_sq", [1, 512], F32))
    ps_b1 = st.enter_context(nc.psum_tensor(pfx + "ps_b1", [128, 512], F32))
    ps_tb = st.enter_context(nc.psum_tensor(pfx + "ps_tb", [128, 128], BF16))
    with st:
        pass

        @block.sync
        def _(sync):
            q.dma(sync, onc[:], ONESC); q.dma(sync, onr[:], ONESR); q.dma(sync, onrb[:], ONESRB)
            q.dma(sync, idb[:], IDENTB)
            q.dma(sync, ba1[:], BA1); q.dma(sync, bb1[:], BB1)
            q.dma(sync, bdc[:], BDC); q.dma(sync, bsdc[:], BSDC)
            q.dma(sync, gwtp[:].rearrange("p (k e) -> p k e", k=NT),
                  GWT.rearrange("(k p) e -> p k e", p=128))

        @block.gpsimd
        def _(gp):
            q.dma(gp, watp[:].rearrange("p (k c) -> p k c", k=NT),
                  WAT.rearrange("(k p) c -> p k c", p=128))
            q.dma(gp, wbtp[:].rearrange("p (k c) -> p k c", k=NT),
                  WBT.rearrange("(k p) c -> p k c", p=128))
            q.dma(gp, bub[:], BU)

        @block.vector
        def _(v):
            q.op(v, v.memset, epst[:], EPS)

        @block.sync
        def _(sync):
            q.dma(sync, xt[:, 0:16], BSUC[:, 0:16])
            q.dma(sync, tmp[:, 0:16], BOUTC[:, 0:16])

        @block.vector
        def _(v):
            q.op(v, v.tensor_tensor, out=bcol[:], in0=xt[:, 0:16], in1=tmp[:, 0:16], op=Op.add)

        for cb in range(4):
            csl = slice(cb * 512, (cb + 1) * 512)

            @block.sync
            def _(sync, cb=cb):
                q.dma(sync, x1w[:].rearrange("p (kd t) -> p kd t", kd=NT),
                      X1.rearrange("(kd p) t -> p kd t", p=128)[:, :, csl])

            for k in range(NT):
                @block.tensor
                def _(t, k=k):
                    q.op(t, t.matmul, ps_sum[:], onc[:], x1w[:, k * 512:(k + 1) * 512],
                         start=(k == 0), stop=(k == NT - 1))

                @block.scalar
                def _(s, k=k):
                    q.op(s, s.activation, out=xsq[:], in_=x1w[:, k * 512:(k + 1) * 512], func=AF.Square)

                @block.tensor
                def _(t, k=k):
                    q.op(t, t.matmul, ps_sq[:], onc[:], xsq[:], start=(k == 0), stop=(k == NT - 1))

            @block.vector
            def _(v):
                mr, msq, m2, var, rstd, mneg = (rows[0:1, i * 512:(i + 1) * 512] for i in range(6))
                q.op(v, v.tensor_scalar, out=mr, in0=ps_sum[:], scalar1=1.0 / D, scalar2=None, op0=Op.mult)
                q.op(v, v.tensor_scalar, out=msq, in0=ps_sq[:], scalar1=1.0 / D, scalar2=None, op0=Op.mult)
                q.op(v, v.tensor_tensor, out=m2, in0=mr, in1=mr, op=Op.mult)
                q.op(v, v.tensor_tensor, out=var, in0=msq, in1=m2, op=Op.subtract)

            @block.scalar
            def _(s):
                var = rows[0:1, 3 * 512:4 * 512]
                q.op(s, s.activation, out=var, in_=var, func=AF.Sqrt, bias=epst[0:1, :])

            @block.vector
            def _(v):
                mr, msq, m2, var, rstd, mneg = (rows[0:1, i * 512:(i + 1) * 512] for i in range(6))
                q.op(v, v.reciprocal, out=rstd, in_=var)
                q.op(v, v.tensor_tensor, out=mneg, in0=mr, in1=rstd, op=Op.mult)
                q.op(v, v.tensor_scalar, out=mneg, in0=mneg, scalar1=-1.0, scalar2=None, op0=Op.mult)

            @block.tensor
            def _(t):
                q.op(t, t.matmul, ps_b1[:], onr[:], rows[0:1, 4 * 512:5 * 512], start=True, stop=True)
                q.op(t, t.matmul, ps_g[:], onr[:], rows[0:1, 5 * 512:6 * 512], start=True, stop=True)

            for k in range(NT):
                @block.vector
                def _(v, k=k, cb=cb):
                    q.op(v, v.tensor_tensor, out=tmp[:], in0=x1w[:, k * 512:(k + 1) * 512], in1=ps_b1[:], op=Op.mult)
                    q.op(v, v.tensor_tensor, out=tmp[:], in0=tmp[:], in1=ps_g[:], op=Op.add)
                    q.op(v, v.tensor_copy, xnt1[:, k * T + cb * 512:k * T + (cb + 1) * 512], tmp[:])

                @block.sync
                def _(sync, k=k, cb=cb):
                    q.dma(sync, X132[k * 128:(k + 1) * 128, csl], tmp[:])

        for which, wp, bias, dst in ((0, watp, ba1, aa), (1, wbtp, bb1, ab)):
            for tb in range(4):
                for kd in range(NT):
                    @block.tensor
                    def _(t, wp=wp, kd=kd, tb=tb):
                        q.op(t, t.matmul, ps_a[0:64, :], wp[:, kd * 64:(kd + 1) * 64],
                             xnt1[:, kd * T + tb * 512:kd * T + (tb + 1) * 512],
                             start=(kd == 0), stop=(kd == NT - 1))

                @block.scalar
                def _(s, bias=bias, dst=dst, tb=tb):
                    q.op(s, s.activation, out=dst[:, tb * 512:(tb + 1) * 512],
                         in_=ps_a[0:64, :], func=AF.Tanh, bias=bias[:])

        @block.vector
        def _(v):
            q.op(v, v.tensor_tensor_scan, out=hsb[:], data0=aa[:], data1=ab[:],
                 initial=0.0, op0=Op.mult, op1=Op.add)

        for tt in range(NT):
            @block.sync
            def _(sync, tt=tt):
                q.dma(sync, x32w[:].rearrange("p (kd t) -> p kd t", kd=NT),
                      X132.rearrange("(kd p) t -> p kd t", p=128)[:, :, tt * 128:(tt + 1) * 128])

            for kd in range(NT):
                @block.tensor
                def _(t, kd=kd):
                    q.op(t, t.matmul, ps_lg[:], x32w[:, kd * 128:(kd + 1) * 128], gwtp[:, kd * 8:(kd + 1) * 8],
                         start=(kd == 0), stop=(kd == NT - 1))

            @block.vector
            def _(v):
                q.op(v, v.tensor_copy, lg[:], ps_lg[:])
                q.op(v, v.tensor_reduce, out=colsb[:, 0:1], in_=lg[:], axis=mybir.AxisListType.X, op=Op.max)
                q.op(v, v.tensor_scalar, out=colsb[:, 1:2], in0=colsb[:, 0:1],
                     scalar1=-1.0, scalar2=None, op0=Op.mult)

            @block.scalar
            def _(s):
                q.op(s, s.activation, out=ex[:], in_=lg[:], func=AF.Exp, bias=colsb[:, 1:2])

            @block.vector
            def _(v):
                q.op(v, v.tensor_reduce, out=colsb[:, 2:3], in_=ex[:], axis=mybir.AxisListType.X, op=Op.add)
                q.op(v, v.reciprocal, out=colsb[:, 3:4], in_=colsb[:, 2:3])
                q.op(v, v.tensor_scalar, out=pp[:], in0=ex[:], scalar1=colsb[:, 3:4],
                     scalar2=None, op0=Op.mult)
                q.op(v, v.tensor_reduce, out=colsb[:, 4:5], in_=pp[:], axis=mybir.AxisListType.X, op=Op.max)
                q.op(v, v.tensor_tensor, out=mh[:], in0=pp[:],
                     in1=colsb[:, 4:5].broadcast_to([128, 8]), op=Op.is_equal)
                q.op(v, v.tensor_copy, colsb[:, 6:7], mh[:, 0:1])
                for e in range(1, 8):
                    q.op(v, v.tensor_scalar, out=colsb[:, 7:8], in0=colsb[:, 6:7],
                         scalar1=-1.0, scalar2=1.0, op0=Op.mult, op1=Op.add)
                    q.op(v, v.tensor_tensor, out=mh[:, e:e + 1], in0=mh[:, e:e + 1],
                         in1=colsb[:, 7:8], op=Op.mult)
                    q.op(v, v.tensor_tensor, out=colsb[:, 6:7], in0=colsb[:, 6:7],
                         in1=mh[:, e:e + 1], op=Op.max)
                q.op(v, v.tensor_scalar, out=colsb[:, 7:8], in0=colsb[:, 6:7],
                     scalar1=-1.0, scalar2=1.0, op0=Op.mult, op1=Op.add)
                q.op(v, v.tensor_tensor, out=mh2[:], in0=mh[:],
                     in1=mh[:], op=Op.mult)
                q.op(v, v.tensor_scalar, out=mh2[:], in0=mh[:], scalar1=-1.0, scalar2=1.0,
                     op0=Op.mult, op1=Op.add)
                q.op(v, v.tensor_tensor, out=mh2[:], in0=pp[:], in1=mh2[:], op=Op.mult)
                q.op(v, v.tensor_reduce, out=colsb[:, 5:6], in_=mh2[:], axis=mybir.AxisListType.X, op=Op.max)
                q.op(v, v.tensor_tensor, out=mh2[:], in0=mh2[:],
                     in1=colsb[:, 5:6].broadcast_to([128, 8]), op=Op.is_equal)
                q.op(v, v.tensor_copy, colsb[:, 8:9], mh2[:, 0:1])
                for e in range(1, 8):
                    q.op(v, v.tensor_scalar, out=colsb[:, 9:10], in0=colsb[:, 8:9],
                         scalar1=-1.0, scalar2=1.0, op0=Op.mult, op1=Op.add)
                    q.op(v, v.tensor_tensor, out=mh2[:, e:e + 1], in0=mh2[:, e:e + 1],
                         in1=colsb[:, 9:10], op=Op.mult)
                    q.op(v, v.tensor_tensor, out=colsb[:, 8:9], in0=colsb[:, 8:9],
                         in1=mh2[:, e:e + 1], op=Op.max)
                q.op(v, v.tensor_tensor, out=colsb[:, 6:7], in0=colsb[:, 4:5],
                     in1=colsb[:, 5:6], op=Op.add)
                q.op(v, v.reciprocal, out=colsb[:, 7:8], in_=colsb[:, 6:7])
                q.op(v, v.tensor_tensor, out=colsb[:, 8:9], in0=colsb[:, 4:5],
                     in1=colsb[:, 7:8], op=Op.mult)
                q.op(v, v.tensor_tensor, out=colsb[:, 9:10], in0=colsb[:, 5:6],
                     in1=colsb[:, 7:8], op=Op.mult)
                q.op(v, v.tensor_scalar, out=mh[:], in0=mh[:], scalar1=colsb[:, 8:9],
                     scalar2=None, op0=Op.mult)
                q.op(v, v.scalar_tensor_tensor, out=pp[:], in0=mh2[:], scalar=colsb[:, 9:10],
                     in1=mh[:], op0=Op.mult, op1=Op.add)
                q.op(v, v.tensor_copy, cmb[:], pp[:])

            for e in range(8):
                @block.tensor
                def _(t, e=e):
                    q.op(t, t.transpose, ps_tb[0:1, :], cmb[:, e:e + 1], idb[:])

                @block.vector
                def _(v, e=e, tt=tt):
                    q.op(v, v.tensor_copy, cRow[0:1, e * T + tt * 128:e * T + (tt + 1) * 128],
                         ps_tb[0:1, 0:128])

        @block.sync
        def _(sync):
            for e in range(8):
                q.dma(sync, cT8[e:e + 1, :], cRow[0:1, e * T:(e + 1) * T])

        _alloc_ffn()
        wdtp = late["wdtp"]; wutp = late["wutp"]; wsutp = late["wsutp"]
        wotp = late["wotp"]; gsb = late["gsb"]; he = late["he"]; hsh = late["hsh"]

        for tb in range(4):
            tsl = slice(tb * 512, (tb + 1) * 512)
            for e in range(8):
                for bt in range(4):
                    @block.gpsimd
                    def _(gp, e=e, bt=bt):
                        q.dma(gp, wdtp[:].rearrange("p (kd c) -> p kd c", kd=NT),
                              WDT[e].rearrange("(kd p) c -> p kd c", p=128)[:, :, bt * 128:(bt + 1) * 128])

                    for kd in range(NT):
                        @block.tensor
                        def _(t, kd=kd, tb=tb):
                            q.op(t, t.matmul, ps_a[:], wdtp[:, kd * 128:(kd + 1) * 128],
                                 xnt1[:, kd * T + tb * 512:kd * T + (tb + 1) * 512],
                                 start=(kd == 0), stop=(kd == NT - 1))

                    @block.scalar
                    def _(s, e=e, bt=bt):
                        q.op(s, s.activation, out=gsb[:], in_=ps_a[:], func=AF.Gelu,
                             bias=bdc[:, e * 4 + bt:e * 4 + bt + 1])

                    @block.tensor
                    def _(t, e=e, tb=tb):
                        q.op(t, t.matmul, ps_g[:], onrb[:], cRow[0:1, e * T + tb * 512:e * T + (tb + 1) * 512],
                             start=True, stop=True)

                    @block.vector
                    def _(v, e=e, bt=bt):
                        q.op(v, v.tensor_tensor, out=he[:, (e * 4 + bt) * 512:(e * 4 + bt + 1) * 512],
                             in0=gsb[:], in1=ps_g[:], op=Op.mult)

            for bt in range(4):
                @block.gpsimd
                def _(gp, bt=bt):
                    q.dma(gp, wdtp[:].rearrange("p (kd c) -> p kd c", kd=NT),
                          WSDT.rearrange("(kd p) c -> p kd c", p=128)[:, :, bt * 128:(bt + 1) * 128])

                for kd in range(NT):
                    @block.tensor
                    def _(t, kd=kd, tb=tb):
                        q.op(t, t.matmul, ps_a[:], wdtp[:, kd * 128:(kd + 1) * 128],
                             xnt1[:, kd * T + tb * 512:kd * T + (tb + 1) * 512],
                             start=(kd == 0), stop=(kd == NT - 1))

                @block.scalar
                def _(s, bt=bt):
                    q.op(s, s.activation, out=hsh[:, bt * 512:(bt + 1) * 512], in_=ps_a[:],
                         func=AF.Gelu, bias=bsdc[:, bt:bt + 1])

            for k in range(NT):
                @block.gpsimd
                def _(gp, k=k):
                    for e in range(8):
                        q.dma(gp, wutp[:, e * 4 * 128:(e + 1) * 4 * 128].rearrange("p (bt c) -> p bt c", bt=4),
                              WUT[e].rearrange("(bt p) c -> p bt c", p=128)[:, :, k * 128:(k + 1) * 128])
                    q.dma(gp, wsutp[:].rearrange("p (bt c) -> p bt c", bt=4),
                          WSUT.rearrange("(bt p) c -> p bt c", p=128)[:, :, k * 128:(k + 1) * 128])
                    q.dma(gp, wotp[:], WOT[:, k * 128:(k + 1) * 128])

                for e in range(8):
                    for bt in range(4):
                        @block.tensor
                        def _(t, e=e, bt=bt):
                            q.op(t, t.matmul, ps_out[:],
                                 wutp[:, (e * 4 + bt) * 128:(e * 4 + bt + 1) * 128],
                                 he[:, (e * 4 + bt) * 512:(e * 4 + bt + 1) * 512],
                                 start=(e == 0 and bt == 0), stop=False)

                for bt in range(4):
                    @block.tensor
                    def _(t, bt=bt):
                        q.op(t, t.matmul, ps_out[:], wsutp[:, bt * 128:(bt + 1) * 128],
                             hsh[:, bt * 512:(bt + 1) * 512], start=False, stop=False)

                @block.tensor
                def _(t, k=k, tb=tb):
                    q.op(t, t.matmul, ps_out[:], bub[:, k * 128:(k + 1) * 128], cT8[:, tsl],
                         start=False, stop=False)
                    q.op(t, t.matmul, ps_out[:], wotp[:], hsb[:, tsl], start=False, stop=True)

                @block.sync
                def _(sync, k=k, tb=tb):
                    q.dma(sync, x1t[:], X1[k * 128:(k + 1) * 128, tsl])

                @block.vector
                def _(v, k=k, tb=tb):
                    q.op(v, v.tensor_tensor, out=outt[:], in0=x1t[:], in1=ps_out[:], op=Op.add)
                    q.op(v, v.tensor_tensor, out=outt[:], in0=outt[:],
                         in1=xnt1[:, k * T + tb * 512:k * T + (tb + 1) * 512], op=Op.add)
                    q.op(v, v.tensor_scalar, out=outt[:], in0=outt[:],
                         scalar1=bcol[:, k:k + 1], scalar2=None, op0=Op.add)

                @block.sync
                def _(sync, k=k, tb=tb):
                    q.dma(sync, OUT[k * 128:(k + 1) * 128, tsl], outt[:])



def build_l1():
    ctx = Ctx(fused=False).begin()
    emit_l1(ctx)
    ctx.finish()
    return ctx.nc


def _consts():
    GAM = np.ones((128, T), np.float32)
    GAM[:, ::CH] = 0.0
    p = np.arange(128)
    c = np.arange(128)
    bd = (p[:, None] // 8 == c[None, :] // 8)
    return dict(
        GAM=GAM,
        IDENT=np.eye(128, dtype=np.float32),
        IDENTB=np.eye(128, dtype=np.float32).astype(BF),
        ONESC=np.ones((128, 1), np.float32),
        ONESR=np.ones((1, 128), np.float32),
        NONESR=-np.ones((1, 128), np.float32),
        ONESRB=np.ones((1, 128), np.float32).astype(BF),
        MBD=bd.astype(BF),
        MSL=(bd & ((p[:, None] % 8) < (c[None, :] % 8))).astype(BF),
        MTL=(bd & ((p[:, None] % 8) <= (c[None, :] % 8))).astype(BF),
        MBD4=np.tile(bd.astype(BF), (1, 4)),
        MSL4=np.tile((bd & ((p[:, None] % 8) < (c[None, :] % 8))).astype(BF), (1, 4)),
        MTL4=np.tile((bd & ((p[:, None] % 8) <= (c[None, :] % 8))).astype(BF), (1, 4)),
    )


def kernel_multilaunch(**ins):
    NCORES = 8
    cores = list(range(NCORES))
    C = _consts()
    cn = lambda a: np.ascontiguousarray(np.asarray(a), dtype=np.float32)

    xs = [np.ascontiguousarray(np.asarray(ins["L0_attn_x"][c]).T) for c in range(NCORES)]

    nc = build_ln0()
    res = run_bass_kernel_spmd(
        nc, [dict(x=xs[c], ONESC=C["ONESC"], ONESR=C["ONESR"]) for c in range(NCORES)], cores)
    xn = [np.asarray(res.results[c]["xn"]) for c in range(NCORES)]
    xn32 = [np.asarray(res.results[c]["xn32"]) for c in range(NCORES)]

    shared2 = dict(
        Wq=cn(ins["L0_attn_Wq"]), Wk=cn(ins["L0_attn_Wk"]), Wv=cn(ins["L0_attn_Wv"]),
        Wa=cn(ins["L0_attn_Wa"]), Wg=cn(ins["L0_attn_Wg"]), Wb=cn(ins["L0_attn_Wb"]),
        CQW=cn(np.asarray(ins["L0_attn_cqw"])[:, 0, :]),
        CKW=cn(np.asarray(ins["L0_attn_ckw"])[:, 0, :]),
        CVW=cn(np.asarray(ins["L0_attn_cvw"])[:, 0, :]),
        CQB=cn(np.asarray(ins["L0_attn_cqb"]).reshape(16, 128).T),
        CKB=cn(np.asarray(ins["L0_attn_ckb"]).reshape(16, 128).T),
        CVB=cn(np.asarray(ins["L0_attn_cvb"]).reshape(16, 128).T),
        BA=cn(np.asarray(ins["L0_attn_ba"]).reshape(16, 128).T),
        BB=cn(np.asarray(ins["L0_attn_bb"]).reshape(16, 1)),
        GAM=C["GAM"], IDENT=C["IDENT"],
    )
    nc = build_proj()
    res = run_bass_kernel_spmd(nc, [dict(xn=xn[c], **shared2) for c in range(NCORES)], cores)
    pj = [dict(res.results[c]) for c in range(NCORES)]

    shared3 = dict(
        MBD=C["MBD"], MSL=C["MSL"], MTL=C["MTL"],
        MBD4=C["MBD4"], MSL4=C["MSL4"], MTL4=C["MTL4"], IDENTB=C["IDENTB"], IDENT=C["IDENT"],
        ONESR=C["ONESR"],
        LNW=cn(np.asarray(ins["L0_attn_ln_w"]).reshape(128, 1)),
        LNB=cn(np.asarray(ins["L0_attn_ln_b"]).reshape(128, 1)),
    )
    nc = build_delta()
    res = run_bass_kernel_spmd(
        nc, [dict(q_s=pj[c]["q_s"], k_s=pj[c]["k_s"], g_s=pj[c]["g_s"], p1_s=pj[c]["p1_s"],
                  af_s=pj[c]["af_s"], ia_s=pj[c]["ia_s"], bta_s=pj[c]["bta_s"],
                  eom_s=pj[c]["eom_s"], elc_s=pj[c]["elc_s"], decrows_s=pj[c]["decrows_s"],
                  **shared3) for c in range(NCORES)], cores)
    of = [np.asarray(res.results[c]["of_s"]) for c in range(NCORES)]

    shared4 = dict(
        Wo=cn(ins["L0_attn_Wo"]), WG8=cn(ins["L0_ffn_wg"]),
        WET=cn(np.asarray(ins["L0_ffn_W"]).transpose(0, 2, 1)),
        BE=cn(ins["L0_ffn_b"]),
        IDENT=C["IDENT"], IDENTB=C["IDENTB"], ONESRB=C["ONESRB"],
    )
    nc = build_l0mix()
    res = run_bass_kernel_spmd(
        nc, [dict(x=xs[c], xn=xn[c], xn32=xn32[c], of_s=of[c], **shared4)
             for c in range(NCORES)], cores)
    x1 = [np.asarray(res.results[c]["x1_s"]) for c in range(NCORES)]

    shared5 = dict(
        WAT=cn(np.asarray(ins["L1_attn_Wa"]).T), WBT=cn(np.asarray(ins["L1_attn_Wb"]).T),
        BA1=cn(np.asarray(ins["L1_attn_ba"]).reshape(64, 1)),
        BB1=cn(np.asarray(ins["L1_attn_bb"]).reshape(64, 1)),
        WOT=cn(np.asarray(ins["L1_attn_Wout"]).T),
        GWT=cn(np.asarray(ins["L1_ffn_gate_w"]).T),
        WDT=cn(np.asarray(ins["L1_ffn_wd"]).transpose(0, 2, 1)),
        WUT=cn(np.asarray(ins["L1_ffn_wu"]).transpose(0, 2, 1)),
        BDC=cn(np.asarray(ins["L1_ffn_bd"]).reshape(8, 4, 128).transpose(2, 0, 1).reshape(128, 32)),
        BU=cn(ins["L1_ffn_bu"]),
        WSDT=cn(np.asarray(ins["L1_ffn_ws_d"]).T),
        BSDC=cn(np.asarray(ins["L1_ffn_bs_d"]).reshape(4, 128).T),
        WSUT=cn(np.asarray(ins["L1_ffn_ws_u"]).T),
        BSUC=cn(np.asarray(ins["L1_ffn_bs_u"]).reshape(16, 128).T),
        BOUTC=cn(np.asarray(ins["L1_attn_bout"]).reshape(16, 128).T),
        ONESC=C["ONESC"], ONESR=C["ONESR"], IDENTB=C["IDENTB"], ONESRB=C["ONESRB"],
    )
    nc = build_l1()
    res = run_bass_kernel_spmd(
        nc, [dict(x1_s=x1[c], **shared5) for c in range(NCORES)], cores)
    out = np.stack([np.asarray(res.results[c]["out_s"]).T for c in range(NCORES)])
    return np.ascontiguousarray(out)


def build_fused():
    ctx = Ctx(fused=True).begin()
    emit_ln0(ctx)
    ctx.phase()
    emit_proj(ctx)
    ctx.phase()
    emit_delta3(ctx)
    ctx.phase()
    emit_l0mix2(ctx)
    ctx.phase()
    emit_l12(ctx)
    ctx.finish()
    return ctx.nc


def _in_map(ins, C, xcore):
    cn = lambda a: np.ascontiguousarray(np.asarray(a), dtype=np.float32)
    return dict(
        x=xcore,
        ONESC=C["ONESC"], ONESR=C["ONESR"], NONESR=C["NONESR"], ONESRB=C["ONESRB"],
        IDENT=C["IDENT"], IDENTB=C["IDENTB"], GAM=C["GAM"],
        MBD4=C["MBD4"], MSL4=C["MSL4"], MTL4=C["MTL4"],
        Wq=cn(ins["L0_attn_Wq"]), Wk=cn(ins["L0_attn_Wk"]), Wv=cn(ins["L0_attn_Wv"]),
        Wa=cn(ins["L0_attn_Wa"]), Wg=cn(ins["L0_attn_Wg"]), Wb=cn(ins["L0_attn_Wb"]),
        CQW=cn(np.asarray(ins["L0_attn_cqw"])[:, 0, :]),
        CKW=cn(np.asarray(ins["L0_attn_ckw"])[:, 0, :]),
        CVW=cn(np.asarray(ins["L0_attn_cvw"])[:, 0, :]),
        CQB=cn(np.asarray(ins["L0_attn_cqb"]).reshape(16, 128).T),
        CKB=cn(np.asarray(ins["L0_attn_ckb"]).reshape(16, 128).T),
        CVB=cn(np.asarray(ins["L0_attn_cvb"]).reshape(16, 128).T),
        BA=cn(np.asarray(ins["L0_attn_ba"]).reshape(16, 128).T),
        BB=cn(np.asarray(ins["L0_attn_bb"]).reshape(16, 1)),
        LNW=cn(np.asarray(ins["L0_attn_ln_w"]).reshape(128, 1)),
        LNB=cn(np.asarray(ins["L0_attn_ln_b"]).reshape(128, 1)),
        Wo=cn(ins["L0_attn_Wo"]), WG8=cn(ins["L0_ffn_wg"]),
        WET=cn(np.asarray(ins["L0_ffn_W"]).transpose(0, 2, 1)),
        BE=cn(ins["L0_ffn_b"]),
        WAT=cn(np.asarray(ins["L1_attn_Wa"]).T), WBT=cn(np.asarray(ins["L1_attn_Wb"]).T),
        BA1=cn(np.asarray(ins["L1_attn_ba"]).reshape(64, 1)),
        BB1=cn(np.asarray(ins["L1_attn_bb"]).reshape(64, 1)),
        WOT=cn(np.asarray(ins["L1_attn_Wout"]).T),
        GWT=cn(np.asarray(ins["L1_ffn_gate_w"]).T),
        WDT=cn(np.asarray(ins["L1_ffn_wd"]).transpose(0, 2, 1)),
        WUT=cn(np.asarray(ins["L1_ffn_wu"]).transpose(0, 2, 1)),
        BDC=cn(np.asarray(ins["L1_ffn_bd"]).reshape(8, 4, 128).transpose(2, 0, 1).reshape(128, 32)),
        BU=cn(ins["L1_ffn_bu"]),
        WSDT=cn(np.asarray(ins["L1_ffn_ws_d"]).T),
        BSDC=cn(np.asarray(ins["L1_ffn_bs_d"]).reshape(4, 128).T),
        WSUT=cn(np.asarray(ins["L1_ffn_ws_u"]).T),
        BSUC=cn(np.asarray(ins["L1_ffn_bs_u"]).reshape(16, 128).T),
        BOUTC=cn(np.asarray(ins["L1_attn_bout"]).reshape(16, 128).T),
    )


def kernel(**ins):
    NCORES = 8
    C = _consts()
    xs = [np.ascontiguousarray(np.asarray(ins["L0_attn_x"][c]).T) for c in range(NCORES)]
    nc = build_fused()
    res = run_bass_kernel_spmd(nc, [_in_map(ins, C, xs[c]) for c in range(NCORES)],
                               core_ids=list(range(NCORES)))
    out = np.stack([np.asarray(res.results[c]["out_s"]).T for c in range(NCORES)])
    return np.ascontiguousarray(out)


def emit_l0mix2(ctx, pfx="m2"):
    nc, block = ctx.nc, ctx.block
    X = ctx.din("x", [D, T], F32)
    XN = ctx.dmid("xn", [D, T], BF16)
    XN32 = ctx.dmid("xn32", [D, T], F32)
    OF = ctx.dmid("of_s", [2048, T], BF16)
    Wo = ctx.din("Wo", [2048, 2048], F32)
    WG8 = ctx.din("WG8", [D, 8], F32)
    WET = ctx.din("WET", [8, D, 2048], F32)
    BE = ctx.din("BE", [8, 2048], F32)
    IDENTB = ctx.din("IDENTB", [128, 128], BF16)
    ONESRB = ctx.din("ONESRB", [1, 128], BF16)
    X1 = ctx.dmid("x1_s", [D, T], F32, producer=True)
    xe_s = nc.dram_tensor(pfx + "xe_s", [8, D, T], BF16, kind="Internal").ap()

    st = contextlib.ExitStack()
    sb = lambda name, shape, dt=F32: st.enter_context(nc.sbuf_tensor(pfx + name, shape, dt))
    ps = lambda name, shape, dt=F32: st.enter_context(nc.psum_tensor(pfx + name, shape, dt))
    big = sb("big", [128, 8 * NT * 512], BF16)
    xes = big
    oft = big
    wep = [sb(f"wep{i}", [128, 2 * NT * 128], BF16) for i in range(2)]
    wop = [sb(f"wop{i}", [128, NT * 128], BF16) for i in range(2)]
    gRow = sb("gRow", [1, 8 * T], BF16)
    beb = sb("beb", [8, 2048], BF16)
    gT8 = sb("gT8", [8, T], BF16)
    onrb = sb("onrb", [1, 128], BF16)
    idb = sb("idb", [128, 128], BF16)
    wgp = sb("wgp", [128, NT * 8])
    x32w = sb("x32w", [128, 4 * 128])
    lg = sb("lg", [128, 8]); ex = sb("ex", [128, 8]); mh = sb("mh", [128, 8])
    colsb = sb("colsb", [128, 8]); gmb = sb("gmb", [128, 8], BF16)
    gbce = sb("gbce", [128, 512])
    x1t = [sb(f"x1t{i}", [128, 512]) for i in range(2)]
    outt = [sb(f"outt{i}", [128, 512]) for i in range(2)]
    ps_out = [ps(f"ps_out{i}", [128, 512]) for i in range(2)]
    ps_g = ps("ps_g", [128, 512])
    ps_lg = [ps(f"ps_lg{i}", [128, 8]) for i in range(2)]
    ps_tb = ps("ps_tb", [128, 128], BF16)
    m = MSer(st, nc, ["ld", "pld", "mm", "ev"])

    with st:
        @block.sync
        def _(sync):
            m.dma("ld", sync, onrb[:], ONESRB)
            m.dma("ld", sync, idb[:], IDENTB)
            m.dma("ld", sync, wgp[:].rearrange("p (k e) -> p k e", k=NT),
                  WG8.rearrange("(k p) e -> p k e", p=128))

        @block.gpsimd
        def _(gp):
            m.dma("pld", gp, beb[:], BE)

        lg_free = {}
        for tt in range(NT):
            for half in range(4):
                @block.sync
                def _(sync, tt=tt, half=half):
                    m.dma("ld", sync, x32w[:].rearrange("p (kd t) -> p kd t", kd=4),
                          XN32.rearrange("(kd p) t -> p kd t", p=128)[:, half * 4:(half + 1) * 4,
                                                                     tt * 128:(tt + 1) * 128],
                          joins=[m.mark("mm")])
                ldm = m.mark("ld")

                @block.tensor
                def _(t, tt=tt, half=half, ldm=ldm):
                    for kd8 in range(4):
                        kd = half * 4 + kd8
                        joins = []
                        if kd8 == 0:
                            joins = [ldm]
                            if half == 0 and tt >= 2:
                                joins.append(lg_free[tt - 2])
                        m.op("mm", t, t.matmul, ps_lg[tt % 2][:], x32w[:, kd8 * 128:(kd8 + 1) * 128],
                             wgp[:, kd * 8:(kd + 1) * 8], start=(kd == 0), stop=(kd == NT - 1),
                             joins=joins)
            mmm = m.mark("mm")

            @block.vector
            def _(v, tt=tt, mmm=mmm):
                m.op("ev", v, v.tensor_copy, lg[:], ps_lg[tt % 2][:], joins=[mmm])
            lg_free[tt] = m.mark("ev")

            @block.vector
            def _(v):
                m.op("ev", v, v.tensor_reduce, out=colsb[:, 0:1], in_=lg[:], axis=mybir.AxisListType.X, op=Op.max)
                m.op("ev", v, v.tensor_scalar, out=colsb[:, 1:2], in0=colsb[:, 0:1],
                     scalar1=-1.0, scalar2=None, op0=Op.mult)

            @block.scalar
            def _(s):
                m.op("ev", s, s.activation, out=ex[:], in_=lg[:], func=AF.Exp, bias=colsb[:, 1:2])

            @block.vector
            def _(v):
                m.op("ev", v, v.tensor_reduce, out=colsb[:, 2:3], in_=ex[:], axis=mybir.AxisListType.X, op=Op.add)
                m.op("ev", v, v.reciprocal, out=colsb[:, 3:4], in_=colsb[:, 2:3])
                m.op("ev", v, v.tensor_tensor, out=mh[:], in0=lg[:],
                     in1=colsb[:, 0:1].broadcast_to([128, 8]), op=Op.is_equal)
                m.op("ev", v, v.tensor_copy, colsb[:, 4:5], mh[:, 0:1])
                for e in range(1, 8):
                    m.op("ev", v, v.tensor_scalar, out=colsb[:, 5:6], in0=colsb[:, 4:5],
                         scalar1=-1.0, scalar2=1.0, op0=Op.mult, op1=Op.add)
                    m.op("ev", v, v.tensor_tensor, out=mh[:, e:e + 1], in0=mh[:, e:e + 1],
                         in1=colsb[:, 5:6], op=Op.mult)
                    m.op("ev", v, v.tensor_tensor, out=colsb[:, 4:5], in0=colsb[:, 4:5],
                         in1=mh[:, e:e + 1], op=Op.max)
                m.op("ev", v, v.tensor_scalar, out=gmb[:], in0=mh[:], scalar1=colsb[:, 3:4],
                     scalar2=None, op0=Op.mult)

            for e in range(8):
                @block.tensor
                def _(t, e=e):
                    m.op("ev", t, t.transpose, ps_tb[0:1, :], gmb[:, e:e + 1], idb[:])

                @block.vector
                def _(v, e=e, tt=tt):
                    m.op("ev", v, v.tensor_copy, gRow[0:1, e * T + tt * 128:e * T + (tt + 1) * 128],
                         ps_tb[0:1, 0:128])

        @block.sync
        def _(sync):
            for e in range(8):
                m.dma("ld", sync, gT8[e:e + 1, :], gRow[0:1, e * T:(e + 1) * T],
                      joins=[m.mark("ev")])

        for tb in range(4):
            for half in range(2):
                @block.sync
                def _(sync, tb=tb, half=half):
                    m.dma("ld", sync, wep[half][:].rearrange("p (kd t) -> p kd t", kd=8),
                          XN.rearrange("(kd p) t -> p kd t", p=128)[:, half * 8:(half + 1) * 8,
                                                                   tb * 512:(tb + 1) * 512],
                          joins=[m.mark("ev")])
            ld_xn = m.mark("ld")

            for e in range(8):
                @block.tensor
                def _(t, e=e, tb=tb):
                    m.op("mm", t, t.matmul, ps_g[:], onrb[:],
                         gRow[0:1, e * T + tb * 512:e * T + (tb + 1) * 512],
                         start=True, stop=True, joins=[m.mark("ev")])
                gm = m.mark("mm")

                @block.vector
                def _(v, gm=gm, ld_xn=ld_xn):
                    m.op("ev", v, v.tensor_copy, gbce[:], ps_g[:], joins=[gm, ld_xn])

                @block.vector
                def _(v, e=e):
                    for kd in range(NT):
                        src = wep[kd // 8]
                        m.op("ev", v, v.tensor_tensor,
                             out=xes[:, (e * NT + kd) * 512:(e * NT + kd + 1) * 512],
                             in0=src[:, (kd % 8) * 512:((kd % 8) + 1) * 512], in1=gbce[:], op=Op.mult)

            @block.sync
            def _(sync, tb=tb):
                for e in range(8):
                    m.dma("ld", sync, xe_s[e].rearrange("(kd p) t -> p kd t", p=128)[:, :, tb * 512:(tb + 1) * 512],
                          xes[:, e * NT * 512:(e + 1) * NT * 512].rearrange("p (kd t) -> p kd t", kd=NT),
                          joins=[m.mark("ev")])

        ev_done = {}
        xe_done = m.mark("ev")
        pld_free = {0: xe_done, 1: xe_done}
        for tb in range(4):
            ld_xe = {}

            @block.sync
            def _(sync, tb=tb):
                for e in range(8):
                    m.dma("ld", sync, xes[:, e * NT * 512:(e + 1) * NT * 512].rearrange("p (kd t) -> p kd t", kd=NT),
                          xe_s[e].rearrange("(kd p) t -> p kd t", p=128)[:, :, tb * 512:(tb + 1) * 512],
                          joins=[m.mark("mm")] if e == 0 else ())
                    ld_xe[e] = m.mark("ld")

            for k in range(NT):
                for pj in range(4):
                    @block.gpsimd
                    def _(gp, k=k, pj=pj, tb=tb):
                        for sub in range(2):
                            e = pj * 2 + sub
                            for kh in range(2):
                                m.dma("pld", gp,
                                      wep[pj % 2][:, (sub * NT + kh * 8) * 128:(sub * NT + (kh + 1) * 8) * 128].rearrange("p (kd c) -> p kd c", kd=8),
                                      WET[e].rearrange("(kd p) c -> p kd c", p=128)[:, kh * 8:(kh + 1) * 8, k * 128:(k + 1) * 128],
                                      joins=[pld_free[pj % 2]] if (sub == 0 and kh == 0) else ())
                    pldm = m.mark("pld")

                    @block.tensor
                    def _(t, k=k, pj=pj, tb=tb, pldm=pldm):
                        for sub in range(2):
                            e = pj * 2 + sub
                            for kd in range(NT):
                                joins = []
                                if sub == 0 and kd == 0:
                                    joins = [pldm, ld_xe[pj * 2 + 1]]
                                    if pj == 0:
                                        if k == 0 and tb > 0:
                                            joins.append(ev_done[tb - 1])
                                        if k >= 2:
                                            joins.append(ev_done[(tb, k - 2)])
                                m.op("mm", t, t.matmul, ps_out[k % 2][:],
                                     wep[pj % 2][:, (sub * NT + kd) * 128:(sub * NT + kd + 1) * 128],
                                     xes[:, (e * NT + kd) * 512:(e * NT + kd + 1) * 512],
                                     start=(pj == 0 and sub == 0 and kd == 0), stop=False,
                                     joins=joins)
                    pld_free[pj % 2] = m.mark("mm")

                @block.tensor
                def _(t, k=k, tb=tb):
                    m.op("mm", t, t.matmul, ps_out[k % 2][:], beb[:, k * 128:(k + 1) * 128],
                         gT8[:, tb * 512:(tb + 1) * 512], start=False, stop=True)
                mmk = m.mark("mm")

                @block.vector
                def _(v, k=k, mmk=mmk):
                    m.op("ev", v, v.tensor_copy, outt[k % 2][:], ps_out[k % 2][:], joins=[mmk])
                ev_done[(tb, k)] = m.mark("ev")

                @block.sync
                def _(sync, k=k, tb=tb):
                    m.dma("ld", sync, X1[k * 128:(k + 1) * 128, tb * 512:(tb + 1) * 512],
                          outt[k % 2][:], joins=[ev_done[(tb, k)]])
            ev_done[tb] = m.mark("ev")

        @block.sync
        def _(sync):
            for hv in range(NT):
                m.dma("ld", sync, oft[:, hv * T:(hv + 1) * T], OF[hv * 128:(hv + 1) * 128, :],
                      joins=[m.mark("mm")] if hv == 0 else ())
        of_ld = m.mark("ld")

        wo_free = {0: ("none", 0), 1: ("none", 0)}
        for k in range(NT):
            @block.gpsimd
            def _(gp, k=k):
                m.dma("pld", gp, wop[k % 2][:].rearrange("p (hv c) -> p hv c", hv=NT),
                      Wo.rearrange("(hv p) c -> p hv c", p=128)[:, :, k * 128:(k + 1) * 128],
                      joins=[wo_free[k % 2]])
            wldm = m.mark("pld")

            for tb in range(4):
                @block.sync
                def _(sync, k=k, tb=tb):
                    m.dma("ld", sync, outt[tb % 2][:], X[k * 128:(k + 1) * 128, tb * 512:(tb + 1) * 512],
                          joins=[m.mark("ev")])
                    m.dma("ld", sync, x1t[tb % 2][:], X1[k * 128:(k + 1) * 128, tb * 512:(tb + 1) * 512])
                xldm = m.mark("ld")

                @block.tensor
                def _(t, k=k, tb=tb, wldm=wldm):
                    for hv in range(NT):
                        joins = []
                        if hv == 0:
                            joins = [wldm, of_ld]
                            if (k, tb) != (0, 0):
                                joins.append(m.mark("ev"))
                        m.op("mm", t, t.matmul, ps_out[tb % 2][:], wop[k % 2][:, hv * 128:(hv + 1) * 128],
                             oft[:, hv * T + tb * 512: hv * T + (tb + 1) * 512],
                             start=(hv == 0), stop=(hv == NT - 1), joins=joins)
                mmk = m.mark("mm")

                @block.vector
                def _(v, k=k, tb=tb, mmk=mmk, xldm=xldm):
                    m.op("ev", v, v.tensor_tensor, out=outt[tb % 2][:], in0=outt[tb % 2][:],
                         in1=ps_out[tb % 2][:], op=Op.add, joins=[mmk, xldm])
                    m.op("ev", v, v.tensor_tensor, out=outt[tb % 2][:], in0=outt[tb % 2][:],
                         in1=x1t[tb % 2][:], op=Op.add)

                @block.sync
                def _(sync, k=k, tb=tb):
                    m.dma("ld", sync, X1[k * 128:(k + 1) * 128, tb * 512:(tb + 1) * 512],
                          outt[tb % 2][:], joins=[m.mark("ev")])
            wo_free[k % 2] = m.mark("mm")

        for name in ("sync", "tensor", "vector", "scalar", "gpsimd"):
            @getattr(block, name)
            def _(eng):
                for sn in ("ld", "pld", "mm", "ev"):
                    sm = m.streams[sn]
                    eng.wait_ge(sm.sem, sm.count)


def build_l0mix2():
    ctx = Ctx(fused=False).begin()
    emit_l0mix2(ctx)
    ctx.finish()
    return ctx.nc


def emit_l12(ctx, pfx="l2", paranoid=False, par=()):
    nc, block = ctx.nc, ctx.block
    X1 = ctx.dmid("x1_s", [D, T], F32)
    WAT = ctx.din("WAT", [D, 64], F32)
    WBT = ctx.din("WBT", [D, 64], F32)
    BA1 = ctx.din("BA1", [64, 1], F32)
    BB1 = ctx.din("BB1", [64, 1], F32)
    WOT = ctx.din("WOT", [64, D], F32)
    GWT = ctx.din("GWT", [D, 8], F32)
    WDT = ctx.din("WDT", [8, D, 512], F32)
    WUT = ctx.din("WUT", [8, 512, D], F32)
    BDC = ctx.din("BDC", [128, 8 * 4], F32)
    BU = ctx.din("BU", [8, D], F32)
    WSDT = ctx.din("WSDT", [D, 512], F32)
    BSDC = ctx.din("BSDC", [128, 4], F32)
    WSUT = ctx.din("WSUT", [512, D], F32)
    BSUC = ctx.din("BSUC", [128, 16], F32)
    BOUTC = ctx.din("BOUTC", [128, 16], F32)
    ONESC = ctx.din("ONESC", [128, 1], F32)
    ONESR = ctx.din("ONESR", [1, 128], F32)
    IDENTB = ctx.din("IDENTB", [128, 128], BF16)
    ONESRB = ctx.din("ONESRB", [1, 128], BF16)
    OUT = ctx.dout("out_s", [D, T], F32)
    X132 = ctx.dmid("xn132_i", [D, T], F32) if ctx.fused else nc.dram_tensor("xn132_i", [D, T], F32, kind="Internal").ap()

    st = contextlib.ExitStack()
    sb = lambda name, shape, dt=F32: st.enter_context(nc.sbuf_tensor(pfx + name, shape, dt))
    ps = lambda name, shape, dt=F32: st.enter_context(nc.psum_tensor(pfx + name, shape, dt))
    xnt1 = sb("xnt1", [128, NT * T], BF16)
    tmp = sb("tmp", [128, 512])
    epst = sb("epst", [128, 1])
    onc = sb("onc", [128, 1]); onr = sb("onr", [1, 128]); onrb = sb("onrb", [1, 128], BF16)
    idb = sb("idb", [128, 128], BF16)
    ba1 = sb("ba1", [64, 1]); bb1 = sb("bb1", [64, 1])
    hsb = sb("hsb", [64, T], BF16)
    lg = sb("lg", [128, 8]); ex = sb("ex", [128, 8]); mh = sb("mh", [128, 8])
    mh2 = sb("mh2", [128, 8]); pp = sb("pp", [128, 8]); colsb = sb("colsb", [128, 10])
    cmb = sb("cmb", [128, 8], BF16)
    cRow = sb("cRow", [1, 8 * T], BF16)
    cT8 = sb("cT8", [8, T], BF16)
    bdc = sb("bdc", [128, 8 * 4]); bsdc = sb("bsdc", [128, 4])
    bub = sb("bub", [8, D], BF16)
    bcol = sb("bcol", [128, 16])
    outt = [sb(f"outt{i}", [128, 512]) for i in range(2)]
    x1t = [sb(f"x1t{i}", [128, 512]) for i in range(2)]
    gsb = [sb(f"gsb{i}", [128, 512]) for i in range(2)]
    psA = ps("psA", [128, 512]); psB = ps("psB", [128, 512])
    psC = ps("psC", [128, 512]); psD = ps("psD", [128, 512])
    psE = ps("psE", [128, 512]); psF = ps("psF", [128, 512])
    ps_tb = ps("ps_tb", [128, 128], BF16)
    m = MSer(st, nc, ["ld", "pld", "mm", "ev"])
    m.paranoid = paranoid
    stA = contextlib.ExitStack()
    sba = lambda name, shape, dt=F32: stA.enter_context(nc.sbuf_tensor(pfx + name, shape, dt))
    x1w = sba("x1w", [128, NT * 512])
    rows = sba("rows", [1, 6 * 512])
    watp = sba("watp", [128, NT * 64], BF16); wbtp = sba("wbtp", [128, NT * 64], BF16)
    aa = sba("aa", [64, T]); ab = sba("ab", [64, T])
    gwtp = sba("gwtp", [128, NT * 8]); x32w = sba("x32w", [128, 4 * 128])

    with st:
        @block.sync
        def _(sync):
            m.dma("ld", sync, onc[:], ONESC); m.dma("ld", sync, onr[:], ONESR)
            m.dma("ld", sync, onrb[:], ONESRB); m.dma("ld", sync, idb[:], IDENTB)
            m.dma("ld", sync, ba1[:], BA1); m.dma("ld", sync, bb1[:], BB1)
            m.dma("ld", sync, bdc[:], BDC); m.dma("ld", sync, bsdc[:], BSDC)
            m.dma("ld", sync, gwtp[:].rearrange("p (k e) -> p k e", k=NT),
                  GWT.rearrange("(k p) e -> p k e", p=128))
            m.dma("ld", sync, tmp[:, 0:16], BSUC[:, 0:16])
            m.dma("ld", sync, x1w[:, 0:16], BOUTC[:, 0:16])

        @block.gpsimd
        def _(gp):
            m.dma("pld", gp, watp[:].rearrange("p (k c) -> p k c", k=NT),
                  WAT.rearrange("(k p) c -> p k c", p=128))
            m.dma("pld", gp, wbtp[:].rearrange("p (k c) -> p k c", k=NT),
                  WBT.rearrange("(k p) c -> p k c", p=128))
            m.dma("pld", gp, bub[:], BU)

        @block.vector
        def _(v):
            m.op("ev", v, v.memset, epst[:], EPS, joins=[m.mark("ld")])
            m.op("ev", v, v.tensor_tensor, out=bcol[:], in0=tmp[:, 0:16], in1=x1w[:, 0:16], op=Op.add)

        m.paranoid = paranoid or ("ln" in par)
        apply_done = ("none", 0)
        for cb in range(4):
            csl = slice(cb * 512, (cb + 1) * 512)

            @block.sync
            def _(sync, cb=cb):
                m.dma("ld", sync, x1w[:].rearrange("p (kd t) -> p kd t", kd=NT),
                      X1.rearrange("(kd p) t -> p kd t", p=128)[:, :, csl],
                      joins=[apply_done, m.mark("ev")])
            ldm = m.mark("ld")

            for k in range(NT):
                @block.tensor
                def _(t, k=k, ldm=ldm):
                    m.op("mm", t, t.matmul, psA[0:1, :], onc[:], x1w[:, k * 512:(k + 1) * 512],
                         start=(k == 0), stop=(k == NT - 1), joins=[ldm] if k == 0 else ())

            @block.scalar
            def _(s, ldm=ldm):
                for k in range(NT):
                    joins = [ldm] if k < 2 else [m.mark("mm")]
                    m.op("ev", s, s.activation, out=gsb[k % 2][:], in_=x1w[:, k * 512:(k + 1) * 512],
                         func=AF.Square, joins=joins)
                    mq = m.mark("ev")

                    @block.tensor
                    def _(t, k=k, mq=mq):
                        m.op("mm", t, t.matmul, psB[0:1, :], onc[:], gsb[k % 2][:],
                             start=(k == 0), stop=(k == NT - 1), joins=[mq])

            mmst = m.mark("mm")

            @block.vector
            def _(v, mmst=mmst):
                mr, msq, m2, var, rstd, mneg = (rows[0:1, i * 512:(i + 1) * 512] for i in range(6))
                m.op("ev", v, v.tensor_scalar, out=mr, in0=psA[0:1, :], scalar1=1.0 / D,
                     scalar2=None, op0=Op.mult, joins=[mmst])
                m.op("ev", v, v.tensor_scalar, out=msq, in0=psB[0:1, :], scalar1=1.0 / D,
                     scalar2=None, op0=Op.mult)
                m.op("ev", v, v.tensor_tensor, out=m2, in0=mr, in1=mr, op=Op.mult)
                m.op("ev", v, v.tensor_tensor, out=var, in0=msq, in1=m2, op=Op.subtract)

            @block.scalar
            def _(s):
                var = rows[0:1, 3 * 512:4 * 512]
                m.op("ev", s, s.activation, out=var, in_=var, func=AF.Sqrt, bias=epst[0:1, :])

            @block.vector
            def _(v):
                mr, msq, m2, var, rstd, mneg = (rows[0:1, i * 512:(i + 1) * 512] for i in range(6))
                m.op("ev", v, v.reciprocal, out=rstd, in_=var)
                m.op("ev", v, v.tensor_tensor, out=mneg, in0=mr, in1=rstd, op=Op.mult)
                m.op("ev", v, v.tensor_scalar, out=mneg, in0=mneg, scalar1=-1.0, scalar2=None, op0=Op.mult)
            rowm = m.mark("ev")

            @block.tensor
            def _(t, rowm=rowm):
                m.op("mm", t, t.matmul, psC[:], onr[:], rows[0:1, 4 * 512:5 * 512],
                     start=True, stop=True, joins=[rowm])
                m.op("mm", t, t.matmul, psD[:], onr[:], rows[0:1, 5 * 512:6 * 512], start=True, stop=True)
            bcm = m.mark("mm")

            for k in range(NT):
                @block.vector
                def _(v, k=k, cb=cb, bcm=bcm):
                    buf = gsb[k % 2]
                    m.op("ev", v, v.tensor_tensor, out=buf[:], in0=x1w[:, k * 512:(k + 1) * 512],
                         in1=psC[:], op=Op.mult, joins=[bcm, m.mark("ld")] if k == 0 else [m.mark("ld")])
                    m.op("ev", v, v.tensor_tensor, out=buf[:], in0=buf[:], in1=psD[:], op=Op.add)
                    m.op("ev", v, v.tensor_copy, xnt1[:, k * T + cb * 512:k * T + (cb + 1) * 512], buf[:])

                @block.sync
                def _(sync, k=k, cb=cb):
                    m.dma("ld", sync, X132[k * 128:(k + 1) * 128, csl], gsb[k % 2][:], joins=[m.mark("ev")])
            apply_done = m.mark("ev")

        m.paranoid = paranoid or ("attn" in par)
        for which, wp, bias, dst in ((0, watp, ba1, aa), (1, wbtp, bb1, ab)):
            for tb in range(4):
                @block.tensor
                def _(t, wp=wp, tb=tb, which=which):
                    for kd in range(NT):
                        m.op("mm", t, t.matmul, psE[0:64, :], wp[:, kd * 64:(kd + 1) * 64],
                             xnt1[:, kd * T + tb * 512:kd * T + (tb + 1) * 512],
                             start=(kd == 0), stop=(kd == NT - 1),
                             joins=[apply_done, m.mark("pld"), m.mark("ev")] if (tb == 0 and kd == 0 and which == 0)
                             else ([m.mark("ev")] if kd == 0 else ()))
                pm = m.mark("mm")

                @block.scalar
                def _(s, bias=bias, dst=dst, tb=tb, pm=pm):
                    m.op("ev", s, s.activation, out=dst[:, tb * 512:(tb + 1) * 512],
                         in_=psE[0:64, :], func=AF.Tanh, bias=bias[:], joins=[pm])

        @block.vector
        def _(v):
            m.op("ev", v, v.tensor_tensor_scan, out=hsb[:], data0=aa[:], data1=ab[:],
                 initial=0.0, op0=Op.mult, op1=Op.add)

        m.paranoid = paranoid or ("rt" in par)
        lgf = {}
        for tt in range(NT):
            for qt in range(4):
                @block.sync
                def _(sync, tt=tt, qt=qt):
                    m.dma("ld", sync, x32w[:].rearrange("p (kd t) -> p kd t", kd=4),
                          X132.rearrange("(kd p) t -> p kd t", p=128)[:, qt * 4:(qt + 1) * 4,
                                                                     tt * 128:(tt + 1) * 128],
                          joins=[m.mark("mm")])
                ldm = m.mark("ld")

                @block.tensor
                def _(t, tt=tt, qt=qt, ldm=ldm):
                    for k4 in range(4):
                        kd = qt * 4 + k4
                        joins = []
                        if k4 == 0:
                            joins = [ldm]
                            if qt == 0 and tt >= 2:
                                joins.append(lgf[tt - 2])
                        m.op("mm", t, t.matmul, (psA if tt % 2 == 0 else psB)[:, 0:8],
                             x32w[:, k4 * 128:(k4 + 1) * 128], gwtp[:, kd * 8:(kd + 1) * 8],
                             start=(kd == 0), stop=(kd == NT - 1), joins=joins)
            mmm = m.mark("mm")

            @block.vector
            def _(v, tt=tt, mmm=mmm):
                m.op("ev", v, v.tensor_copy, lg[:], (psA if tt % 2 == 0 else psB)[:, 0:8], joins=[mmm])
            lgf[tt] = m.mark("ev")

            @block.vector
            def _(v):
                m.op("ev", v, v.tensor_reduce, out=colsb[:, 0:1], in_=lg[:], axis=mybir.AxisListType.X, op=Op.max)
                m.op("ev", v, v.tensor_scalar, out=colsb[:, 1:2], in0=colsb[:, 0:1],
                     scalar1=-1.0, scalar2=None, op0=Op.mult)

            @block.scalar
            def _(s):
                m.op("ev", s, s.activation, out=ex[:], in_=lg[:], func=AF.Exp, bias=colsb[:, 1:2])

            @block.vector
            def _(v):
                m.op("ev", v, v.tensor_reduce, out=colsb[:, 2:3], in_=ex[:], axis=mybir.AxisListType.X, op=Op.add)
                m.op("ev", v, v.reciprocal, out=colsb[:, 3:4], in_=colsb[:, 2:3])
                m.op("ev", v, v.tensor_scalar, out=pp[:], in0=ex[:], scalar1=colsb[:, 3:4],
                     scalar2=None, op0=Op.mult)
                m.op("ev", v, v.tensor_reduce, out=colsb[:, 4:5], in_=pp[:], axis=mybir.AxisListType.X, op=Op.max)
                m.op("ev", v, v.tensor_tensor, out=mh[:], in0=pp[:],
                     in1=colsb[:, 4:5].broadcast_to([128, 8]), op=Op.is_equal)
                m.op("ev", v, v.tensor_copy, colsb[:, 6:7], mh[:, 0:1])
                for e in range(1, 8):
                    m.op("ev", v, v.tensor_scalar, out=colsb[:, 7:8], in0=colsb[:, 6:7],
                         scalar1=-1.0, scalar2=1.0, op0=Op.mult, op1=Op.add)
                    m.op("ev", v, v.tensor_tensor, out=mh[:, e:e + 1], in0=mh[:, e:e + 1],
                         in1=colsb[:, 7:8], op=Op.mult)
                    m.op("ev", v, v.tensor_tensor, out=colsb[:, 6:7], in0=colsb[:, 6:7],
                         in1=mh[:, e:e + 1], op=Op.max)
                m.op("ev", v, v.tensor_scalar, out=mh2[:], in0=mh[:], scalar1=-1.0, scalar2=1.0,
                     op0=Op.mult, op1=Op.add)
                m.op("ev", v, v.tensor_tensor, out=mh2[:], in0=pp[:], in1=mh2[:], op=Op.mult)
                m.op("ev", v, v.tensor_reduce, out=colsb[:, 5:6], in_=mh2[:], axis=mybir.AxisListType.X, op=Op.max)
                m.op("ev", v, v.tensor_tensor, out=mh2[:], in0=mh2[:],
                     in1=colsb[:, 5:6].broadcast_to([128, 8]), op=Op.is_equal)
                m.op("ev", v, v.tensor_copy, colsb[:, 8:9], mh2[:, 0:1])
                for e in range(1, 8):
                    m.op("ev", v, v.tensor_scalar, out=colsb[:, 9:10], in0=colsb[:, 8:9],
                         scalar1=-1.0, scalar2=1.0, op0=Op.mult, op1=Op.add)
                    m.op("ev", v, v.tensor_tensor, out=mh2[:, e:e + 1], in0=mh2[:, e:e + 1],
                         in1=colsb[:, 9:10], op=Op.mult)
                    m.op("ev", v, v.tensor_tensor, out=colsb[:, 8:9], in0=colsb[:, 8:9],
                         in1=mh2[:, e:e + 1], op=Op.max)
                m.op("ev", v, v.tensor_tensor, out=colsb[:, 6:7], in0=colsb[:, 4:5],
                     in1=colsb[:, 5:6], op=Op.add)
                m.op("ev", v, v.reciprocal, out=colsb[:, 7:8], in_=colsb[:, 6:7])
                m.op("ev", v, v.tensor_tensor, out=colsb[:, 8:9], in0=colsb[:, 4:5],
                     in1=colsb[:, 7:8], op=Op.mult)
                m.op("ev", v, v.tensor_tensor, out=colsb[:, 9:10], in0=colsb[:, 5:6],
                     in1=colsb[:, 7:8], op=Op.mult)
                m.op("ev", v, v.tensor_scalar, out=mh[:], in0=mh[:], scalar1=colsb[:, 8:9],
                     scalar2=None, op0=Op.mult)
                m.op("ev", v, v.scalar_tensor_tensor, out=pp[:], in0=mh2[:], scalar=colsb[:, 9:10],
                     in1=mh[:], op0=Op.mult, op1=Op.add)
                m.op("ev", v, v.tensor_copy, cmb[:], pp[:])

            for e in range(8):
                @block.tensor
                def _(t, e=e):
                    m.op("ev", t, t.transpose, ps_tb[0:1, :], cmb[:, e:e + 1], idb[:])

                @block.vector
                def _(v, e=e, tt=tt):
                    m.op("ev", v, v.tensor_copy, cRow[0:1, e * T + tt * 128:e * T + (tt + 1) * 128],
                         ps_tb[0:1, 0:128])

        @block.sync
        def _(sync):
            for e in range(8):
                m.dma("ld", sync, cT8[e:e + 1, :], cRow[0:1, e * T:(e + 1) * T],
                      joins=[m.mark("ev")])

        m.paranoid = paranoid or ("ffn" in par)
        m.noskip = "noskip" in par
        m.peonly = "peonly" in par
        stA.close()
        wdtp = [sb(f"wdtp{i}", [128, NT * 128], BF16) for i in range(2)]
        wutp = [sb(f"wutp{i}", [128, 32 * 128], BF16) for i in range(2)]
        wsutp = sb("wsutp", [128, 4 * 128], BF16)
        wotp = sb("wotp", [64, 128], BF16)
        he = sb("he", [128, 8 * 4 * 512], BF16)
        hsh = sb("hsh", [128, 4 * 512], BF16)

        dn_free = {0: m.mark("ev"), 1: m.mark("ev")}
        ev_all = m.mark("ev")
        pld_free = {0: ev_all, 1: ev_all}
        up_evac = {}
        up_free = {0: ev_all, 1: ev_all}
        for tb in range(4):
            tsl = slice(tb * 512, (tb + 1) * 512)
            ups_prev = m.mark("mm")
            for e in range(9):
                for bt in range(4):
                    pi = (e * 4 + bt) % 2

                    @block.gpsimd
                    def _(gp, e=e, bt=bt, pi=pi):
                        src = (WSDT if e == 8 else WDT[e])
                        m.dma("pld", gp, wdtp[pi][:].rearrange("p (kd c) -> p kd c", kd=NT),
                              src.rearrange("(kd p) c -> p kd c", p=128)[:, :, bt * 128:(bt + 1) * 128],
                              joins=[pld_free[pi]])
                    pldm = m.mark("pld")

                    @block.tensor
                    def _(t, e=e, bt=bt, tb=tb, pi=pi, pldm=pldm):
                        bank = psA if pi == 0 else psB
                        for kd in range(NT):
                            joins = []
                            if kd == 0:
                                joins = [pldm, dn_free[pi]]
                                if tb == 0 and e == 0 and bt == 0:
                                    joins.append(m.mark("ld"))
                            m.op("mm", t, t.matmul, bank[:], wdtp[pi][:, kd * 128:(kd + 1) * 128],
                                 xnt1[:, kd * T + tb * 512:kd * T + (tb + 1) * 512],
                                 start=(kd == 0), stop=(kd == NT - 1), joins=joins)
                    pld_free[pi] = m.mark("mm")
                    dmm = m.mark("mm")

                    if e == 8:
                        @block.scalar
                        def _(s, bt=bt, pi=pi, dmm=dmm):
                            bank = psA if pi == 0 else psB
                            m.op("ev", s, s.activation, out=hsh[:, bt * 512:(bt + 1) * 512],
                                 in_=bank[:], func=AF.Gelu, bias=bsdc[:, bt:bt + 1], joins=[dmm])
                        dn_free[pi] = m.mark("ev")
                    else:
                        @block.scalar
                        def _(s, e=e, bt=bt, pi=pi, dmm=dmm):
                            bank = psA if pi == 0 else psB
                            m.op("ev", s, s.activation, out=gsb[pi][:], in_=bank[:],
                                 func=AF.Gelu, bias=bdc[:, e * 4 + bt:e * 4 + bt + 1], joins=[dmm])
                        dn_free[pi] = m.mark("ev")

                        @block.tensor
                        def _(t, e=e, tb=tb):
                            m.op("mm", t, t.matmul, psC[:], onrb[:],
                                 cRow[0:1, e * T + tb * 512:e * T + (tb + 1) * 512],
                                 start=True, stop=True, joins=[m.mark("ev")])
                        gbm = m.mark("mm")

                        @block.vector
                        def _(v, e=e, bt=bt, pi=pi, gbm=gbm, tb=tb):
                            joins = [gbm] + ([ups_prev] if (e == 0 and bt == 0 and tb > 0) else [])
                            m.op("ev", v, v.tensor_tensor,
                                 out=he[:, (e * 4 + bt) * 512:(e * 4 + bt + 1) * 512],
                                 in0=gsb[pi][:], in1=psC[:], op=Op.mult, joins=joins)
            heready = m.mark("ev")
            for k in range(NT):
                kp = k % 2

                @block.gpsimd
                def _(gp, k=k, kp=kp):
                    for e in range(8):
                        m.dma("pld", gp,
                              wutp[kp][:, e * 4 * 128:(e + 1) * 4 * 128].rearrange("p (bt c) -> p bt c", bt=4),
                              WUT[e].rearrange("(bt p) c -> p bt c", p=128)[:, :, k * 128:(k + 1) * 128],
                              joins=[up_free[kp]] if e == 0 else ())
                    m.dma("pld", gp, wsutp[:].rearrange("p (bt c) -> p bt c", bt=4),
                          WSUT.rearrange("(bt p) c -> p bt c", p=128)[:, :, k * 128:(k + 1) * 128])
                    m.dma("pld", gp, wotp[:], WOT[:, k * 128:(k + 1) * 128])
                updm = m.mark("pld")

                @block.sync
                def _(sync, k=k, tb=tb, kp=kp):
                    m.dma("ld", sync, x1t[kp][:], X1[k * 128:(k + 1) * 128, tsl],
                          joins=[m.mark("ev")])
                xldm = m.mark("ld")

                @block.tensor
                def _(t, k=k, tb=tb, kp=kp, updm=updm):
                    bank = psD if kp == 0 else psE
                    first = True
                    for e in range(8):
                        for bt in range(4):
                            joins = []
                            if first:
                                joins = [updm, heready]
                                if (tb, k - 2) in up_evac:
                                    joins.append(up_evac[(tb, k - 2)])
                                elif tb > 0:
                                    joins.append(up_evac[(tb - 1, NT - 2 + k)])
                                first = False
                            m.op("mm", t, t.matmul, bank[:],
                                 wutp[kp][:, (e * 4 + bt) * 128:(e * 4 + bt + 1) * 128],
                                 he[:, (e * 4 + bt) * 512:(e * 4 + bt + 1) * 512],
                                 start=(e == 0 and bt == 0), stop=False, joins=joins)
                    for bt in range(4):
                        m.op("mm", t, t.matmul, bank[:], wsutp[:, bt * 128:(bt + 1) * 128],
                             hsh[:, bt * 512:(bt + 1) * 512], start=False, stop=False)
                    m.op("mm", t, t.matmul, bank[:], bub[:, k * 128:(k + 1) * 128], cT8[:, tsl],
                         start=False, stop=False)
                    m.op("mm", t, t.matmul, bank[:], wotp[:], hsb[:, tsl], start=False, stop=True)
                up_free[kp] = m.mark("mm")
                umm = m.mark("mm")

                @block.vector
                def _(v, k=k, tb=tb, kp=kp, umm=umm, xldm=xldm):
                    bank = psD if kp == 0 else psE
                    m.op("ev", v, v.tensor_tensor, out=outt[kp][:], in0=x1t[kp][:],
                         in1=bank[:], op=Op.add, joins=[umm, xldm])
                    m.op("ev", v, v.tensor_tensor, out=outt[kp][:], in0=outt[kp][:],
                         in1=xnt1[:, k * T + tb * 512:k * T + (tb + 1) * 512], op=Op.add)
                    m.op("ev", v, v.tensor_scalar, out=outt[kp][:], in0=outt[kp][:],
                         scalar1=bcol[:, k:k + 1], scalar2=None, op0=Op.add)
                up_evac[(tb, k)] = m.mark("ev")

                @block.sync
                def _(sync, k=k, tb=tb, kp=kp):
                    m.dma("ld", sync, OUT[k * 128:(k + 1) * 128, tsl], outt[kp][:],
                          joins=[up_evac[(tb, k)]])

        for name in ("sync", "tensor", "vector", "scalar", "gpsimd"):
            @getattr(block, name)
            def _(eng):
                for sn in ("ld", "pld", "mm", "ev"):
                    sm = m.streams[sn]
                    eng.wait_ge(sm.sem, sm.count)


def build_l12(paranoid=False, par=()):
    ctx = Ctx(fused=False).begin()
    emit_l12(ctx, paranoid=paranoid, par=par)
    ctx.finish()
    return ctx.nc


def emit_delta3(ctx, pfx="d3"):
    nc, block = ctx.nc, ctx.block
    q_s = ctx.dmid("q_s", [2048, T], BF16)
    k_s = ctx.dmid("k_s", [2048, T], BF16)
    g_s = ctx.dmid("g_s", [2048, T], BF16)
    p1_s = ctx.dmid("p1_s", [2048, T], F32)
    af_s = ctx.dmid("af_s", [2048, T], F32)
    ia_s = ctx.dmid("ia_s", [2048, T], F32)
    bta_s = ctx.dmid("bta_s", [16, T], F32)
    eom_s = ctx.dmid("eom_s", [128, NCH * NH], F32)
    decrows_s = ctx.dmid("decrows_s", [NCH, 2048], F32)
    elcrows_s = ctx.dmid("elcrows_s", [NCH, 2048], F32)
    MBD4 = ctx.din("MBD4", [128, 512], BF16)
    MSL4 = ctx.din("MSL4", [128, 512], BF16)
    MTL4 = ctx.din("MTL4", [128, 512], BF16)
    IDENTB = ctx.din("IDENTB", [128, 128], BF16)
    IDENT = ctx.din("IDENT", [128, 128], F32)
    NONESR = ctx.din("NONESR", [1, 128], F32)
    ONESRB = ctx.din("ONESRB", [1, 128], BF16)
    LNW = ctx.din("LNW", [128, 1], F32)
    LNB = ctx.din("LNB", [128, 1], F32)
    of_s = ctx.dmid("of_s", [2048, T], BF16, producer=True)
    y_scr = nc.dram_tensor(pfx + "y_scr", [T, 2048], BF16, kind="Internal").ap()

    st = contextlib.ExitStack()
    sb = lambda name, shape, dt=F32: st.enter_context(nc.sbuf_tensor(pfx + name, shape, dt))
    ps = lambda name, shape, dt=F32: st.enter_context(nc.psum_tensor(pfx + name, shape, dt))
    Kc2 = [sb(f"Kc{p}", [128, 1024], BF16) for p in range(2)]
    Qc2 = [sb(f"Qc{p}", [128, 1024], BF16) for p in range(2)]
    Gc2 = [sb(f"Gc{p}", [128, 1024], BF16) for p in range(2)]
    P1c2 = [sb(f"P1c{p}", [128, 1024]) for p in range(2)]
    Afc2 = [sb(f"Afc{p}", [128, 1024]) for p in range(2)]
    IAc2 = [sb(f"IAc{p}", [128, 1024]) for p in range(2)]
    kT2 = [sb(f"kT{p}", [128, 2048], BF16) for p in range(2)]
    y2b = sb("y2b", [128, 2048], BF16)
    brow2 = [sb(f"brow{p}", [1, 1024]) for p in range(2)]
    drobb2 = [sb(f"drobb{p}", [1, 2048], BF16) for p in range(2)]
    erobb2 = [sb(f"erobb{p}", [1, 2048], BF16) for p in range(2)]
    gms2 = [sb(f"gms{p}", [128, 36 * 128], BF16) for p in range(2)]
    hms2 = [sb(f"hms{p}", [128, 36 * 128], BF16) for p in range(2)]
    nETs2 = [sb(f"nETs{p}", [128, 1024]) for p in range(2)]
    abTs2 = [sb(f"abTs{p}", [128, 1024]) for p in range(2)]
    Kord = sb("Kord", [128, 1024], BF16); Qord = sb("Qord", [128, 1024], BF16)
    nEc = sb("nEc", [128, 1024]); nEord = sb("nEord", [128, 1024])
    afod = sb("afod", [128, 1024])
    ptmp = [sb(f"ptmp{i}", [128, 512]) for i in range(2)]
    mbd4 = sb("mbd4", [128, 512], BF16); msl4 = sb("msl4", [128, 512], BF16)
    mtl4 = sb("mtl4", [128, 512], BF16)
    btmp = sb("btmp", [128, 1024])
    nrot = sb("nrot", [128, 1024]); dtot = sb("dtot", [128, 1024]); tmpw = sb("tmpw", [128, 1024])
    ysb = sb("ysb", [128, 1024], BF16)
    obs = sb("obs", [128, 128]); obsq = sb("obsq", [128, 128])
    ons2 = [sb(f"ons{i}", [128, 128]) for i in range(2)]
    besb = sb("besb", [128, 128], BF16)
    cols = sb("cols", [128, 8])
    ofchunk = sb("ofchunk", [128, 1024]); ofb = sb("ofb", [128, 1024], BF16)
    S_B = sb("S_B", [128, 2048]); S_Bb = sb("S_Bb", [128, 2048], BF16)
    eomres = sb("eomres", [128, NCH * NH])
    idb = sb("idb", [128, 128], BF16)
    idf = sb("idf", [128, 128]); nonr = sb("nonr", [1, 128]); onrb = sb("onrb", [1, 128], BF16)
    lnw = sb("lnw", [128, 1]); lnb = sb("lnb", [128, 1]); epst = sb("epst", [128, 1])
    ps_big = ps("ps_big", [128, 1024])
    ps_pair = ps("ps_pair", [128, 512])
    ps_pair2 = ps("ps_pair2", [128, 512])
    ps_sol = ps("ps_sol", [128, 128])
    ps_o2 = [ps(f"ps_o2{p}", [128, 128]) for p in range(2)]
    ps_tr = ps("ps_tr", [128, 128])
    m = MSer(st, nc, ["ld", "prep", "prepg", "main", "out", "ys"])
    hview = lambda ap: ap.rearrange("(h p) t -> p h t", p=128)

    gidx = {}
    for J in range(NSB):
        for i in range(J):
            gidx[(i, J)] = len(gidx)
    gdiag = {J: 28 + J for J in range(NSB)}
    hidx = dict(gidx)
    hdiag = dict(gdiag)

    with st:
        @block.sync
        def _(sync):
            m.dma("ld", sync, idb[:], IDENTB)
            m.dma("ld", sync, idf[:], IDENT); m.dma("ld", sync, nonr[:], NONESR)
            m.dma("ld", sync, onrb[:], ONESRB)
            m.dma("ld", sync, lnw[:], LNW); m.dma("ld", sync, lnb[:], LNB)
            m.dma("ld", sync, eomres[:], eom_s)
            m.dma("ld", sync, mbd4[:], MBD4); m.dma("ld", sync, msl4[:], MSL4)
            m.dma("ld", sync, mtl4[:], MTL4)

        @block.vector
        def _(v):
            m.op("main", v, v.memset, epst[:], EPS, joins=[m.mark("ld")])
            m.op("main", v, v.memset, S_B[:], 0.0)
            m.op("main", v, v.memset, S_Bb[:], 0.0)

        ld_done = [None] * NCH
        prep_done = [None] * NCH
        cons_done = [None] * (NCH + 2)
        ys_done = [None] * (NCH + 1)
        out_evac = {0: ("none", 0), 1: ("none", 0)}
        bank_free = {0: ("none", 0), 1: ("none", 0)}

        def emit_ld(n):
            p = n % 2
            csl = slice(n * CH, (n + 1) * CH)
            joins = [cons_done[n - 2]] if n >= 2 else []

            @block.gpsimd
            def _(gp):
                m.dma("ld", gp, drobb2[p][:], decrows_s[n:n + 1, :], joins=joins)
                m.dma("ld", gp, erobb2[p][:], elcrows_s[n:n + 1, :])

            @block.sync
            def _(sync):
                m.dma("ld", sync, Kc2[p][:].rearrange("p (h t) -> p h t", h=NH),
                      hview(k_s)[:, :, csl])
                m.dma("ld", sync, Qc2[p][:].rearrange("p (h t) -> p h t", h=NH), hview(q_s)[:, :, csl])
                m.dma("ld", sync, Gc2[p][:].rearrange("p (h t) -> p h t", h=NH), hview(g_s)[:, :, csl])
                m.dma("ld", sync, P1c2[p][:].rearrange("p (h t) -> p h t", h=NH), hview(p1_s)[:, :, csl])
                m.dma("ld", sync, Afc2[p][:].rearrange("p (h t) -> p h t", h=NH), hview(af_s)[:, :, csl])
                m.dma("ld", sync, IAc2[p][:].rearrange("p (h t) -> p h t", h=NH), hview(ia_s)[:, :, csl])
                if n % 2 == 0:
                    m.dma("ld", sync, kT2[(n // 2) % 2][:], k_s[:, n * CH:(n + 2) * CH], transpose=True)
                m.dma("ld", sync, brow2[p][:].rearrange("o (h t) -> o h t", h=NH),
                      bta_s.rearrange("(o h) t -> o h t", o=1)[0:1, :, csl])
            @block.gpsimd
            def _(gp):
                m.dma("ld", gp, drobb2[p][:], decrows_s[n:n + 1, :])
                m.dma("ld", gp, erobb2[p][:], elcrows_s[n:n + 1, :])
            ld_done[n] = m.mark("ld")

        def prep_items(n):
            p = n % 2
            items = []
            marks = {}

            def mk_reord(src, dst, hf, first):
                def it():
                    @block.gpsimd
                    def _(gp):
                        m.op("prepg", gp, gp.tensor_copy,
                             dst[:].rearrange("p (s h t) -> p h s t", s=NSB, h=NH)[:, hf * 8:(hf + 1) * 8],
                             src[:].rearrange("p (h s t) -> p h s t", h=NH, s=NSB)[:, hf * 8:(hf + 1) * 8],
                             joins=[ld_done[n]] if first else ())
                    marks["reord"] = m.mark("prepg")
                return it
            for src, dst in ((Kc2[p], Kord), (Qc2[p], Qord), (Afc2[p], afod)):
                for hf in range(2):
                    items.append(mk_reord(src, dst, hf, src is Kc2[p] and hf == 0))

            def it_brow():
                @block.tensor
                def _(t):
                    m.op("prep", t, t.matmul, ps_pair[:], nonr[:], brow2[p][0:1, 0:512],
                         start=True, stop=True, joins=[ld_done[n], bank_free[0]])
                mmk = m.mark("prep")

                @block.vector
                def _(v):
                    m.op("prep", v, v.tensor_copy, btmp[:, 0:512], ps_pair[:], joins=[mmk])
                bank_free[0] = m.mark("prep")

                @block.tensor
                def _(t):
                    m.op("prep", t, t.matmul, ps_pair2[:], nonr[:], brow2[p][0:1, 512:1024],
                         start=True, stop=True, joins=[bank_free[1]])
                mmk2 = m.mark("prep")

                @block.vector
                def _(v):
                    m.op("prep", v, v.tensor_copy, btmp[:, 512:1024], ps_pair2[:], joins=[mmk2])
                bank_free[1] = m.mark("prep")
                marks["btmp"] = bank_free[1]
            items.append(it_brow)

            def mk_nE(hf):
                def it():
                    @block.gpsimd
                    def _(gp):
                        hs = slice(hf * 512, (hf + 1) * 512)
                        m.op("prepg", gp, gp.tensor_tensor, out=nEc[:, hs], in0=IAc2[p][:, hs],
                             in1=btmp[:, hs], op=Op.mult,
                             joins=[marks["btmp"]] if hf == 0 else ())
                        m.op("prepg", gp, gp.tensor_copy,
                             nEord[:].rearrange("p (s h t) -> p h s t", s=NSB, h=NH)[:, hf * 8:(hf + 1) * 8],
                             nEc[:].rearrange("p (h s t) -> p h s t", h=NH, s=NSB)[:, hf * 8:(hf + 1) * 8])
                    marks["nEord"] = m.mark("prepg")
                return it
            items.append(mk_nE(0))
            items.append(mk_nE(1))

            tgroups = [("nE", nETs2[p], 0), ("nE", nETs2[p], 1), ("af", abTs2[p], 0), ("af", abTs2[p], 1)]
            for tg, (which, dst, hf) in enumerate(tgroups):
                def it_tg(which=which, dst=dst, hf=hf, tg=tg):
                    src = nEord if which == "nE" else afod
                    srcmark = marks["nEord"] if which == "nE" else marks["reord"]
                    bank = ps_pair if tg % 2 == 0 else ps_pair2

                    @block.tensor
                    def _(t):
                        for j in range(4):
                            s_ = hf * 4 + j
                            m.op("prep", t, t.transpose, bank[:, j * 128:(j + 1) * 128],
                                 src[:, s_ * 128:(s_ + 1) * 128], idf[:],
                                 joins=[srcmark, bank_free[tg % 2]] if j == 0 else ())
                    tmm = m.mark("prep")

                    @block.scalar
                    def _(sc):
                        m.op("prep", sc, sc.copy, out=dst[:, hf * 512:(hf + 1) * 512], in_=bank[:], joins=[tmm])
                    bank_free[tg % 2] = m.mark("prep")
                items.append(it_tg)

            pairL = []
            for J in range(NSB):
                for i in range(J):
                    pairL.append((Kord, i, Kord, J, gms2[p], gidx[(i, J)]))
            for J in range(NSB):
                pairL.append((Kord, J, Kord, J, gms2[p], gdiag[J]))
            for J in range(NSB):
                for i in range(J):
                    pairL.append((Kord, i, Qord, J, hms2[p], hidx[(i, J)]))
            for J in range(NSB):
                pairL.append((Kord, J, Qord, J, hms2[p], hdiag[J]))
            mask4s = [mbd4] * 7 + [msl4] * 2 + [mbd4] * 7 + [mtl4] * 2
            for g in range(18):
                def it_pair(g=g):
                    bank = ps_pair if g % 2 == 0 else ps_pair2
                    quad = pairL[g * 4:(g + 1) * 4]

                    @block.tensor
                    def _(t):
                        for j, (sa, si, ma, mi, dstt, di) in enumerate(quad):
                            jn = []
                            if j == 0:
                                jn = [bank_free[g % 2]]
                                if g == 0:
                                    jn.append(marks["reord"])
                            m.op("prep", t, t.matmul, bank[:, j * 128:(j + 1) * 128],
                                 sa[:, si * 128:(si + 1) * 128], ma[:, mi * 128:(mi + 1) * 128],
                                 start=(j == 0), stop=(j == 3), joins=jn)
                    mm_mark = m.mark("prep")

                    @block.scalar
                    def _(sc):
                        m.op("prep", sc, sc.copy, out=ptmp[g % 2][:], in_=bank[:], joins=[mm_mark])
                    bank_free[g % 2] = m.mark("prep")
                    ptm = bank_free[g % 2]

                    @block.gpsimd
                    def _(gp):
                        msk = mask4s[g]
                        base = pairL[g * 4][5]
                        m.op("prepg", gp, gp.tensor_tensor,
                             out=pairL[g * 4][4][:, base * 128:(base + 4) * 128],
                             in0=ptmp[g % 2][:], in1=msk[:], op=Op.mult, joins=[ptm])
                items.append(it_pair)
            return items

        def emit_main_out(n, feeder):
            p = n % 2
            fi = [0]

            def feed(k=1):
                while k > 0 and feeder and fi[0] < len(feeder):
                    feeder[fi[0]]()
                    fi[0] += 1
                    k -= 1

            for h in range(NH):
                @block.tensor
                def _(t, h=h):
                    m.op("main", t, t.matmul, ps_big[:, h * 64:(h + 1) * 64],
                         S_Bb[:, h * 128:(h + 1) * 128], Kc2[p][:, h * 64:(h + 1) * 64],
                         start=(h % 8 == 0), stop=(h % 8 == 7),
                         joins=[ld_done[n]] if h == 0 else ())

            @block.vector
            def _(v):
                for b in range(2):
                    bs = slice(b * 512, (b + 1) * 512)
                    eosl = eomres[:, n * 16 + b * 8: n * 16 + (b + 1) * 8]
                    m.op("main", v, v.tensor_tensor,
                         out=tmpw[:, bs].rearrange("p (h t) -> p h t", h=8),
                         in0=ps_big[:, bs].rearrange("p (h t) -> p h t", h=8),
                         in1=eosl.broadcast_to([128, 8, 64]), op=Op.mult)
                    m.op("main", v, v.tensor_tensor,
                         out=nrot[:].rearrange("p (s h t) -> p h s t", s=NSB, h=NH)[:, b * 8:(b + 1) * 8],
                         in0=tmpw[:, bs].rearrange("p (h s t) -> p h s t", h=8, s=NSB),
                         in1=P1c2[p][:].rearrange("p (h s t) -> p h s t", h=NH, s=NSB)[:, b * 8:(b + 1) * 8],
                         op=Op.subtract)

            for h in range(NH):
                @block.tensor
                def _(t, h=h):
                    m.op("main", t, t.matmul, ps_big[:, h * 64:(h + 1) * 64],
                         S_Bb[:, h * 128:(h + 1) * 128], Qc2[p][:, h * 64:(h + 1) * 64],
                         start=(h % 8 == 0), stop=(h % 8 == 7))

            @block.vector
            def _(v):
                for b in range(2):
                    bs = slice(b * 512, (b + 1) * 512)
                    eosl = eomres[:, n * 16 + b * 8: n * 16 + (b + 1) * 8]
                    m.op("main", v, v.tensor_tensor,
                         out=dtot[:].rearrange("p (s h t) -> p h s t", s=NSB, h=NH)[:, b * 8:(b + 1) * 8],
                         in0=ps_big[:, bs].rearrange("p (h s t) -> p h s t", h=8, s=NSB),
                         in1=eosl.broadcast_to([128, 8, 64]).rearrange("p h (s t) -> p h s t", s=NSB),
                         op=Op.mult)

            deferred = []

            def emit_deferred():
                if deferred:
                    deferred.pop(0)()

            for J in range(NSB):
                jsl = slice(J * 128, (J + 1) * 128)

                @block.tensor
                def _(t, J=J):
                    jn = list(prep_done[n]) if J == 0 and prep_done[n] else []
                    m.op("main", t, t.transpose, ps_sol[:], nrot[:, jsl], idf[:], joins=jn)
                    for i in range(J):
                        gi = gidx[(i, J)]
                        m.op("main", t, t.matmul, ps_sol[:], gms2[p][:, gi * 128:(gi + 1) * 128],
                             ysb[:, i * 128:(i + 1) * 128], start=False, stop=(i == J - 1))

                @block.vector
                def _(v, J=J):
                    jn = [ys_done[n - 1]] if J == 0 and n >= 1 else []
                    m.op("main", v, v.tensor_tensor, out=ysb[:, jsl], in0=ps_sol[:],
                         in1=nETs2[p][:, jsl], op=Op.mult, joins=jn)

                @block.vector
                def _(v):
                    m.op("main", v, v.tensor_copy, besb[:], ps_sol[:])
                if J >= 2:
                    feed(1)

                gd = gdiag[J]
                for r in range(SBL - 1):
                    @block.tensor
                    def _(t, J=J, gd=gd):
                        m.op("main", t, t.matmul, ps_sol[:], idb[:], besb[:], start=True, stop=False)
                        m.op("main", t, t.matmul, ps_sol[:], gms2[p][:, gd * 128:(gd + 1) * 128],
                             ysb[:, jsl], start=False, stop=True)

                    @block.vector
                    def _(v, J=J):
                        m.op("main", v, v.tensor_tensor, out=ysb[:, jsl], in0=ps_sol[:],
                             in1=nETs2[p][:, jsl], op=Op.mult)
                    if J >= 2:
                        feed(1)
                    elif J == 0 and r == 4 and fi[0] == 0:
                        feed(1)

                @block.sync
                def _(sync, J=J, n=n):
                    rnd = m.mark("main")
                    yout = AP(y_scr.tensor, (n * CH + J * 8) * 2048, [[128, 16], [2048, 8], [1, 128]])
                    m.dma("ys", sync, yout, ysb[:, jsl], joins=[rnd])

                @block.tensor
                def _(t, J=J):
                    m.op("main", t, t.transpose, ps_o2[J % 2][:], dtot[:, jsl], idf[:],
                         joins=[out_evac[J % 2]])
                    for i in range(J + 1):
                        hi = hidx[(i, J)] if i < J else hdiag[J]
                        m.op("main", t, t.matmul, ps_o2[J % 2][:], hms2[p][:, hi * 128:(hi + 1) * 128],
                             ysb[:, i * 128:(i + 1) * 128], start=False, stop=(i == J))
                main_o2 = m.mark("main")

                @block.vector
                def _(v, J=J):
                    m.op("out", v, v.tensor_tensor, out=obs[:], in0=ps_o2[J % 2][:],
                         in1=abTs2[p][:, jsl], op=Op.mult, joins=[main_o2])
                out_evac[J % 2] = m.mark("out")

                @block.gpsimd
                def _(gp):
                    m.op("out", gp, gp.tensor_tensor, out=obsq[:], in0=obs[:], in1=obs[:], op=Op.mult)

                @block.vector
                def _(v):
                    m.op("out", v, v.tensor_reduce, out=cols[:, 0:1], in_=obs[:],
                         axis=mybir.AxisListType.X, op=Op.add)
                    m.op("out", v, v.tensor_reduce, out=cols[:, 1:2], in_=obsq[:],
                         axis=mybir.AxisListType.X, op=Op.add)

                @block.gpsimd
                def _(gp):
                    m.op("out", gp, gp.tensor_scalar, out=cols[:, 0:1], in0=cols[:, 0:1],
                         scalar1=1.0 / DV, scalar2=None, op0=Op.mult)
                    m.op("out", gp, gp.tensor_scalar, out=cols[:, 1:2], in0=cols[:, 1:2],
                         scalar1=1.0 / DV, scalar2=None, op0=Op.mult)
                    m.op("out", gp, gp.tensor_tensor, out=cols[:, 2:3], in0=cols[:, 0:1],
                         in1=cols[:, 0:1], op=Op.mult)
                    m.op("out", gp, gp.tensor_tensor, out=cols[:, 3:4], in0=cols[:, 1:2],
                         in1=cols[:, 2:3], op=Op.subtract)

                @block.scalar
                def _(sc):
                    m.op("out", sc, sc.activation, out=cols[:, 4:5], in_=cols[:, 3:4],
                         func=AF.Sqrt, bias=epst[:])

                @block.vector
                def _(v, J=J):
                    m.op("out", v, v.reciprocal, out=cols[:, 5:6], in_=cols[:, 4:5])
                    m.op("out", v, v.tensor_scalar, out=ons2[J % 2][:], in0=obs[:],
                         scalar1=cols[:, 0:1], scalar2=cols[:, 5:6], op0=Op.subtract, op1=Op.mult)

                def part2(J=J):
                    @block.tensor
                    def _(t):
                        m.op("out", t, t.transpose, ps_tr[:], ons2[J % 2][:], idf[:])

                    @block.vector
                    def _(v):
                        m.op("out", v, v.tensor_scalar,
                             out=ofchunk[:].rearrange("p (h t) -> p h t", h=NH)[:, :, J * 8:(J + 1) * 8],
                             in0=ps_tr[:].rearrange("p (h t) -> p h t", h=NH),
                             scalar1=lnw[:], scalar2=lnb[:], op0=Op.mult, op1=Op.add)
                deferred.append(part2)
                emit_deferred() if J > 0 else None
                if J >= 2:
                    feed(1)

            while deferred:
                emit_deferred()
            ys_done[n] = m.mark("ys")

            @block.vector
            def _(v):
                m.op("out", v, v.tensor_tensor, out=ofb[:], in0=ofchunk[:], in1=Gc2[p][:], op=Op.mult)

            @block.sync
            def _(sync):
                m.dma("out", sync, hview(of_s)[:, :, slice(n * CH, (n + 1) * CH)],
                      ofb[:].rearrange("p (h t) -> p h t", h=NH))

            rb = (n % 2) * 64
            kTq = kT2[(n // 2) % 2]
            gath = [None] * 4

            @block.gpsimd
            def _(gp):
                for g4 in range(4):
                    m.dma("ys", gp, y2b[rb:rb + 64, g4 * 512:(g4 + 1) * 512],
                          y_scr[n * CH:(n + 1) * CH, g4 * 512:(g4 + 1) * 512],
                          joins=[ys_done[n]] if g4 == 0 else ())
                    gath[g4] = m.mark("ys")

            for g4 in range(4):
                gsl = slice(g4 * 512, (g4 + 1) * 512)

                @block.tensor
                def _(t, g4=g4, gsl=gsl):
                    m.op("main", t, t.matmul, ps_big[:, 512:1024], onrb[:],
                         erobb2[p][0:1, gsl], start=True, stop=True)
                bce = m.mark("main")

                @block.vector
                def _(v, g4=g4, gsl=gsl, bce=bce):
                    m.op("main", v, v.tensor_tensor, out=y2b[rb:rb + 64, gsl],
                         in0=y2b[rb:rb + 64, gsl], in1=ps_big[rb:rb + 64, 512:1024],
                         op=Op.mult, joins=[gath[g4], bce])

                @block.tensor
                def _(t, g4=g4, gsl=gsl):
                    for hh in range(4):
                        h = g4 * 4 + hh
                        m.op("main", t, t.matmul, ps_big[:, hh * 128:(hh + 1) * 128],
                             kTq[rb:rb + 64, h * 128:(h + 1) * 128],
                             y2b[rb:rb + 64, h * 128:(h + 1) * 128],
                             start=(hh == 0), stop=(hh == 3))
                    m.op("main", t, t.matmul, ps_big[:, 512:1024], onrb[:],
                         drobb2[p][0:1, gsl], start=True, stop=True)

                @block.vector
                def _(v, g4=g4, gsl=gsl):
                    m.op("main", v, v.tensor_tensor, out=S_B[:, gsl], in0=S_B[:, gsl],
                         in1=ps_big[:, 512:1024], op=Op.mult)
                    m.op("main", v, v.tensor_tensor, out=S_B[:, gsl], in0=S_B[:, gsl],
                         in1=ps_big[:, 0:512], op=Op.add)

            @block.vector
            def _(v):
                m.op("main", v, v.tensor_copy, S_Bb[:], S_B[:])
            cons_done[n] = m.mark("main")
            feed(99)

        emit_ld(0)
        items0 = prep_items(0)
        for it in items0:
            it()
        prep_done[0] = [m.mark("prep"), m.mark("prepg")]
        for n in range(NCH):
            feeder = None
            if n + 1 < NCH:
                emit_ld(n + 1)
                feeder = prep_items(n + 1)
            emit_main_out(n, feeder)
            if n + 1 < NCH:
                prep_done[n + 1] = [m.mark("prep"), m.mark("prepg")]

        for name in ("sync", "tensor", "vector", "scalar", "gpsimd"):
            @getattr(block, name)
            def _(eng):
                for sn in ("ld", "prep", "prepg", "main", "out", "ys"):
                    sm = m.streams[sn]
                    eng.wait_ge(sm.sem, sm.count)


# version 16
# speedup vs baseline: 1.2286x; 1.2286x over previous
import contextlib
import numpy as np
import ml_dtypes
import concourse.bass as bass
import concourse.mybir as mybir
from concourse.bass_utils import run_bass_kernel_spmd
from concourse.alu_op_type import AluOpType as Op
from concourse.bass import AP

F32 = mybir.dt.float32
BF16 = mybir.dt.bfloat16
AF = mybir.ActivationFunctionType
BF = ml_dtypes.bfloat16

D = 2048
T = 2048
NH = 16
DK = DV = 128
KW = 4
CH = 64
NCH = T // CH
SBL = 8
NSB = CH // SBL
NT = D // 128
EPS = 1e-5


class Ser:

    def __init__(self, sem):
        self.sem = sem
        self.count = 0
        self.last = None

    def _wait(self, eng):
        key = eng.engine
        if not (key == self.last and key == mybir.EngineType.PE):
            eng.wait_ge(self.sem, self.count)
        self.last = key

    def dma(self, eng, out, in_, **kw):
        self._wait(eng)
        eng.dma_start(out=out, in_=in_, **kw).then_inc(self.sem, 16)
        self.count += 16

    def op(self, eng, fn, *a, **kw):
        self._wait(eng)
        fn(*a, **kw).then_inc(self.sem, 1)
        self.count += 1



class Ctx:

    def __init__(self, fused=False):
        self.nc = bass.Bass("TRN2", target_bir_lowering=False, debug=False)
        self.fused = fused
        self._dram = {}
        self._semidx = 0
        self.stack = contextlib.ExitStack()
        self.block = None
        self.q = None

    def din(self, name, shape, dt):
        if name not in self._dram:
            self._dram[name] = self.nc.dram_tensor(name, shape, dt, kind="ExternalInput").ap()
        return self._dram[name]

    def dmid(self, name, shape, dt, producer=False):
        if name not in self._dram:
            kind = "Internal" if self.fused else ("ExternalOutput" if producer else "ExternalInput")
            self._dram[name] = self.nc.dram_tensor(name, shape, dt, kind=kind).ap()
        return self._dram[name]

    def dout(self, name, shape, dt):
        if name not in self._dram:
            self._dram[name] = self.nc.dram_tensor(name, shape, dt, kind="ExternalOutput").ap()
        return self._dram[name]

    def begin(self):
        self.block = self.stack.enter_context(self.nc.Block())
        self._new_ser()
        return self

    def _new_ser(self):
        sem = self.stack.enter_context(self.nc.semaphore(f"ss{self._semidx}"))
        self._semidx += 1
        self.q = Ser(sem)

    def phase(self):
        old_sem, old_cnt = self.q.sem, self.q.count
        blk = self.block
        for name in ("sync", "tensor", "vector", "scalar", "gpsimd"):
            @getattr(blk, name)
            def _(eng):
                eng.wait_ge(old_sem, old_cnt)
        self._new_ser()

    def finish(self):
        blk, q = self.block, self.q
        @blk.gpsimd
        def _(gp):
            gp.wait_ge(q.sem, q.count)
        self.stack.close()
        return self.nc


class Stream:
    def __init__(self, sem):
        self.sem = sem
        self.count = 0
        self.last = None


class MSer:

    def __init__(self, stack, nc, names):
        self.streams = {n: Stream(stack.enter_context(nc.semaphore("ms_" + n))) for n in names}
        self.paranoid = False
        self.noskip = False
        self.peonly = False

    def mark(self, name):
        st = self.streams[name]
        return (st.sem, st.count)

    def _waits(self, st, eng, joins):
        key = eng.engine
        skip_ok = (key == st.last and not joins and key == mybir.EngineType.PE)
        if self.noskip or not skip_ok:
            eng.wait_ge(st.sem, st.count)
            for (sem, cnt) in joins:
                if cnt > 0:
                    eng.wait_ge(sem, cnt)
            st.last = key

    def op(self, name, eng, fn, *a, joins=(), **kw):
        st = self.streams[name]
        joins = list(joins)
        if self.paranoid:
            joins += [(s2.sem, s2.count) for n2, s2 in self.streams.items() if n2 != name]
        self._waits(st, eng, list(joins))
        fn(*a, **kw).then_inc(st.sem, 1)
        st.count += 1

    def dma(self, name, eng, out, in_, joins=(), **kw):
        st = self.streams[name]
        joins = list(joins)
        if self.paranoid:
            joins += [(s2.sem, s2.count) for n2, s2 in self.streams.items() if n2 != name]
        self._waits(st, eng, list(joins))
        eng.dma_start(out=out, in_=in_, **kw).then_inc(st.sem, 16)
        st.count += 16



def emit_ln0(ctx, xname="x", outname="xn", pfx="n0"):
    nc, block, q = ctx.nc, ctx.block, ctx.q
    X = ctx.din(xname, [D, T], F32)
    ONESC = ctx.din("ONESC", [128, 1], F32)
    ONESR = ctx.din("ONESR", [1, 128], F32)
    XN = ctx.dmid(outname, [D, T], BF16, producer=True)
    XN32 = ctx.dmid(outname + "32", [D, T], F32, producer=True)
    W = 512
    if True:
        st = contextlib.ExitStack()
        sb = lambda name, shape, dt=F32: st.enter_context(nc.sbuf_tensor(pfx + name, shape, dt))
        ps = lambda name, shape: st.enter_context(nc.psum_tensor(pfx + name, shape, F32))
        xt = sb("xt", [128, NT * W])
        xsq = sb("xsq", [128, W])
        onc = sb("onc", [128, 1])
        onr = sb("onr", [1, 128])
        rows = sb("rows", [1, 6 * W])
        tmp = sb("tmp", [128, W])
        xnt = sb("xnt", [128, W], BF16)
        epst = sb("epst", [1, 1])
        ps_sum = ps("ps_sum", [1, W])
        ps_sq = ps("ps_sq", [1, W])
        ps_b1 = ps("ps_b1", [128, W])
        ps_b2 = ps("ps_b2", [128, W])

        @block.sync
        def _(sync):
            q.dma(sync, onc[:], ONESC)
            q.dma(sync, onr[:], ONESR)

        @block.vector
        def _(v):
            q.op(v, v.memset, epst[:], EPS)

        for cb in range(T // W):
            csl = slice(cb * W, (cb + 1) * W)

            @block.sync
            def _(sync):
                for k in range(NT):
                    q.dma(sync, xt[:, k * W:(k + 1) * W], X[k * 128:(k + 1) * 128, csl])

            for k in range(NT):
                @block.tensor
                def _(t, k=k):
                    q.op(t, t.matmul, ps_sum[:], onc[:], xt[:, k * W:(k + 1) * W],
                         start=(k == 0), stop=(k == NT - 1))

            for k in range(NT):
                @block.scalar
                def _(s, k=k):
                    q.op(s, s.activation, out=xsq[:], in_=xt[:, k * W:(k + 1) * W], func=AF.Square)

                @block.tensor
                def _(t, k=k):
                    q.op(t, t.matmul, ps_sq[:], onc[:], xsq[:],
                         start=(k == 0), stop=(k == NT - 1))

            @block.vector
            def _(v):
                mr, msq, m2, var, rstd, mneg = (rows[0:1, i * W:(i + 1) * W] for i in range(6))
                q.op(v, v.tensor_scalar, out=mr, in0=ps_sum[:], scalar1=1.0 / D, scalar2=None, op0=Op.mult)
                q.op(v, v.tensor_scalar, out=msq, in0=ps_sq[:], scalar1=1.0 / D, scalar2=None, op0=Op.mult)
                q.op(v, v.tensor_tensor, out=m2, in0=mr, in1=mr, op=Op.mult)
                q.op(v, v.tensor_tensor, out=var, in0=msq, in1=m2, op=Op.subtract)

            @block.scalar
            def _(s):
                var = rows[0:1, 3 * W:4 * W]
                q.op(s, s.activation, out=var, in_=var, func=AF.Sqrt, bias=epst[:])

            @block.vector
            def _(v):
                mr, msq, m2, var, rstd, mneg = (rows[0:1, i * W:(i + 1) * W] for i in range(6))
                q.op(v, v.reciprocal, out=rstd, in_=var)
                q.op(v, v.tensor_tensor, out=mneg, in0=mr, in1=rstd, op=Op.mult)
                q.op(v, v.tensor_scalar, out=mneg, in0=mneg, scalar1=-1.0, scalar2=None, op0=Op.mult)

            @block.tensor
            def _(t):
                q.op(t, t.matmul, ps_b1[:], onr[:], rows[0:1, 4 * W:5 * W], start=True, stop=True)
                q.op(t, t.matmul, ps_b2[:], onr[:], rows[0:1, 5 * W:6 * W], start=True, stop=True)

            for k in range(NT):
                @block.vector
                def _(v, k=k):
                    q.op(v, v.tensor_tensor, out=tmp[:], in0=xt[:, k * W:(k + 1) * W], in1=ps_b1[:], op=Op.mult)
                    q.op(v, v.tensor_tensor, out=tmp[:], in0=tmp[:], in1=ps_b2[:], op=Op.add)
                    q.op(v, v.tensor_copy, xnt[:], tmp[:])

                @block.sync
                def _(sync, k=k):
                    q.dma(sync, XN[k * 128:(k + 1) * 128, csl], xnt[:])
                    q.dma(sync, XN32[k * 128:(k + 1) * 128, csl], tmp[:])

        st.close()


def build_ln0():
    ctx = Ctx(fused=False).begin()
    emit_ln0(ctx)
    ctx.finish()
    return ctx.nc



def emit_proj(ctx, pfx="pj"):
    nc, block, q = ctx.nc, ctx.block, ctx.q
    XN = ctx.dmid("xn", [D, T], BF16)
    Ws = {n: ctx.din(n, [D, 2048], F32) for n in ("Wq", "Wk", "Wv", "Wa", "Wg")}
    Wb = ctx.din("Wb", [D, 16], F32)
    CW = {n: ctx.din(n, [2048, 4], F32) for n in ("CQW", "CKW", "CVW")}
    CB = {n: ctx.din(n, [128, 16], F32) for n in ("CQB", "CKB", "CVB")}
    BA = ctx.din("BA", [128, 16], F32)
    BB = ctx.din("BB", [16, 1], F32)
    GAM = ctx.din("GAM", [128, T], F32)
    IDENT = ctx.din("IDENT", [128, 128], F32)

    q_s = ctx.dmid("q_s", [2048, T], BF16, producer=True)
    k_s = ctx.dmid("k_s", [2048, T], BF16, producer=True)
    p1_s = ctx.dmid("p1_s", [2048, T], F32, producer=True)
    af_s = ctx.dmid("af_s", [2048, T], F32, producer=True)
    ia_s = ctx.dmid("ia_s", [2048, T], F32, producer=True)
    g_s = ctx.dmid("g_s", [2048, T], BF16, producer=True)
    bta_s = ctx.dmid("bta_s", [16, T], F32, producer=True)
    eom_s = ctx.dmid("eom_s", [128, NCH * NH], F32, producer=True)
    elc_s = ctx.dmid("elc_s", [128, NCH * NH], F32, producer=True)
    decrows_s = ctx.dmid("decrows_s", [NCH, 2048], F32, producer=True)
    elcrows_s = ctx.dmid("elcrows_s", [NCH, 2048], F32, producer=True)

    st = contextlib.ExitStack()
    sb = lambda name, shape, dt=F32: st.enter_context(nc.sbuf_tensor(pfx + name, shape, dt))
    xnt = sb("xnt", [128, NT * T], BF16)
    wpan = sb("wpan", [128, NT * 128], BF16)
    wbpan = sb("wbpan", [128, NT * 16], BF16)
    zt = sb("zt", [128, T]); tA = sb("tA", [128, T]); tB = sb("tB", [128, T])
    Lt = sb("Lt", [128, T]); lp = sb("lp", [128, T]); p1b = sb("p1b", [128, T])
    cvt = sb("cvt", [128, T]); vst = sb("vst", [128, T]); aft = sb("aft", [128, T])
    iat = sb("iat", [128, T])
    outb = sb("outb", [128, T], BF16)
    gam = sb("gam", [128, T])
    cw = sb("cw", [128, 3 * 64]); cb = sb("cb", [128, 3 * 16])
    bac = sb("bac", [128, 16]); bbc = sb("bbc", [16, 1])
    omt = sb("omt", [128, 32]); eomv = sb("eomv", [128, 32])
    elct = sb("elct", [128, 32]); dect = sb("dect", [128, 32])
    dtr = sb("dtr", [32, 128])
    eomres = sb("eomres", [128, NCH * NH]); elcres = sb("elcres", [128, NCH * NH])
    idf = sb("idf", [128, 128])
    btile = sb("btile", [16, T])
    psm = st.enter_context(nc.psum_tensor(pfx + "psm", [128, 2048], F32))
    psx = st.enter_context(nc.psum_tensor(pfx + "psx", [128, 512], F32))
    with st:
        pass

        @block.sync
        def _(sync):
            for k in range(NT):
                q.dma(sync, xnt[:, k * T:(k + 1) * T], XN[k * 128:(k + 1) * 128, :])
            q.dma(sync, gam[:], GAM)
            q.dma(sync, idf[:], IDENT)
            for i, n in enumerate(("CQW", "CKW", "CVW")):
                q.dma(sync, cw[:, i * 64:(i + 1) * 64].rearrange("p (k j) -> p k j", k=16),
                      CW[n].rearrange("(k p) j -> p k j", p=128))
            for i, n in enumerate(("CQB", "CKB", "CVB")):
                q.dma(sync, cb[:, i * 16:(i + 1) * 16], CB[n])
            q.dma(sync, bac[:], BA)
            q.dma(sync, bbc[:], BB)

        def load_panel(W_, h):
            @block.gpsimd
            def _(gp):
                q.dma(gp, wpan[:].rearrange("p (k c) -> p k c", k=NT),
                      W_.rearrange("(k p) c -> p k c", p=128)[:, :, h * 128:(h + 1) * 128])

        def matmul_tile():
            for tb in range(4):
                for kd in range(NT):
                    @block.tensor
                    def _(t, tb=tb, kd=kd):
                        q.op(t, t.matmul, psm[:, tb * 512:(tb + 1) * 512],
                             wpan[:, kd * 128:(kd + 1) * 128],
                             xnt[:, kd * T + tb * 512: kd * T + (tb + 1) * 512],
                             start=(kd == 0), stop=(kd == NT - 1))

        def conv(ci, h):
            w_j = lambda j: cw[:, ci * 64 + h * 4 + j: ci * 64 + h * 4 + j + 1]

            @block.vector
            def _(v):
                q.op(v, v.tensor_scalar, out=cvt[:], in0=psm[:],
                     scalar1=w_j(3), scalar2=None, op0=Op.mult)
                for sft in range(1, 4):
                    q.op(v, v.scalar_tensor_tensor, out=cvt[:, sft:T],
                         in0=psm[:, 0:T - sft], scalar=w_j(3 - sft),
                         in1=cvt[:, sft:T], op0=Op.mult, op1=Op.add)

        for h in range(NH):
            hsl = slice(h * 128, (h + 1) * 128)
            load_panel(Ws["Wa"], h)
            matmul_tile()

            @block.vector
            def _(v, h=h):
                q.op(v, v.tensor_scalar, out=zt[:], in0=psm[:],
                     scalar1=bac[:, h:h + 1], scalar2=None, op0=Op.add)

            @block.scalar
            def _(s):
                q.op(s, s.activation, out=tA[:], in_=zt[:], func=AF.Abs)
                q.op(s, s.activation, out=tB[:], in_=tA[:], func=AF.Exp, scale=-1.0)
                q.op(s, s.activation, out=tA[:], in_=tB[:], func=AF.Ln, bias=1.0)

            @block.vector
            def _(v):
                q.op(v, v.scalar_tensor_tensor, out=tB[:], in0=zt[:], scalar=0.0,
                     in1=tA[:], op0=Op.min, op1=Op.subtract)
                q.op(v, v.tensor_tensor_scan, out=Lt[:], data0=gam[:], data1=tB[:],
                     initial=0.0, op0=Op.mult, op1=Op.add)
                q.op(v, v.tensor_copy, omt[:], Lt[:, 31:T:64])

            @block.scalar
            def _(s, h=h):
                q.op(s, s.activation, out=eomv[:], in_=omt[:], func=AF.Exp)

            @block.vector
            def _(v, h=h):
                q.op(v, v.tensor_copy, eomres[:, h:NCH * NH:16], eomv[:])
                q.op(v, v.tensor_copy, tA[:].rearrange("p (n t) -> p n t", n=32),
                     omt[:].broadcast_to([128, 32, 64]))
                q.op(v, v.tensor_tensor, out=tB[:], in0=Lt[:], in1=tA[:], op=Op.subtract)
                q.op(v, v.tensor_copy, lp[:, 1:T], tB[:, 0:T - 1])
                q.op(v, v.tensor_scalar, out=lp[:, 0:T:64], in0=omt[:],
                     scalar1=-1.0, scalar2=None, op0=Op.mult)

            @block.scalar
            def _(s):
                q.op(s, s.activation, out=aft[:], in_=tB[:], func=AF.Exp)
                q.op(s, s.activation, out=p1b[:], in_=lp[:], func=AF.Exp, scale=-1.0)
                q.op(s, s.activation, out=tA[:], in_=zt[:], func=AF.Exp, scale=-1.0)

            @block.vector
            def _(v, h=h):
                q.op(v, v.tensor_scalar, out=iat[:], in0=tA[:], scalar1=1.0, scalar2=None, op0=Op.add)
                q.op(v, v.tensor_copy, elct[:], aft[:, 63:T:64])
                q.op(v, v.tensor_copy, elcres[:, h:NCH * NH:16], elct[:])
                q.op(v, v.tensor_tensor, out=dect[:], in0=elct[:], in1=eomv[:], op=Op.mult)

            @block.tensor
            def _(t):
                q.op(t, t.transpose, psx[0:32, 0:128], dect[:], idf[:])

            @block.vector
            def _(v):
                q.op(v, v.tensor_copy, dtr[:], psx[0:32, 0:128])

            @block.sync
            def _(sync, h=h):
                q.dma(sync, decrows_s[:, h * 128:(h + 1) * 128], dtr[:])

            @block.tensor
            def _(t):
                q.op(t, t.transpose, psx[0:32, 0:128], elct[:], idf[:])

            @block.vector
            def _(v):
                q.op(v, v.tensor_copy, dtr[:], psx[0:32, 0:128])

            @block.sync
            def _(sync, h=h):
                q.dma(sync, af_s[hsl, :], aft[:])
                q.dma(sync, ia_s[hsl, :], iat[:])
                q.dma(sync, elcrows_s[:, h * 128:(h + 1) * 128], dtr[:])

            load_panel(Ws["Wv"], h)
            matmul_tile()
            conv(2, h)

            @block.scalar
            def _(s, h=h):
                q.op(s, s.activation, out=vst[:], in_=cvt[:], func=AF.Silu,
                     bias=cb[:, 2 * 16 + h:2 * 16 + h + 1])

            @block.vector
            def _(v):
                q.op(v, v.tensor_tensor, out=tB[:], in0=p1b[:], in1=vst[:], op=Op.mult)

            @block.sync
            def _(sync, h=h):
                q.dma(sync, p1_s[hsl, :], tB[:])

            load_panel(Ws["Wq"], h)
            matmul_tile()
            conv(0, h)

            @block.scalar
            def _(s, h=h):
                q.op(s, s.activation, out=outb[:], in_=cvt[:], func=AF.Silu,
                     bias=cb[:, h:h + 1])

            @block.sync
            def _(sync, h=h):
                q.dma(sync, q_s[hsl, :], outb[:])

            load_panel(Ws["Wk"], h)
            matmul_tile()
            conv(1, h)

            @block.scalar
            def _(s, h=h):
                q.op(s, s.activation, out=vst[:], in_=cvt[:], func=AF.Silu,
                     bias=cb[:, 16 + h:16 + h + 1])

            @block.vector
            def _(v):
                q.op(v, v.tensor_scalar, out=outb[:], in0=vst[:],
                     scalar1=float(DK) ** -0.5, scalar2=None, op0=Op.mult)

            @block.sync
            def _(sync, h=h):
                q.dma(sync, k_s[hsl, :], outb[:])

            load_panel(Ws["Wg"], h)
            matmul_tile()

            @block.scalar
            def _(s):
                q.op(s, s.activation, out=outb[:], in_=psm[:], func=AF.Sigmoid)

            @block.sync
            def _(sync, h=h):
                q.dma(sync, g_s[hsl, :], outb[:])

        @block.gpsimd
        def _(gp):
            q.dma(gp, wbpan[:, 0:NT * 16].rearrange("p (k e) -> p k e", k=NT),
                  Wb.rearrange("(k p) e -> p k e", p=128))

        for tb in range(4):
            for kd in range(NT):
                @block.tensor
                def _(t, tb=tb, kd=kd):
                    q.op(t, t.matmul, psm[0:16, tb * 512:(tb + 1) * 512],
                         wbpan[:, kd * 16:(kd + 1) * 16],
                         xnt[:, kd * T + tb * 512: kd * T + (tb + 1) * 512],
                         start=(kd == 0), stop=(kd == NT - 1))

        @block.scalar
        def _(s):
            q.op(s, s.activation, out=btile[:], in_=psm[0:16, :], func=AF.Sigmoid, bias=bbc[:])

        @block.sync
        def _(sync):
            q.dma(sync, bta_s, btile[:])
            q.dma(sync, eom_s, eomres[:])
            q.dma(sync, elc_s, elcres[:])



def build_proj():
    ctx = Ctx(fused=False).begin()
    emit_proj(ctx)
    ctx.finish()
    return ctx.nc


def emit_delta(ctx, pfx="dl"):
    nc, block, q = ctx.nc, ctx.block, ctx.q
    q_s = ctx.dmid("q_s", [2048, T], BF16)
    k_s = ctx.dmid("k_s", [2048, T], BF16)
    g_s = ctx.dmid("g_s", [2048, T], BF16)
    p1_s = ctx.dmid("p1_s", [2048, T], F32)
    af_s = ctx.dmid("af_s", [2048, T], F32)
    ia_s = ctx.dmid("ia_s", [2048, T], F32)
    bta_s = ctx.dmid("bta_s", [16, T], F32)
    eom_s = ctx.dmid("eom_s", [128, NCH * NH], F32)
    elc_s = ctx.dmid("elc_s", [128, NCH * NH], F32)
    decrows_s = ctx.dmid("decrows_s", [NCH, 2048], F32)
    MBD = ctx.din("MBD", [128, 128], BF16)
    MSL = ctx.din("MSL", [128, 128], BF16)
    MTL = ctx.din("MTL", [128, 128], BF16)
    MBD4 = ctx.din("MBD4", [128, 512], BF16)
    MSL4 = ctx.din("MSL4", [128, 512], BF16)
    MTL4 = ctx.din("MTL4", [128, 512], BF16)
    IDENTB = ctx.din("IDENTB", [128, 128], BF16)
    IDENT = ctx.din("IDENT", [128, 128], F32)
    ONESR = ctx.din("ONESR", [1, 128], F32)
    LNW = ctx.din("LNW", [128, 1], F32)
    LNB = ctx.din("LNB", [128, 1], F32)
    of_s = ctx.dmid("of_s", [2048, T], BF16, producer=True)

    st = contextlib.ExitStack()
    sb = lambda name, shape, dt=F32: st.enter_context(nc.sbuf_tensor(pfx + name, shape, dt))
    Kc = sb("Kc", [128, 1024], BF16); Qc = sb("Qc", [128, 1024], BF16)
    Gc = sb("Gc", [128, 1024], BF16)
    Kord = sb("Kord", [128, 1024], BF16); Qord = sb("Qord", [128, 1024], BF16)
    P1c = sb("P1c", [128, 1024]); Afc = sb("Afc", [128, 1024]); IAc = sb("IAc", [128, 1024])
    iaord = sb("iaord", [128, 1024]); afod = sb("afod", [128, 1024])
    iaTs = sb("iaTs", [128, 1024]); abTs = sb("abTs", [128, 1024])
    nrot = sb("nrot", [128, 1024]); dtot = sb("dtot", [128, 1024])
    tmpw = sb("tmpw", [128, 1024])
    ysb = sb("ysb", [128, 1024], BF16)
    Ytb = sb("Ytb", [128, 1024], BF16); Ytbs = sb("Ytbs", [128, 1024], BF16)
    ktr = sb("ktr", [64, 2048], BF16); yh = sb("yh", [64, 128], BF16)
    ofchunk = sb("ofchunk", [128, 1024]); ofb = sb("ofb", [128, 1024], BF16)
    S_B = sb("S_B", [128, 2048]); S_Bb = sb("S_Bb", [128, 2048], BF16)
    eomres = sb("eomres", [128, NCH * NH]); elcres = sb("elcres", [128, NCH * NH])
    mbd = sb("mbd", [128, 128], BF16); msl = sb("msl", [128, 128], BF16)
    mtl = sb("mtl", [128, 128], BF16); idb = sb("idb", [128, 128], BF16)
    idf = sb("idf", [128, 128]); onr = sb("onr", [1, 128])
    lnw = sb("lnw", [128, 1]); lnb = sb("lnb", [128, 1]); epst = sb("epst", [128, 1])
    brow = sb("brow", [1, 1024])
    decrow = sb("decrow", [1, 2048])
    gmb = sb("gmb", [128, 128], BF16); gdb = sb("gdb", [128, 128], BF16)
    besb = sb("besb", [128, 128], BF16)
    obs = sb("obs", [128, 128]); obsq = sb("obsq", [128, 128])
    ons2 = [sb(f"ons{i}", [128, 128]) for i in range(2)]
    cols = sb("cols", [128, 8])
    ps_big = st.enter_context(nc.psum_tensor(pfx + "ps_big", [128, 1024], F32))
    ps_pair = st.enter_context(nc.psum_tensor(pfx + "ps_pair", [128, 128], F32))
    ps_sol = st.enter_context(nc.psum_tensor(pfx + "ps_sol", [128, 128], F32))
    ps_o2 = st.enter_context(nc.psum_tensor(pfx + "ps_o2", [128, 128], F32))
    ps_tr = st.enter_context(nc.psum_tensor(pfx + "ps_tr", [128, 128], F32))
    ps_trb = st.enter_context(nc.psum_tensor(pfx + "ps_trb", [128, 128], BF16))
    with st:
        pass

        @block.sync
        def _(sync):
            q.dma(sync, mbd[:], MBD); q.dma(sync, msl[:], MSL); q.dma(sync, mtl[:], MTL)
            q.dma(sync, idb[:], IDENTB); q.dma(sync, idf[:], IDENT); q.dma(sync, onr[:], ONESR)
            q.dma(sync, lnw[:], LNW); q.dma(sync, lnb[:], LNB)
            q.dma(sync, eomres[:], eom_s); q.dma(sync, elcres[:], elc_s)

        @block.vector
        def _(v):
            q.op(v, v.memset, epst[:], EPS)
            q.op(v, v.memset, S_B[:], 0.0)
            q.op(v, v.memset, S_Bb[:], 0.0)

        hview = lambda ap: ap.rearrange("(h p) t -> p h t", p=128)

        for n in range(NCH):
            csl = slice(n * CH, (n + 1) * CH)

            @block.sync
            def _(sync, n=n):
                q.dma(sync, Kc[:].rearrange("p (h t) -> p h t", h=NH), hview(k_s)[:, :, csl])
                q.dma(sync, Qc[:].rearrange("p (h t) -> p h t", h=NH), hview(q_s)[:, :, csl])
                q.dma(sync, Gc[:].rearrange("p (h t) -> p h t", h=NH), hview(g_s)[:, :, csl])
                q.dma(sync, P1c[:].rearrange("p (h t) -> p h t", h=NH), hview(p1_s)[:, :, csl])
                q.dma(sync, Afc[:].rearrange("p (h t) -> p h t", h=NH), hview(af_s)[:, :, csl])
                q.dma(sync, IAc[:].rearrange("p (h t) -> p h t", h=NH), hview(ia_s)[:, :, csl])
                q.dma(sync, decrow[:], decrows_s[n:n + 1, :])
                q.dma(sync, brow[:].rearrange("o (h t) -> o h t", h=NH),
                      bta_s.rearrange("(o h) t -> o h t", o=1)[0:1, :, csl])

            @block.vector
            def _(v, n=n):
                q.op(v, v.tensor_scalar, out=brow[:], in0=brow[:], scalar1=-1.0, scalar2=None, op0=Op.mult)

            @block.tensor
            def _(t):
                q.op(t, t.matmul, ps_big[:, 0:512], onr[:], brow[0:1, 0:512], start=True, stop=True)
                q.op(t, t.matmul, ps_big[:, 512:1024], onr[:], brow[0:1, 512:1024], start=True, stop=True)

            @block.vector
            def _(v, n=n):
                q.op(v, v.tensor_tensor, out=IAc[:], in0=IAc[:], in1=ps_big[:], op=Op.mult)
                for src, dst in ((Kc, Kord), (Qc, Qord), (IAc, iaord), (Afc, afod)):
                    q.op(v, v.tensor_copy,
                         dst[:].rearrange("p (s h t) -> p h s t", s=NSB, h=NH),
                         src[:].rearrange("p (h s t) -> p h s t", h=NH, s=NSB))

            for h in range(NH):
                @block.tensor
                def _(t, h=h):
                    q.op(t, t.matmul, ps_big[:, h * 64:(h + 1) * 64],
                         S_Bb[:, h * 128:(h + 1) * 128], Kc[:, h * 64:(h + 1) * 64],
                         start=(h % 8 == 0), stop=(h % 8 == 7))

            @block.vector
            def _(v, n=n):
                for b in range(2):
                    bs = slice(b * 512, (b + 1) * 512)
                    eosl = eomres[:, n * 16 + b * 8: n * 16 + (b + 1) * 8]
                    q.op(v, v.tensor_tensor,
                         out=tmpw[:, bs].rearrange("p (h t) -> p h t", h=8),
                         in0=ps_big[:, bs].rearrange("p (h t) -> p h t", h=8),
                         in1=eosl.broadcast_to([128, 8, 64]), op=Op.mult)
                    q.op(v, v.tensor_tensor,
                         out=nrot[:].rearrange("p (s h t) -> p h s t", s=NSB, h=NH)[:, b * 8:(b + 1) * 8],
                         in0=tmpw[:, bs].rearrange("p (h s t) -> p h s t", h=8, s=NSB),
                         in1=P1c[:].rearrange("p (h s t) -> p h s t", h=NH, s=NSB)[:, b * 8:(b + 1) * 8],
                         op=Op.subtract)

            for h in range(NH):
                @block.tensor
                def _(t, h=h):
                    q.op(t, t.matmul, ps_big[:, h * 64:(h + 1) * 64],
                         S_Bb[:, h * 128:(h + 1) * 128], Qc[:, h * 64:(h + 1) * 64],
                         start=(h % 8 == 0), stop=(h % 8 == 7))

            @block.vector
            def _(v, n=n):
                for b in range(2):
                    bs = slice(b * 512, (b + 1) * 512)
                    eosl = eomres[:, n * 16 + b * 8: n * 16 + (b + 1) * 8]
                    q.op(v, v.tensor_tensor,
                         out=dtot[:].rearrange("p (s h t) -> p h s t", s=NSB, h=NH)[:, b * 8:(b + 1) * 8],
                         in0=ps_big[:, bs].rearrange("p (h s t) -> p h s t", h=8, s=NSB),
                         in1=eosl.broadcast_to([128, 8, 64]).rearrange("p h (s t) -> p h s t", s=NSB),
                         op=Op.mult)

            for s_ in range(NSB):
                for src, dst in ((iaord, iaTs), (afod, abTs)):
                    @block.tensor
                    def _(t, s_=s_, src=src):
                        q.op(t, t.transpose, ps_tr[:], src[:, s_ * 128:(s_ + 1) * 128], idf[:])

                    @block.vector
                    def _(v, s_=s_, dst=dst):
                        q.op(v, v.tensor_copy, dst[:, s_ * 128:(s_ + 1) * 128], ps_tr[:])

            for J in range(NSB):
                jsl = slice(J * 128, (J + 1) * 128)

                @block.tensor
                def _(t, J=J):
                    q.op(t, t.transpose, ps_sol[:], nrot[:, jsl], idf[:])

                for i in range(J):
                    @block.tensor
                    def _(t, i=i, J=J):
                        q.op(t, t.matmul, ps_pair[:], Kord[:, i * 128:(i + 1) * 128],
                             Kord[:, J * 128:(J + 1) * 128], start=True, stop=True)

                    @block.vector
                    def _(v):
                        q.op(v, v.tensor_tensor, out=gmb[:], in0=ps_pair[:], in1=mbd[:], op=Op.mult)

                    @block.tensor
                    def _(t, i=i):
                        q.op(t, t.matmul, ps_sol[:], gmb[:], ysb[:, i * 128:(i + 1) * 128],
                             start=False, stop=False)

                @block.tensor
                def _(t, J=J):
                    q.op(t, t.matmul, ps_pair[:], Kord[:, jsl], Kord[:, jsl], start=True, stop=True)

                @block.vector
                def _(v):
                    q.op(v, v.tensor_tensor, out=gdb[:], in0=ps_pair[:], in1=msl[:], op=Op.mult)

                @block.vector
                def _(v, J=J):
                    q.op(v, v.tensor_tensor, out=ysb[:, jsl], in0=ps_sol[:],
                         in1=iaTs[:, jsl], op=Op.mult)

                @block.scalar
                def _(s):
                    q.op(s, s.copy, out=besb[:], in_=ps_sol[:])

                for r in range(SBL - 1):
                    @block.tensor
                    def _(t, J=J):
                        q.op(t, t.matmul, ps_sol[:], idb[:], besb[:], start=True, stop=False)
                        q.op(t, t.matmul, ps_sol[:], gdb[:], ysb[:, jsl], start=False, stop=True)

                    @block.vector
                    def _(v, J=J):
                        q.op(v, v.tensor_tensor, out=ysb[:, jsl], in0=ps_sol[:],
                             in1=iaTs[:, jsl], op=Op.mult)

                @block.tensor
                def _(t, J=J):
                    q.op(t, t.transpose, ps_o2[:], dtot[:, jsl], idf[:])

                for i in range(J + 1):
                    @block.tensor
                    def _(t, i=i, J=J):
                        q.op(t, t.matmul, ps_pair[:], Kord[:, i * 128:(i + 1) * 128],
                             Qord[:, J * 128:(J + 1) * 128], start=True, stop=True)

                    @block.vector
                    def _(v, i=i, J=J):
                        q.op(v, v.tensor_tensor, out=gmb[:], in0=ps_pair[:],
                             in1=(mtl[:] if i == J else mbd[:]), op=Op.mult)

                    @block.tensor
                    def _(t, i=i, J=J):
                        q.op(t, t.matmul, ps_o2[:], gmb[:], ysb[:, i * 128:(i + 1) * 128],
                             start=False, stop=(i == J))

                @block.vector
                def _(v, J=J):
                    q.op(v, v.tensor_tensor, out=obs[:], in0=ps_o2[:], in1=abTs[:, jsl], op=Op.mult)
                    q.op(v, v.tensor_reduce, out=cols[:, 0:1], in_=obs[:], axis=mybir.AxisListType.X, op=Op.add)
                    q.op(v, v.tensor_scalar, out=cols[:, 0:1], in0=cols[:, 0:1],
                         scalar1=1.0 / DV, scalar2=None, op0=Op.mult)

                @block.scalar
                def _(s):
                    q.op(s, s.activation, out=obsq[:], in_=obs[:], func=AF.Square)

                @block.vector
                def _(v):
                    q.op(v, v.tensor_reduce, out=cols[:, 1:2], in_=obsq[:], axis=mybir.AxisListType.X, op=Op.add)
                    q.op(v, v.tensor_scalar, out=cols[:, 1:2], in0=cols[:, 1:2],
                         scalar1=1.0 / DV, scalar2=None, op0=Op.mult)
                    q.op(v, v.tensor_tensor, out=cols[:, 2:3], in0=cols[:, 0:1], in1=cols[:, 0:1], op=Op.mult)
                    q.op(v, v.tensor_tensor, out=cols[:, 3:4], in0=cols[:, 1:2], in1=cols[:, 2:3], op=Op.subtract)

                @block.scalar
                def _(s):
                    q.op(s, s.activation, out=cols[:, 4:5], in_=cols[:, 3:4], func=AF.Sqrt, bias=epst[:])

                @block.vector
                def _(v):
                    q.op(v, v.reciprocal, out=cols[:, 5:6], in_=cols[:, 4:5])
                    q.op(v, v.tensor_scalar, out=ons[:], in0=obs[:],
                         scalar1=cols[:, 0:1], scalar2=cols[:, 5:6], op0=Op.subtract, op1=Op.mult)

                @block.tensor
                def _(t):
                    q.op(t, t.transpose, ps_tr[:], ons[:], idf[:])

                @block.vector
                def _(v, J=J):
                    q.op(v, v.tensor_scalar,
                         out=ofchunk[:].rearrange("p (h t) -> p h t", h=NH)[:, :, J * 8:(J + 1) * 8],
                         in0=ps_tr[:].rearrange("p (h t) -> p h t", h=NH),
                         scalar1=lnw[:], scalar2=lnb[:], op0=Op.mult, op1=Op.add)

            @block.vector
            def _(v):
                q.op(v, v.tensor_tensor, out=ofb[:], in0=ofchunk[:], in1=Gc[:], op=Op.mult)

            @block.sync
            def _(sync, n=n):
                q.dma(sync, hview(of_s)[:, :, csl], ofb[:].rearrange("p (h t) -> p h t", h=NH))

            for h in range(NH):
                @block.tensor
                def _(t, h=h):
                    q.op(t, t.transpose, ps_trb[0:64, :], Kc[:, h * 64:(h + 1) * 64], idb[:])

                @block.vector
                def _(v, h=h):
                    q.op(v, v.tensor_copy, ktr[:, h * 128:(h + 1) * 128], ps_trb[0:64, :])

            for s_ in range(NSB):
                @block.tensor
                def _(t, s_=s_):
                    q.op(t, t.transpose, ps_trb[:], ysb[:, s_ * 128:(s_ + 1) * 128], idb[:])

                @block.vector
                def _(v, s_=s_):
                    q.op(v, v.tensor_copy,
                         Ytb[:].rearrange("p (h t) -> p h t", h=NH)[:, :, s_ * 8:(s_ + 1) * 8],
                         ps_trb[:].rearrange("p (h t) -> p h t", h=NH))

            @block.vector
            def _(v, n=n):
                q.op(v, v.tensor_tensor, out=Ytbs[:].rearrange("p (h t) -> p h t", h=NH),
                     in0=Ytb[:].rearrange("p (h t) -> p h t", h=NH),
                     in1=elcres[:, n * 16:(n + 1) * 16].broadcast_to([128, 16, 64]), op=Op.mult)

            for hg in range(4):
                for hh in range(4):
                    h = hg * 4 + hh

                    @block.tensor
                    def _(t, h=h):
                        q.op(t, t.transpose, ps_trb[0:64, :], Ytbs[:, h * 64:(h + 1) * 64], idb[:])

                    @block.vector
                    def _(v):
                        q.op(v, v.tensor_copy, yh[:], ps_trb[0:64, :])

                    @block.tensor
                    def _(t, h=h, hh=hh):
                        q.op(t, t.matmul, ps_big[:, hh * 128:(hh + 1) * 128],
                             ktr[:, h * 128:(h + 1) * 128], yh[:],
                             start=(hh == 0), stop=(hh == 3))

                @block.tensor
                def _(t, hg=hg):
                    q.op(t, t.matmul, ps_big[:, 512:1024], onr[:],
                         decrow[0:1, hg * 512:(hg + 1) * 512], start=True, stop=True)

                @block.vector
                def _(v, hg=hg):
                    ssl = slice(hg * 512, (hg + 1) * 512)
                    q.op(v, v.tensor_tensor, out=S_B[:, ssl], in0=S_B[:, ssl],
                         in1=ps_big[:, 512:1024], op=Op.mult)
                    q.op(v, v.tensor_tensor, out=S_B[:, ssl], in0=S_B[:, ssl],
                         in1=ps_big[:, 0:512], op=Op.add)

            @block.vector
            def _(v):
                q.op(v, v.tensor_copy, S_Bb[:], S_B[:])



def build_delta():
    ctx = Ctx(fused=False).begin()
    emit_delta(ctx)
    ctx.finish()
    return ctx.nc


def emit_l0mix(ctx, pfx="m0"):
    nc, block, q = ctx.nc, ctx.block, ctx.q
    X = ctx.din("x", [D, T], F32)
    XN = ctx.dmid("xn", [D, T], BF16)
    XN32 = ctx.dmid("xn32", [D, T], F32)
    OF = ctx.dmid("of_s", [2048, T], BF16)
    Wo = ctx.din("Wo", [2048, 2048], F32)
    WG8 = ctx.din("WG8", [D, 8], F32)
    WET = ctx.din("WET", [8, D, 2048], F32)
    BE = ctx.din("BE", [8, 2048], F32)
    IDENT = ctx.din("IDENT", [128, 128], F32)
    IDENTB = ctx.din("IDENTB", [128, 128], BF16)
    ONESRB = ctx.din("ONESRB", [1, 128], BF16)
    X1 = ctx.dmid("x1_s", [D, T], F32, producer=True)
    xe_s = ctx.dmid("xe_s_" + pfx, [8, D, T], BF16, producer=True) if not ctx.fused else ctx.dmid("xe_s", [8, D, T], BF16)

    st = contextlib.ExitStack()
    sb = lambda name, shape, dt=F32: st.enter_context(nc.sbuf_tensor(pfx + name, shape, dt))
    lg = sb("lg", [128, 8]); ex = sb("ex", [128, 8]); mh = sb("mh", [128, 8])
    colsb = sb("colsb", [128, 8])
    gmb = sb("gmb", [128, 8], BF16)
    gRow = sb("gRow", [1, 8 * T], BF16)
    onrb = sb("onrb", [1, 128], BF16)
    idb = sb("idb", [128, 128], BF16)
    wgp = sb("wgp", [128, NT * 8])
    x32w = sb("x32w", [128, 8 * 128])
    big = sb("big", [128, 8 * NT * 512], BF16)
    xes = big
    oft = big
    wep = sb("wep", [128, 4 * NT * 128], BF16)
    xnt = wep
    beb = sb("beb", [8, 2048], BF16)
    gT8 = sb("gT8", [8, T], BF16)
    xt = sb("xt", [128, 512], F32)
    x1t = sb("x1t", [128, 512], F32)
    outt = sb("outt", [128, 512], F32)
    ps_out = st.enter_context(nc.psum_tensor(pfx + "ps_out", [128, 512], F32))
    ps_g = st.enter_context(nc.psum_tensor(pfx + "ps_g", [128, 512], F32))
    ps_lg = st.enter_context(nc.psum_tensor(pfx + "ps_lg", [128, 8], F32))
    ps_tb = st.enter_context(nc.psum_tensor(pfx + "ps_tb", [128, 128], BF16))
    with st:
        pass

        @block.sync
        def _(sync):
            q.dma(sync, onrb[:], ONESRB)
            q.dma(sync, idb[:], IDENTB)

        @block.sync
        def _(sync):
            q.dma(sync, wgp[:].rearrange("p (k e) -> p k e", k=NT),
                  WG8.rearrange("(k p) e -> p k e", p=128))

        @block.gpsimd
        def _(gp):
            q.dma(gp, beb[:], BE)

        for tt in range(NT):
            for half in range(2):
                @block.sync
                def _(sync, tt=tt, half=half):
                    q.dma(sync, x32w[:].rearrange("p (kd t) -> p kd t", kd=8),
                          XN32.rearrange("(kd p) t -> p kd t", p=128)[:, half * 8:(half + 1) * 8,
                                                                     tt * 128:(tt + 1) * 128])

                for kd8 in range(8):
                    @block.tensor
                    def _(t, kd8=kd8, half=half):
                        kd = half * 8 + kd8
                        q.op(t, t.matmul, ps_lg[:], x32w[:, kd8 * 128:(kd8 + 1) * 128],
                             wgp[:, kd * 8:(kd + 1) * 8], start=(kd == 0), stop=(kd == NT - 1))

            @block.vector
            def _(v):
                q.op(v, v.tensor_copy, lg[:], ps_lg[:])
                q.op(v, v.tensor_reduce, out=colsb[:, 0:1], in_=lg[:], axis=mybir.AxisListType.X, op=Op.max)
                q.op(v, v.tensor_scalar, out=colsb[:, 1:2], in0=colsb[:, 0:1],
                     scalar1=-1.0, scalar2=None, op0=Op.mult)

            @block.scalar
            def _(s):
                q.op(s, s.activation, out=ex[:], in_=lg[:], func=AF.Exp, bias=colsb[:, 1:2])

            @block.vector
            def _(v):
                q.op(v, v.tensor_reduce, out=colsb[:, 2:3], in_=ex[:], axis=mybir.AxisListType.X, op=Op.add)
                q.op(v, v.reciprocal, out=colsb[:, 3:4], in_=colsb[:, 2:3])
                q.op(v, v.tensor_tensor, out=mh[:], in0=lg[:],
                     in1=colsb[:, 0:1].broadcast_to([128, 8]), op=Op.is_equal)
                q.op(v, v.tensor_copy, colsb[:, 4:5], mh[:, 0:1])
                for e in range(1, 8):
                    q.op(v, v.tensor_scalar, out=colsb[:, 5:6], in0=colsb[:, 4:5],
                         scalar1=-1.0, scalar2=1.0, op0=Op.mult, op1=Op.add)
                    q.op(v, v.tensor_tensor, out=mh[:, e:e + 1], in0=mh[:, e:e + 1],
                         in1=colsb[:, 5:6], op=Op.mult)
                    q.op(v, v.tensor_tensor, out=colsb[:, 4:5], in0=colsb[:, 4:5],
                         in1=mh[:, e:e + 1], op=Op.max)
                q.op(v, v.tensor_scalar, out=gmb[:], in0=mh[:], scalar1=colsb[:, 3:4],
                     scalar2=None, op0=Op.mult)

            for e in range(8):
                @block.tensor
                def _(t, e=e):
                    q.op(t, t.transpose, ps_tb[0:1, :], gmb[:, e:e + 1], idb[:])

                @block.vector
                def _(v, e=e, tt=tt):
                    q.op(v, v.tensor_copy, gRow[0:1, e * T + tt * 128:e * T + (tt + 1) * 128],
                         ps_tb[0:1, 0:128])

        @block.sync
        def _(sync):
            for e in range(8):
                q.dma(sync, gT8[e:e + 1, :], gRow[0:1, e * T:(e + 1) * T])

        for tb in range(4):
            @block.sync
            def _(sync, tb=tb):
                q.dma(sync, xnt[:].rearrange("p (kd t) -> p kd t", kd=NT),
                      XN.rearrange("(kd p) t -> p kd t", p=128)[:, :, tb * 512:(tb + 1) * 512])

            for e in range(8):
                @block.tensor
                def _(t, e=e, tb=tb):
                    q.op(t, t.matmul, ps_g[:], onrb[:], gRow[0:1, e * T + tb * 512:e * T + (tb + 1) * 512],
                         start=True, stop=True)

                @block.vector
                def _(v, e=e):
                    for kd in range(NT):
                        q.op(v, v.tensor_tensor, out=xes[:, (e * NT + kd) * 512:(e * NT + kd + 1) * 512],
                             in0=xnt[:, kd * 512:(kd + 1) * 512], in1=ps_g[:], op=Op.mult)

            @block.sync
            def _(sync, tb=tb):
                for e in range(8):
                    q.dma(sync, xe_s[e].rearrange("(kd p) t -> p kd t", p=128)[:, :, tb * 512:(tb + 1) * 512],
                          xes[:, e * NT * 512:(e + 1) * NT * 512].rearrange("p (kd t) -> p kd t", kd=NT))

        for tb in range(4):
            @block.sync
            def _(sync, tb=tb):
                for e in range(8):
                    q.dma(sync, xes[:, e * NT * 512:(e + 1) * NT * 512].rearrange("p (kd t) -> p kd t", kd=NT),
                          xe_s[e].rearrange("(kd p) t -> p kd t", p=128)[:, :, tb * 512:(tb + 1) * 512])

            for k in range(NT):
                for eh in range(2):
                    @block.gpsimd
                    def _(gp, k=k, eh=eh):
                        for ei in range(4):
                            e = eh * 4 + ei
                            q.dma(gp, wep[:, ei * NT * 128:(ei + 1) * NT * 128].rearrange("p (kd c) -> p kd c", kd=NT),
                                  WET[e].rearrange("(kd p) c -> p kd c", p=128)[:, :, k * 128:(k + 1) * 128])

                    for ei in range(4):
                        e = eh * 4 + ei
                        for kd in range(NT):
                            @block.tensor
                            def _(t, e=e, ei=ei, kd=kd):
                                q.op(t, t.matmul, ps_out[:],
                                     wep[:, (ei * NT + kd) * 128:(ei * NT + kd + 1) * 128],
                                     xes[:, (e * NT + kd) * 512:(e * NT + kd + 1) * 512],
                                     start=(e == 0 and kd == 0), stop=False)

                @block.tensor
                def _(t, k=k, tb=tb):
                    q.op(t, t.matmul, ps_out[:], beb[:, k * 128:(k + 1) * 128],
                         gT8[:, tb * 512:(tb + 1) * 512], start=False, stop=True)

                @block.vector
                def _(v):
                    q.op(v, v.tensor_copy, outt[:], ps_out[:])

                @block.sync
                def _(sync, k=k, tb=tb):
                    q.dma(sync, X1[k * 128:(k + 1) * 128, tb * 512:(tb + 1) * 512], outt[:])

        @block.sync
        def _(sync):
            for hv in range(NT):
                q.dma(sync, oft[:, hv * T:(hv + 1) * T], OF[hv * 128:(hv + 1) * 128, :])

        for k in range(NT):
            @block.gpsimd
            def _(gp, k=k):
                q.dma(gp, wep[:, 0:NT * 128].rearrange("p (hv c) -> p hv c", hv=NT),
                      Wo.rearrange("(hv p) c -> p hv c", p=128)[:, :, k * 128:(k + 1) * 128])

            for tb in range(4):
                for hv in range(NT):
                    @block.tensor
                    def _(t, hv=hv, tb=tb):
                        q.op(t, t.matmul, ps_out[:], wep[:, hv * 128:(hv + 1) * 128],
                             oft[:, hv * T + tb * 512: hv * T + (tb + 1) * 512],
                             start=(hv == 0), stop=(hv == NT - 1))

                @block.sync
                def _(sync, k=k, tb=tb):
                    q.dma(sync, xt[:], X[k * 128:(k + 1) * 128, tb * 512:(tb + 1) * 512])
                    q.dma(sync, x1t[:], X1[k * 128:(k + 1) * 128, tb * 512:(tb + 1) * 512])

                @block.vector
                def _(v):
                    q.op(v, v.tensor_tensor, out=outt[:], in0=xt[:], in1=ps_out[:], op=Op.add)
                    q.op(v, v.tensor_tensor, out=outt[:], in0=outt[:], in1=x1t[:], op=Op.add)

                @block.sync
                def _(sync, k=k, tb=tb):
                    q.dma(sync, X1[k * 128:(k + 1) * 128, tb * 512:(tb + 1) * 512], outt[:])



def build_l0mix():
    ctx = Ctx(fused=False).begin()
    emit_l0mix(ctx)
    ctx.finish()
    return ctx.nc


def emit_l1(ctx, pfx="l1"):
    nc, block, q = ctx.nc, ctx.block, ctx.q
    X1 = ctx.dmid("x1_s", [D, T], F32)
    WAT = ctx.din("WAT", [D, 64], F32)
    WBT = ctx.din("WBT", [D, 64], F32)
    BA1 = ctx.din("BA1", [64, 1], F32)
    BB1 = ctx.din("BB1", [64, 1], F32)
    WOT = ctx.din("WOT", [64, D], F32)
    GWT = ctx.din("GWT", [D, 8], F32)
    WDT = ctx.din("WDT", [8, D, 512], F32)
    WUT = ctx.din("WUT", [8, 512, D], F32)
    BDC = ctx.din("BDC", [128, 8 * 4], F32)
    BU = ctx.din("BU", [8, D], F32)
    WSDT = ctx.din("WSDT", [D, 512], F32)
    BSDC = ctx.din("BSDC", [128, 4], F32)
    WSUT = ctx.din("WSUT", [512, D], F32)
    BSUC = ctx.din("BSUC", [128, 16], F32)
    BOUTC = ctx.din("BOUTC", [128, 16], F32)
    ONESC = ctx.din("ONESC", [128, 1], F32)
    ONESR = ctx.din("ONESR", [1, 128], F32)
    IDENTB = ctx.din("IDENTB", [128, 128], BF16)
    ONESRB = ctx.din("ONESRB", [1, 128], BF16)
    OUT = ctx.dout("out_s", [D, T], F32)
    X132 = ctx.dmid("xn132_i", [D, T], F32) if ctx.fused else nc.dram_tensor("xn132_i", [D, T], F32, kind="Internal").ap()

    st = contextlib.ExitStack()
    sb = lambda name, shape, dt=F32: st.enter_context(nc.sbuf_tensor(pfx + name, shape, dt))
    xnt1 = sb("xnt1", [128, NT * T], BF16)
    xt = sb("xt", [128, 512]); tmp = sb("tmp", [128, 512]); outt = sb("outt", [128, 512])
    epst = sb("epst", [128, 1])
    onc = sb("onc", [128, 1]); onr = sb("onr", [1, 128]); onrb = sb("onrb", [1, 128], BF16)
    idb = sb("idb", [128, 128], BF16)
    ba1 = sb("ba1", [64, 1]); bb1 = sb("bb1", [64, 1])
    hsb = sb("hsb", [64, T], BF16)
    lg = sb("lg", [128, 8]); ex = sb("ex", [128, 8]); mh = sb("mh", [128, 8])
    mh2 = sb("mh2", [128, 8]); pp = sb("pp", [128, 8]); colsb = sb("colsb", [128, 10])
    cmb = sb("cmb", [128, 8], BF16)
    cRow = sb("cRow", [1, 8 * T], BF16)
    cT8 = sb("cT8", [8, T], BF16)
    bdc = sb("bdc", [128, 8 * 4]); bsdc = sb("bsdc", [128, 4])
    bub = sb("bub", [8, D], BF16)
    bcol = sb("bcol", [128, 16])
    x1t = sb("x1t", [128, 512])
    stA = contextlib.ExitStack()
    sba = lambda name, shape, dt=F32: stA.enter_context(nc.sbuf_tensor(pfx + name, shape, dt))
    x1w = sba("x1w", [128, NT * 512])
    xsq = sba("xsq", [128, 512])
    rows = sba("rows", [1, 6 * 512])
    watp = sba("watp", [128, NT * 64], BF16); wbtp = sba("wbtp", [128, NT * 64], BF16)
    aa = sba("aa", [64, T]); ab = sba("ab", [64, T])
    gwtp = sba("gwtp", [128, NT * 8]); x32w = sba("x32w", [128, NT * 128])
    late = {}

    def _alloc_ffn():
        stA.close()
        late["wdtp"] = sb("wdtp", [128, NT * 128], BF16)
        late["wutp"] = sb("wutp", [128, 32 * 128], BF16)
        late["wsutp"] = sb("wsutp", [128, 4 * 128], BF16)
        late["wotp"] = sb("wotp", [64, 128], BF16)
        late["gsb"] = sb("gsb", [128, 512])
        late["he"] = sb("he", [128, 8 * 4 * 512], BF16)
        late["hsh"] = sb("hsh", [128, 4 * 512], BF16)
    ps_out = st.enter_context(nc.psum_tensor(pfx + "ps_out", [128, 512], F32))
    ps_g = st.enter_context(nc.psum_tensor(pfx + "ps_g", [128, 512], F32))
    ps_a = st.enter_context(nc.psum_tensor(pfx + "ps_a", [128, 512], F32))
    ps_lg = st.enter_context(nc.psum_tensor(pfx + "ps_lg", [128, 8], F32))
    ps_sum = st.enter_context(nc.psum_tensor(pfx + "ps_sum", [1, 512], F32))
    ps_sq = st.enter_context(nc.psum_tensor(pfx + "ps_sq", [1, 512], F32))
    ps_b1 = st.enter_context(nc.psum_tensor(pfx + "ps_b1", [128, 512], F32))
    ps_tb = st.enter_context(nc.psum_tensor(pfx + "ps_tb", [128, 128], BF16))
    with st:
        pass

        @block.sync
        def _(sync):
            q.dma(sync, onc[:], ONESC); q.dma(sync, onr[:], ONESR); q.dma(sync, onrb[:], ONESRB)
            q.dma(sync, idb[:], IDENTB)
            q.dma(sync, ba1[:], BA1); q.dma(sync, bb1[:], BB1)
            q.dma(sync, bdc[:], BDC); q.dma(sync, bsdc[:], BSDC)
            q.dma(sync, gwtp[:].rearrange("p (k e) -> p k e", k=NT),
                  GWT.rearrange("(k p) e -> p k e", p=128))

        @block.gpsimd
        def _(gp):
            q.dma(gp, watp[:].rearrange("p (k c) -> p k c", k=NT),
                  WAT.rearrange("(k p) c -> p k c", p=128))
            q.dma(gp, wbtp[:].rearrange("p (k c) -> p k c", k=NT),
                  WBT.rearrange("(k p) c -> p k c", p=128))
            q.dma(gp, bub[:], BU)

        @block.vector
        def _(v):
            q.op(v, v.memset, epst[:], EPS)

        @block.sync
        def _(sync):
            q.dma(sync, xt[:, 0:16], BSUC[:, 0:16])
            q.dma(sync, tmp[:, 0:16], BOUTC[:, 0:16])

        @block.vector
        def _(v):
            q.op(v, v.tensor_tensor, out=bcol[:], in0=xt[:, 0:16], in1=tmp[:, 0:16], op=Op.add)

        for cb in range(4):
            csl = slice(cb * 512, (cb + 1) * 512)

            @block.sync
            def _(sync, cb=cb):
                q.dma(sync, x1w[:].rearrange("p (kd t) -> p kd t", kd=NT),
                      X1.rearrange("(kd p) t -> p kd t", p=128)[:, :, csl])

            for k in range(NT):
                @block.tensor
                def _(t, k=k):
                    q.op(t, t.matmul, ps_sum[:], onc[:], x1w[:, k * 512:(k + 1) * 512],
                         start=(k == 0), stop=(k == NT - 1))

                @block.scalar
                def _(s, k=k):
                    q.op(s, s.activation, out=xsq[:], in_=x1w[:, k * 512:(k + 1) * 512], func=AF.Square)

                @block.tensor
                def _(t, k=k):
                    q.op(t, t.matmul, ps_sq[:], onc[:], xsq[:], start=(k == 0), stop=(k == NT - 1))

            @block.vector
            def _(v):
                mr, msq, m2, var, rstd, mneg = (rows[0:1, i * 512:(i + 1) * 512] for i in range(6))
                q.op(v, v.tensor_scalar, out=mr, in0=ps_sum[:], scalar1=1.0 / D, scalar2=None, op0=Op.mult)
                q.op(v, v.tensor_scalar, out=msq, in0=ps_sq[:], scalar1=1.0 / D, scalar2=None, op0=Op.mult)
                q.op(v, v.tensor_tensor, out=m2, in0=mr, in1=mr, op=Op.mult)
                q.op(v, v.tensor_tensor, out=var, in0=msq, in1=m2, op=Op.subtract)

            @block.scalar
            def _(s):
                var = rows[0:1, 3 * 512:4 * 512]
                q.op(s, s.activation, out=var, in_=var, func=AF.Sqrt, bias=epst[0:1, :])

            @block.vector
            def _(v):
                mr, msq, m2, var, rstd, mneg = (rows[0:1, i * 512:(i + 1) * 512] for i in range(6))
                q.op(v, v.reciprocal, out=rstd, in_=var)
                q.op(v, v.tensor_tensor, out=mneg, in0=mr, in1=rstd, op=Op.mult)
                q.op(v, v.tensor_scalar, out=mneg, in0=mneg, scalar1=-1.0, scalar2=None, op0=Op.mult)

            @block.tensor
            def _(t):
                q.op(t, t.matmul, ps_b1[:], onr[:], rows[0:1, 4 * 512:5 * 512], start=True, stop=True)
                q.op(t, t.matmul, ps_g[:], onr[:], rows[0:1, 5 * 512:6 * 512], start=True, stop=True)

            for k in range(NT):
                @block.vector
                def _(v, k=k, cb=cb):
                    q.op(v, v.tensor_tensor, out=tmp[:], in0=x1w[:, k * 512:(k + 1) * 512], in1=ps_b1[:], op=Op.mult)
                    q.op(v, v.tensor_tensor, out=tmp[:], in0=tmp[:], in1=ps_g[:], op=Op.add)
                    q.op(v, v.tensor_copy, xnt1[:, k * T + cb * 512:k * T + (cb + 1) * 512], tmp[:])

                @block.sync
                def _(sync, k=k, cb=cb):
                    q.dma(sync, X132[k * 128:(k + 1) * 128, csl], tmp[:])

        for which, wp, bias, dst in ((0, watp, ba1, aa), (1, wbtp, bb1, ab)):
            for tb in range(4):
                for kd in range(NT):
                    @block.tensor
                    def _(t, wp=wp, kd=kd, tb=tb):
                        q.op(t, t.matmul, ps_a[0:64, :], wp[:, kd * 64:(kd + 1) * 64],
                             xnt1[:, kd * T + tb * 512:kd * T + (tb + 1) * 512],
                             start=(kd == 0), stop=(kd == NT - 1))

                @block.scalar
                def _(s, bias=bias, dst=dst, tb=tb):
                    q.op(s, s.activation, out=dst[:, tb * 512:(tb + 1) * 512],
                         in_=ps_a[0:64, :], func=AF.Tanh, bias=bias[:])

        @block.vector
        def _(v):
            q.op(v, v.tensor_tensor_scan, out=hsb[:], data0=aa[:], data1=ab[:],
                 initial=0.0, op0=Op.mult, op1=Op.add)

        for tt in range(NT):
            @block.sync
            def _(sync, tt=tt):
                q.dma(sync, x32w[:].rearrange("p (kd t) -> p kd t", kd=NT),
                      X132.rearrange("(kd p) t -> p kd t", p=128)[:, :, tt * 128:(tt + 1) * 128])

            for kd in range(NT):
                @block.tensor
                def _(t, kd=kd):
                    q.op(t, t.matmul, ps_lg[:], x32w[:, kd * 128:(kd + 1) * 128], gwtp[:, kd * 8:(kd + 1) * 8],
                         start=(kd == 0), stop=(kd == NT - 1))

            @block.vector
            def _(v):
                q.op(v, v.tensor_copy, lg[:], ps_lg[:])
                q.op(v, v.tensor_reduce, out=colsb[:, 0:1], in_=lg[:], axis=mybir.AxisListType.X, op=Op.max)
                q.op(v, v.tensor_scalar, out=colsb[:, 1:2], in0=colsb[:, 0:1],
                     scalar1=-1.0, scalar2=None, op0=Op.mult)

            @block.scalar
            def _(s):
                q.op(s, s.activation, out=ex[:], in_=lg[:], func=AF.Exp, bias=colsb[:, 1:2])

            @block.vector
            def _(v):
                q.op(v, v.tensor_reduce, out=colsb[:, 2:3], in_=ex[:], axis=mybir.AxisListType.X, op=Op.add)
                q.op(v, v.reciprocal, out=colsb[:, 3:4], in_=colsb[:, 2:3])
                q.op(v, v.tensor_scalar, out=pp[:], in0=ex[:], scalar1=colsb[:, 3:4],
                     scalar2=None, op0=Op.mult)
                q.op(v, v.tensor_reduce, out=colsb[:, 4:5], in_=pp[:], axis=mybir.AxisListType.X, op=Op.max)
                q.op(v, v.tensor_tensor, out=mh[:], in0=pp[:],
                     in1=colsb[:, 4:5].broadcast_to([128, 8]), op=Op.is_equal)
                q.op(v, v.tensor_copy, colsb[:, 6:7], mh[:, 0:1])
                for e in range(1, 8):
                    q.op(v, v.tensor_scalar, out=colsb[:, 7:8], in0=colsb[:, 6:7],
                         scalar1=-1.0, scalar2=1.0, op0=Op.mult, op1=Op.add)
                    q.op(v, v.tensor_tensor, out=mh[:, e:e + 1], in0=mh[:, e:e + 1],
                         in1=colsb[:, 7:8], op=Op.mult)
                    q.op(v, v.tensor_tensor, out=colsb[:, 6:7], in0=colsb[:, 6:7],
                         in1=mh[:, e:e + 1], op=Op.max)
                q.op(v, v.tensor_scalar, out=colsb[:, 7:8], in0=colsb[:, 6:7],
                     scalar1=-1.0, scalar2=1.0, op0=Op.mult, op1=Op.add)
                q.op(v, v.tensor_tensor, out=mh2[:], in0=mh[:],
                     in1=mh[:], op=Op.mult)
                q.op(v, v.tensor_scalar, out=mh2[:], in0=mh[:], scalar1=-1.0, scalar2=1.0,
                     op0=Op.mult, op1=Op.add)
                q.op(v, v.tensor_tensor, out=mh2[:], in0=pp[:], in1=mh2[:], op=Op.mult)
                q.op(v, v.tensor_reduce, out=colsb[:, 5:6], in_=mh2[:], axis=mybir.AxisListType.X, op=Op.max)
                q.op(v, v.tensor_tensor, out=mh2[:], in0=mh2[:],
                     in1=colsb[:, 5:6].broadcast_to([128, 8]), op=Op.is_equal)
                q.op(v, v.tensor_copy, colsb[:, 8:9], mh2[:, 0:1])
                for e in range(1, 8):
                    q.op(v, v.tensor_scalar, out=colsb[:, 9:10], in0=colsb[:, 8:9],
                         scalar1=-1.0, scalar2=1.0, op0=Op.mult, op1=Op.add)
                    q.op(v, v.tensor_tensor, out=mh2[:, e:e + 1], in0=mh2[:, e:e + 1],
                         in1=colsb[:, 9:10], op=Op.mult)
                    q.op(v, v.tensor_tensor, out=colsb[:, 8:9], in0=colsb[:, 8:9],
                         in1=mh2[:, e:e + 1], op=Op.max)
                q.op(v, v.tensor_tensor, out=colsb[:, 6:7], in0=colsb[:, 4:5],
                     in1=colsb[:, 5:6], op=Op.add)
                q.op(v, v.reciprocal, out=colsb[:, 7:8], in_=colsb[:, 6:7])
                q.op(v, v.tensor_tensor, out=colsb[:, 8:9], in0=colsb[:, 4:5],
                     in1=colsb[:, 7:8], op=Op.mult)
                q.op(v, v.tensor_tensor, out=colsb[:, 9:10], in0=colsb[:, 5:6],
                     in1=colsb[:, 7:8], op=Op.mult)
                q.op(v, v.tensor_scalar, out=mh[:], in0=mh[:], scalar1=colsb[:, 8:9],
                     scalar2=None, op0=Op.mult)
                q.op(v, v.scalar_tensor_tensor, out=pp[:], in0=mh2[:], scalar=colsb[:, 9:10],
                     in1=mh[:], op0=Op.mult, op1=Op.add)
                q.op(v, v.tensor_copy, cmb[:], pp[:])

            for e in range(8):
                @block.tensor
                def _(t, e=e):
                    q.op(t, t.transpose, ps_tb[0:1, :], cmb[:, e:e + 1], idb[:])

                @block.vector
                def _(v, e=e, tt=tt):
                    q.op(v, v.tensor_copy, cRow[0:1, e * T + tt * 128:e * T + (tt + 1) * 128],
                         ps_tb[0:1, 0:128])

        @block.sync
        def _(sync):
            for e in range(8):
                q.dma(sync, cT8[e:e + 1, :], cRow[0:1, e * T:(e + 1) * T])

        _alloc_ffn()
        wdtp = late["wdtp"]; wutp = late["wutp"]; wsutp = late["wsutp"]
        wotp = late["wotp"]; gsb = late["gsb"]; he = late["he"]; hsh = late["hsh"]

        for tb in range(4):
            tsl = slice(tb * 512, (tb + 1) * 512)
            for e in range(8):
                for bt in range(4):
                    @block.gpsimd
                    def _(gp, e=e, bt=bt):
                        q.dma(gp, wdtp[:].rearrange("p (kd c) -> p kd c", kd=NT),
                              WDT[e].rearrange("(kd p) c -> p kd c", p=128)[:, :, bt * 128:(bt + 1) * 128])

                    for kd in range(NT):
                        @block.tensor
                        def _(t, kd=kd, tb=tb):
                            q.op(t, t.matmul, ps_a[:], wdtp[:, kd * 128:(kd + 1) * 128],
                                 xnt1[:, kd * T + tb * 512:kd * T + (tb + 1) * 512],
                                 start=(kd == 0), stop=(kd == NT - 1))

                    @block.scalar
                    def _(s, e=e, bt=bt):
                        q.op(s, s.activation, out=gsb[:], in_=ps_a[:], func=AF.Gelu,
                             bias=bdc[:, e * 4 + bt:e * 4 + bt + 1])

                    @block.tensor
                    def _(t, e=e, tb=tb):
                        q.op(t, t.matmul, ps_g[:], onrb[:], cRow[0:1, e * T + tb * 512:e * T + (tb + 1) * 512],
                             start=True, stop=True)

                    @block.vector
                    def _(v, e=e, bt=bt):
                        q.op(v, v.tensor_tensor, out=he[:, (e * 4 + bt) * 512:(e * 4 + bt + 1) * 512],
                             in0=gsb[:], in1=ps_g[:], op=Op.mult)

            for bt in range(4):
                @block.gpsimd
                def _(gp, bt=bt):
                    q.dma(gp, wdtp[:].rearrange("p (kd c) -> p kd c", kd=NT),
                          WSDT.rearrange("(kd p) c -> p kd c", p=128)[:, :, bt * 128:(bt + 1) * 128])

                for kd in range(NT):
                    @block.tensor
                    def _(t, kd=kd, tb=tb):
                        q.op(t, t.matmul, ps_a[:], wdtp[:, kd * 128:(kd + 1) * 128],
                             xnt1[:, kd * T + tb * 512:kd * T + (tb + 1) * 512],
                             start=(kd == 0), stop=(kd == NT - 1))

                @block.scalar
                def _(s, bt=bt):
                    q.op(s, s.activation, out=hsh[:, bt * 512:(bt + 1) * 512], in_=ps_a[:],
                         func=AF.Gelu, bias=bsdc[:, bt:bt + 1])

            for k in range(NT):
                @block.gpsimd
                def _(gp, k=k):
                    for e in range(8):
                        q.dma(gp, wutp[:, e * 4 * 128:(e + 1) * 4 * 128].rearrange("p (bt c) -> p bt c", bt=4),
                              WUT[e].rearrange("(bt p) c -> p bt c", p=128)[:, :, k * 128:(k + 1) * 128])
                    q.dma(gp, wsutp[:].rearrange("p (bt c) -> p bt c", bt=4),
                          WSUT.rearrange("(bt p) c -> p bt c", p=128)[:, :, k * 128:(k + 1) * 128])
                    q.dma(gp, wotp[:], WOT[:, k * 128:(k + 1) * 128])

                for e in range(8):
                    for bt in range(4):
                        @block.tensor
                        def _(t, e=e, bt=bt):
                            q.op(t, t.matmul, ps_out[:],
                                 wutp[:, (e * 4 + bt) * 128:(e * 4 + bt + 1) * 128],
                                 he[:, (e * 4 + bt) * 512:(e * 4 + bt + 1) * 512],
                                 start=(e == 0 and bt == 0), stop=False)

                for bt in range(4):
                    @block.tensor
                    def _(t, bt=bt):
                        q.op(t, t.matmul, ps_out[:], wsutp[:, bt * 128:(bt + 1) * 128],
                             hsh[:, bt * 512:(bt + 1) * 512], start=False, stop=False)

                @block.tensor
                def _(t, k=k, tb=tb):
                    q.op(t, t.matmul, ps_out[:], bub[:, k * 128:(k + 1) * 128], cT8[:, tsl],
                         start=False, stop=False)
                    q.op(t, t.matmul, ps_out[:], wotp[:], hsb[:, tsl], start=False, stop=True)

                @block.sync
                def _(sync, k=k, tb=tb):
                    q.dma(sync, x1t[:], X1[k * 128:(k + 1) * 128, tsl])

                @block.vector
                def _(v, k=k, tb=tb):
                    q.op(v, v.tensor_tensor, out=outt[:], in0=x1t[:], in1=ps_out[:], op=Op.add)
                    q.op(v, v.tensor_tensor, out=outt[:], in0=outt[:],
                         in1=xnt1[:, k * T + tb * 512:k * T + (tb + 1) * 512], op=Op.add)
                    q.op(v, v.tensor_scalar, out=outt[:], in0=outt[:],
                         scalar1=bcol[:, k:k + 1], scalar2=None, op0=Op.add)

                @block.sync
                def _(sync, k=k, tb=tb):
                    q.dma(sync, OUT[k * 128:(k + 1) * 128, tsl], outt[:])



def build_l1():
    ctx = Ctx(fused=False).begin()
    emit_l1(ctx)
    ctx.finish()
    return ctx.nc


def _consts():
    GAM = np.ones((128, T), np.float32)
    GAM[:, ::CH] = 0.0
    p = np.arange(128)
    c = np.arange(128)
    bd = (p[:, None] // 8 == c[None, :] // 8)
    return dict(
        GAM=GAM,
        IDENT=np.eye(128, dtype=np.float32),
        IDENTB=np.eye(128, dtype=np.float32).astype(BF),
        ONESC=np.ones((128, 1), np.float32),
        ONESR=np.ones((1, 128), np.float32),
        NONESR=-np.ones((1, 128), np.float32),
        ONESRB=np.ones((1, 128), np.float32).astype(BF),
        MBD=bd.astype(BF),
        MSL=(bd & ((p[:, None] % 8) < (c[None, :] % 8))).astype(BF),
        MTL=(bd & ((p[:, None] % 8) <= (c[None, :] % 8))).astype(BF),
        MBD4=np.tile(bd.astype(BF), (1, 4)),
        MSL4=np.tile((bd & ((p[:, None] % 8) < (c[None, :] % 8))).astype(BF), (1, 4)),
        MTL4=np.tile((bd & ((p[:, None] % 8) <= (c[None, :] % 8))).astype(BF), (1, 4)),
    )


def kernel_multilaunch(**ins):
    NCORES = 8
    cores = list(range(NCORES))
    C = _consts()
    cn = lambda a: np.ascontiguousarray(np.asarray(a), dtype=np.float32)

    xs = [np.ascontiguousarray(np.asarray(ins["L0_attn_x"][c]).T) for c in range(NCORES)]

    nc = build_ln0()
    res = run_bass_kernel_spmd(
        nc, [dict(x=xs[c], ONESC=C["ONESC"], ONESR=C["ONESR"]) for c in range(NCORES)], cores)
    xn = [np.asarray(res.results[c]["xn"]) for c in range(NCORES)]
    xn32 = [np.asarray(res.results[c]["xn32"]) for c in range(NCORES)]

    shared2 = dict(
        Wq=cn(ins["L0_attn_Wq"]), Wk=cn(ins["L0_attn_Wk"]), Wv=cn(ins["L0_attn_Wv"]),
        Wa=cn(ins["L0_attn_Wa"]), Wg=cn(ins["L0_attn_Wg"]), Wb=cn(ins["L0_attn_Wb"]),
        CQW=cn(np.asarray(ins["L0_attn_cqw"])[:, 0, :]),
        CKW=cn(np.asarray(ins["L0_attn_ckw"])[:, 0, :]),
        CVW=cn(np.asarray(ins["L0_attn_cvw"])[:, 0, :]),
        CQB=cn(np.asarray(ins["L0_attn_cqb"]).reshape(16, 128).T),
        CKB=cn(np.asarray(ins["L0_attn_ckb"]).reshape(16, 128).T),
        CVB=cn(np.asarray(ins["L0_attn_cvb"]).reshape(16, 128).T),
        BA=cn(np.asarray(ins["L0_attn_ba"]).reshape(16, 128).T),
        BB=cn(np.asarray(ins["L0_attn_bb"]).reshape(16, 1)),
        GAM=C["GAM"], IDENT=C["IDENT"],
    )
    nc = build_proj()
    res = run_bass_kernel_spmd(nc, [dict(xn=xn[c], **shared2) for c in range(NCORES)], cores)
    pj = [dict(res.results[c]) for c in range(NCORES)]

    shared3 = dict(
        MBD=C["MBD"], MSL=C["MSL"], MTL=C["MTL"],
        MBD4=C["MBD4"], MSL4=C["MSL4"], MTL4=C["MTL4"], IDENTB=C["IDENTB"], IDENT=C["IDENT"],
        ONESR=C["ONESR"],
        LNW=cn(np.asarray(ins["L0_attn_ln_w"]).reshape(128, 1)),
        LNB=cn(np.asarray(ins["L0_attn_ln_b"]).reshape(128, 1)),
    )
    nc = build_delta()
    res = run_bass_kernel_spmd(
        nc, [dict(q_s=pj[c]["q_s"], k_s=pj[c]["k_s"], g_s=pj[c]["g_s"], p1_s=pj[c]["p1_s"],
                  af_s=pj[c]["af_s"], ia_s=pj[c]["ia_s"], bta_s=pj[c]["bta_s"],
                  eom_s=pj[c]["eom_s"], elc_s=pj[c]["elc_s"], decrows_s=pj[c]["decrows_s"],
                  **shared3) for c in range(NCORES)], cores)
    of = [np.asarray(res.results[c]["of_s"]) for c in range(NCORES)]

    shared4 = dict(
        Wo=cn(ins["L0_attn_Wo"]), WG8=cn(ins["L0_ffn_wg"]),
        WET=cn(np.asarray(ins["L0_ffn_W"]).transpose(0, 2, 1)),
        BE=cn(ins["L0_ffn_b"]),
        IDENT=C["IDENT"], IDENTB=C["IDENTB"], ONESRB=C["ONESRB"],
    )
    nc = build_l0mix()
    res = run_bass_kernel_spmd(
        nc, [dict(x=xs[c], xn=xn[c], xn32=xn32[c], of_s=of[c], **shared4)
             for c in range(NCORES)], cores)
    x1 = [np.asarray(res.results[c]["x1_s"]) for c in range(NCORES)]

    shared5 = dict(
        WAT=cn(np.asarray(ins["L1_attn_Wa"]).T), WBT=cn(np.asarray(ins["L1_attn_Wb"]).T),
        BA1=cn(np.asarray(ins["L1_attn_ba"]).reshape(64, 1)),
        BB1=cn(np.asarray(ins["L1_attn_bb"]).reshape(64, 1)),
        WOT=cn(np.asarray(ins["L1_attn_Wout"]).T),
        GWT=cn(np.asarray(ins["L1_ffn_gate_w"]).T),
        WDT=cn(np.asarray(ins["L1_ffn_wd"]).transpose(0, 2, 1)),
        WUT=cn(np.asarray(ins["L1_ffn_wu"]).transpose(0, 2, 1)),
        BDC=cn(np.asarray(ins["L1_ffn_bd"]).reshape(8, 4, 128).transpose(2, 0, 1).reshape(128, 32)),
        BU=cn(ins["L1_ffn_bu"]),
        WSDT=cn(np.asarray(ins["L1_ffn_ws_d"]).T),
        BSDC=cn(np.asarray(ins["L1_ffn_bs_d"]).reshape(4, 128).T),
        WSUT=cn(np.asarray(ins["L1_ffn_ws_u"]).T),
        BSUC=cn(np.asarray(ins["L1_ffn_bs_u"]).reshape(16, 128).T),
        BOUTC=cn(np.asarray(ins["L1_attn_bout"]).reshape(16, 128).T),
        ONESC=C["ONESC"], ONESR=C["ONESR"], IDENTB=C["IDENTB"], ONESRB=C["ONESRB"],
    )
    nc = build_l1()
    res = run_bass_kernel_spmd(
        nc, [dict(x1_s=x1[c], **shared5) for c in range(NCORES)], cores)
    out = np.stack([np.asarray(res.results[c]["out_s"]).T for c in range(NCORES)])
    return np.ascontiguousarray(out)


def build_fused():
    ctx = Ctx(fused=True).begin()
    emit_ln0(ctx)
    ctx.phase()
    emit_proj(ctx)
    ctx.phase()
    emit_delta3(ctx)
    ctx.phase()
    emit_l0mix2(ctx)
    ctx.phase()
    emit_l12(ctx)
    ctx.finish()
    return ctx.nc


def _in_map(ins, C, xcore):
    cn = lambda a: np.ascontiguousarray(np.asarray(a), dtype=np.float32)
    return dict(
        x=xcore,
        ONESC=C["ONESC"], ONESR=C["ONESR"], NONESR=C["NONESR"], ONESRB=C["ONESRB"],
        IDENT=C["IDENT"], IDENTB=C["IDENTB"], GAM=C["GAM"],
        MBD4=C["MBD4"], MSL4=C["MSL4"], MTL4=C["MTL4"],
        Wq=cn(ins["L0_attn_Wq"]), Wk=cn(ins["L0_attn_Wk"]), Wv=cn(ins["L0_attn_Wv"]),
        Wa=cn(ins["L0_attn_Wa"]), Wg=cn(ins["L0_attn_Wg"]), Wb=cn(ins["L0_attn_Wb"]),
        CQW=cn(np.asarray(ins["L0_attn_cqw"])[:, 0, :]),
        CKW=cn(np.asarray(ins["L0_attn_ckw"])[:, 0, :]),
        CVW=cn(np.asarray(ins["L0_attn_cvw"])[:, 0, :]),
        CQB=cn(np.asarray(ins["L0_attn_cqb"]).reshape(16, 128).T),
        CKB=cn(np.asarray(ins["L0_attn_ckb"]).reshape(16, 128).T),
        CVB=cn(np.asarray(ins["L0_attn_cvb"]).reshape(16, 128).T),
        BA=cn(np.asarray(ins["L0_attn_ba"]).reshape(16, 128).T),
        BB=cn(np.asarray(ins["L0_attn_bb"]).reshape(16, 1)),
        LNW=cn(np.asarray(ins["L0_attn_ln_w"]).reshape(128, 1)),
        LNB=cn(np.asarray(ins["L0_attn_ln_b"]).reshape(128, 1)),
        Wo=cn(ins["L0_attn_Wo"]), WG8=cn(ins["L0_ffn_wg"]),
        WET=cn(np.asarray(ins["L0_ffn_W"]).transpose(0, 2, 1)),
        BE=cn(ins["L0_ffn_b"]),
        WAT=cn(np.asarray(ins["L1_attn_Wa"]).T), WBT=cn(np.asarray(ins["L1_attn_Wb"]).T),
        BA1=cn(np.asarray(ins["L1_attn_ba"]).reshape(64, 1)),
        BB1=cn(np.asarray(ins["L1_attn_bb"]).reshape(64, 1)),
        WOT=cn(np.asarray(ins["L1_attn_Wout"]).T),
        GWT=cn(np.asarray(ins["L1_ffn_gate_w"]).T),
        WDT=cn(np.asarray(ins["L1_ffn_wd"]).transpose(0, 2, 1)),
        WUT=cn(np.asarray(ins["L1_ffn_wu"]).transpose(0, 2, 1)),
        BDC=cn(np.asarray(ins["L1_ffn_bd"]).reshape(8, 4, 128).transpose(2, 0, 1).reshape(128, 32)),
        BU=cn(ins["L1_ffn_bu"]),
        WSDT=cn(np.asarray(ins["L1_ffn_ws_d"]).T),
        BSDC=cn(np.asarray(ins["L1_ffn_bs_d"]).reshape(4, 128).T),
        WSUT=cn(np.asarray(ins["L1_ffn_ws_u"]).T),
        BSUC=cn(np.asarray(ins["L1_ffn_bs_u"]).reshape(16, 128).T),
        BOUTC=cn(np.asarray(ins["L1_attn_bout"]).reshape(16, 128).T),
    )


def kernel(**ins):
    NCORES = 8
    C = _consts()
    xs = [np.ascontiguousarray(np.asarray(ins["L0_attn_x"][c]).T) for c in range(NCORES)]
    nc = build_fused()
    res = run_bass_kernel_spmd(nc, [_in_map(ins, C, xs[c]) for c in range(NCORES)],
                               core_ids=list(range(NCORES)))
    out = np.stack([np.asarray(res.results[c]["out_s"]).T for c in range(NCORES)])
    return np.ascontiguousarray(out)


def emit_l0mix2(ctx, pfx="m2"):
    nc, block = ctx.nc, ctx.block
    X = ctx.din("x", [D, T], F32)
    XN = ctx.dmid("xn", [D, T], BF16)
    XN32 = ctx.dmid("xn32", [D, T], F32)
    OF = ctx.dmid("of_s", [2048, T], BF16)
    Wo = ctx.din("Wo", [2048, 2048], F32)
    WG8 = ctx.din("WG8", [D, 8], F32)
    WET = ctx.din("WET", [8, D, 2048], F32)
    BE = ctx.din("BE", [8, 2048], F32)
    IDENTB = ctx.din("IDENTB", [128, 128], BF16)
    ONESRB = ctx.din("ONESRB", [1, 128], BF16)
    X1 = ctx.dmid("x1_s", [D, T], F32, producer=True)
    xe_s = nc.dram_tensor(pfx + "xe_s", [8, D, T], BF16, kind="Internal").ap()

    st = contextlib.ExitStack()
    sb = lambda name, shape, dt=F32: st.enter_context(nc.sbuf_tensor(pfx + name, shape, dt))
    ps = lambda name, shape, dt=F32: st.enter_context(nc.psum_tensor(pfx + name, shape, dt))
    big = sb("big", [128, 8 * NT * 512], BF16)
    xes = big
    oft = big
    wep = [sb(f"wep{i}", [128, 2 * NT * 128], BF16) for i in range(2)]
    wop = [sb(f"wop{i}", [128, NT * 128], BF16) for i in range(2)]
    gRow = sb("gRow", [1, 8 * T], BF16)
    beb = sb("beb", [8, 2048], BF16)
    gT8 = sb("gT8", [8, T], BF16)
    onrb = sb("onrb", [1, 128], BF16)
    idb = sb("idb", [128, 128], BF16)
    wgp = sb("wgp", [128, NT * 8])
    x32w = sb("x32w", [128, 4 * 128])
    lg = sb("lg", [128, 8]); ex = sb("ex", [128, 8]); mh = sb("mh", [128, 8])
    colsb = sb("colsb", [128, 8]); gmb = sb("gmb", [128, 8], BF16)
    gbce = sb("gbce", [128, 512])
    x1t = [sb(f"x1t{i}", [128, 512]) for i in range(2)]
    outt = [sb(f"outt{i}", [128, 512]) for i in range(2)]
    ps_out = [ps(f"ps_out{i}", [128, 512]) for i in range(2)]
    ps_g = ps("ps_g", [128, 512])
    ps_lg = [ps(f"ps_lg{i}", [128, 8]) for i in range(2)]
    ps_tb = ps("ps_tb", [128, 128], BF16)
    m = MSer(st, nc, ["ld", "pld", "mm", "ev"])

    with st:
        @block.sync
        def _(sync):
            m.dma("ld", sync, onrb[:], ONESRB)
            m.dma("ld", sync, idb[:], IDENTB)
            m.dma("ld", sync, wgp[:].rearrange("p (k e) -> p k e", k=NT),
                  WG8.rearrange("(k p) e -> p k e", p=128))

        @block.gpsimd
        def _(gp):
            m.dma("pld", gp, beb[:], BE)

        lg_free = {}
        for tt in range(NT):
            for half in range(4):
                @block.sync
                def _(sync, tt=tt, half=half):
                    m.dma("ld", sync, x32w[:].rearrange("p (kd t) -> p kd t", kd=4),
                          XN32.rearrange("(kd p) t -> p kd t", p=128)[:, half * 4:(half + 1) * 4,
                                                                     tt * 128:(tt + 1) * 128],
                          joins=[m.mark("mm")])
                ldm = m.mark("ld")

                @block.tensor
                def _(t, tt=tt, half=half, ldm=ldm):
                    for kd8 in range(4):
                        kd = half * 4 + kd8
                        joins = []
                        if kd8 == 0:
                            joins = [ldm]
                            if half == 0 and tt >= 2:
                                joins.append(lg_free[tt - 2])
                        m.op("mm", t, t.matmul, ps_lg[tt % 2][:], x32w[:, kd8 * 128:(kd8 + 1) * 128],
                             wgp[:, kd * 8:(kd + 1) * 8], start=(kd == 0), stop=(kd == NT - 1),
                             joins=joins)
            mmm = m.mark("mm")

            @block.vector
            def _(v, tt=tt, mmm=mmm):
                m.op("ev", v, v.tensor_copy, lg[:], ps_lg[tt % 2][:], joins=[mmm])
            lg_free[tt] = m.mark("ev")

            @block.vector
            def _(v):
                m.op("ev", v, v.tensor_reduce, out=colsb[:, 0:1], in_=lg[:], axis=mybir.AxisListType.X, op=Op.max)
                m.op("ev", v, v.tensor_scalar, out=colsb[:, 1:2], in0=colsb[:, 0:1],
                     scalar1=-1.0, scalar2=None, op0=Op.mult)

            @block.scalar
            def _(s):
                m.op("ev", s, s.activation, out=ex[:], in_=lg[:], func=AF.Exp, bias=colsb[:, 1:2])

            @block.vector
            def _(v):
                m.op("ev", v, v.tensor_reduce, out=colsb[:, 2:3], in_=ex[:], axis=mybir.AxisListType.X, op=Op.add)
                m.op("ev", v, v.reciprocal, out=colsb[:, 3:4], in_=colsb[:, 2:3])
                m.op("ev", v, v.tensor_tensor, out=mh[:], in0=lg[:],
                     in1=colsb[:, 0:1].broadcast_to([128, 8]), op=Op.is_equal)
                m.op("ev", v, v.tensor_copy, colsb[:, 4:5], mh[:, 0:1])
                for e in range(1, 8):
                    m.op("ev", v, v.tensor_scalar, out=colsb[:, 5:6], in0=colsb[:, 4:5],
                         scalar1=-1.0, scalar2=1.0, op0=Op.mult, op1=Op.add)
                    m.op("ev", v, v.tensor_tensor, out=mh[:, e:e + 1], in0=mh[:, e:e + 1],
                         in1=colsb[:, 5:6], op=Op.mult)
                    m.op("ev", v, v.tensor_tensor, out=colsb[:, 4:5], in0=colsb[:, 4:5],
                         in1=mh[:, e:e + 1], op=Op.max)
                m.op("ev", v, v.tensor_scalar, out=gmb[:], in0=mh[:], scalar1=colsb[:, 3:4],
                     scalar2=None, op0=Op.mult)

            for e in range(8):
                @block.tensor
                def _(t, e=e):
                    m.op("ev", t, t.transpose, ps_tb[0:1, :], gmb[:, e:e + 1], idb[:])

                @block.vector
                def _(v, e=e, tt=tt):
                    m.op("ev", v, v.tensor_copy, gRow[0:1, e * T + tt * 128:e * T + (tt + 1) * 128],
                         ps_tb[0:1, 0:128])

        @block.sync
        def _(sync):
            for e in range(8):
                m.dma("ld", sync, gT8[e:e + 1, :], gRow[0:1, e * T:(e + 1) * T],
                      joins=[m.mark("ev")])

        for tb in range(4):
            for half in range(2):
                @block.sync
                def _(sync, tb=tb, half=half):
                    m.dma("ld", sync, wep[half][:].rearrange("p (kd t) -> p kd t", kd=8),
                          XN.rearrange("(kd p) t -> p kd t", p=128)[:, half * 8:(half + 1) * 8,
                                                                   tb * 512:(tb + 1) * 512],
                          joins=[m.mark("ev")])
            ld_xn = m.mark("ld")

            for e in range(8):
                @block.tensor
                def _(t, e=e, tb=tb):
                    m.op("mm", t, t.matmul, ps_g[:], onrb[:],
                         gRow[0:1, e * T + tb * 512:e * T + (tb + 1) * 512],
                         start=True, stop=True, joins=[m.mark("ev")])
                gm = m.mark("mm")

                @block.vector
                def _(v, gm=gm, ld_xn=ld_xn):
                    m.op("ev", v, v.tensor_copy, gbce[:], ps_g[:], joins=[gm, ld_xn])

                @block.vector
                def _(v, e=e):
                    for kd in range(NT):
                        src = wep[kd // 8]
                        m.op("ev", v, v.tensor_tensor,
                             out=xes[:, (e * NT + kd) * 512:(e * NT + kd + 1) * 512],
                             in0=src[:, (kd % 8) * 512:((kd % 8) + 1) * 512], in1=gbce[:], op=Op.mult)

            @block.sync
            def _(sync, tb=tb):
                for e in range(8):
                    m.dma("ld", sync, xe_s[e].rearrange("(kd p) t -> p kd t", p=128)[:, :, tb * 512:(tb + 1) * 512],
                          xes[:, e * NT * 512:(e + 1) * NT * 512].rearrange("p (kd t) -> p kd t", kd=NT),
                          joins=[m.mark("ev")])

        ev_done = {}
        xe_done = m.mark("ev")
        pld_free = {0: xe_done, 1: xe_done}
        for tb in range(4):
            ld_xe = {}

            @block.sync
            def _(sync, tb=tb):
                for e in range(8):
                    m.dma("ld", sync, xes[:, e * NT * 512:(e + 1) * NT * 512].rearrange("p (kd t) -> p kd t", kd=NT),
                          xe_s[e].rearrange("(kd p) t -> p kd t", p=128)[:, :, tb * 512:(tb + 1) * 512],
                          joins=[m.mark("mm")] if e == 0 else ())
                    ld_xe[e] = m.mark("ld")

            for k in range(NT):
                for pj in range(4):
                    @block.gpsimd
                    def _(gp, k=k, pj=pj, tb=tb):
                        for sub in range(2):
                            e = pj * 2 + sub
                            m.dma("pld", gp,
                                  wep[pj % 2][:, sub * NT * 128:(sub + 1) * NT * 128].rearrange("p (kd c) -> p kd c", kd=NT),
                                  WET[e].rearrange("(kd p) c -> p kd c", p=128)[:, :, k * 128:(k + 1) * 128],
                                  joins=[pld_free[pj % 2]] if sub == 0 else ())
                    pldm = m.mark("pld")

                    @block.tensor
                    def _(t, k=k, pj=pj, tb=tb, pldm=pldm):
                        for sub in range(2):
                            e = pj * 2 + sub
                            for kd in range(NT):
                                joins = []
                                if sub == 0 and kd == 0:
                                    joins = [pldm, ld_xe[pj * 2 + 1]]
                                    if pj == 0:
                                        if k == 0 and tb > 0:
                                            joins.append(ev_done[tb - 1])
                                        if k >= 2:
                                            joins.append(ev_done[(tb, k - 2)])
                                m.op("mm", t, t.matmul, ps_out[k % 2][:],
                                     wep[pj % 2][:, (sub * NT + kd) * 128:(sub * NT + kd + 1) * 128],
                                     xes[:, (e * NT + kd) * 512:(e * NT + kd + 1) * 512],
                                     start=(pj == 0 and sub == 0 and kd == 0), stop=False,
                                     joins=joins)
                    pld_free[pj % 2] = m.mark("mm")

                @block.tensor
                def _(t, k=k, tb=tb):
                    m.op("mm", t, t.matmul, ps_out[k % 2][:], beb[:, k * 128:(k + 1) * 128],
                         gT8[:, tb * 512:(tb + 1) * 512], start=False, stop=True)
                mmk = m.mark("mm")

                @block.vector
                def _(v, k=k, mmk=mmk):
                    m.op("ev", v, v.tensor_copy, outt[k % 2][:], ps_out[k % 2][:], joins=[mmk])
                ev_done[(tb, k)] = m.mark("ev")

                @block.sync
                def _(sync, k=k, tb=tb):
                    m.dma("ld", sync, X1[k * 128:(k + 1) * 128, tb * 512:(tb + 1) * 512],
                          outt[k % 2][:], joins=[ev_done[(tb, k)]])
            ev_done[tb] = m.mark("ev")

        @block.sync
        def _(sync):
            for hv in range(NT):
                m.dma("ld", sync, oft[:, hv * T:(hv + 1) * T], OF[hv * 128:(hv + 1) * 128, :],
                      joins=[m.mark("mm")] if hv == 0 else ())
        of_ld = m.mark("ld")

        wo_free = {0: ("none", 0), 1: ("none", 0)}
        for k in range(NT):
            @block.gpsimd
            def _(gp, k=k):
                m.dma("pld", gp, wop[k % 2][:].rearrange("p (hv c) -> p hv c", hv=NT),
                      Wo.rearrange("(hv p) c -> p hv c", p=128)[:, :, k * 128:(k + 1) * 128],
                      joins=[wo_free[k % 2]])
            wldm = m.mark("pld")

            for tb in range(4):
                @block.sync
                def _(sync, k=k, tb=tb):
                    m.dma("ld", sync, outt[tb % 2][:], X[k * 128:(k + 1) * 128, tb * 512:(tb + 1) * 512],
                          joins=[m.mark("ev")])
                    m.dma("ld", sync, x1t[tb % 2][:], X1[k * 128:(k + 1) * 128, tb * 512:(tb + 1) * 512])
                xldm = m.mark("ld")

                @block.tensor
                def _(t, k=k, tb=tb, wldm=wldm):
                    for hv in range(NT):
                        joins = []
                        if hv == 0:
                            joins = [wldm, of_ld]
                            if (k, tb) != (0, 0):
                                joins.append(m.mark("ev"))
                        m.op("mm", t, t.matmul, ps_out[tb % 2][:], wop[k % 2][:, hv * 128:(hv + 1) * 128],
                             oft[:, hv * T + tb * 512: hv * T + (tb + 1) * 512],
                             start=(hv == 0), stop=(hv == NT - 1), joins=joins)
                mmk = m.mark("mm")

                @block.vector
                def _(v, k=k, tb=tb, mmk=mmk, xldm=xldm):
                    m.op("ev", v, v.tensor_tensor, out=outt[tb % 2][:], in0=outt[tb % 2][:],
                         in1=ps_out[tb % 2][:], op=Op.add, joins=[mmk, xldm])
                    m.op("ev", v, v.tensor_tensor, out=outt[tb % 2][:], in0=outt[tb % 2][:],
                         in1=x1t[tb % 2][:], op=Op.add)

                @block.sync
                def _(sync, k=k, tb=tb):
                    m.dma("ld", sync, X1[k * 128:(k + 1) * 128, tb * 512:(tb + 1) * 512],
                          outt[tb % 2][:], joins=[m.mark("ev")])
            wo_free[k % 2] = m.mark("mm")

        for name in ("sync", "tensor", "vector", "scalar", "gpsimd"):
            @getattr(block, name)
            def _(eng):
                for sn in ("ld", "pld", "mm", "ev"):
                    sm = m.streams[sn]
                    eng.wait_ge(sm.sem, sm.count)


def build_l0mix2():
    ctx = Ctx(fused=False).begin()
    emit_l0mix2(ctx)
    ctx.finish()
    return ctx.nc


def emit_l12(ctx, pfx="l2", paranoid=False, par=()):
    nc, block = ctx.nc, ctx.block
    X1 = ctx.dmid("x1_s", [D, T], F32)
    WAT = ctx.din("WAT", [D, 64], F32)
    WBT = ctx.din("WBT", [D, 64], F32)
    BA1 = ctx.din("BA1", [64, 1], F32)
    BB1 = ctx.din("BB1", [64, 1], F32)
    WOT = ctx.din("WOT", [64, D], F32)
    GWT = ctx.din("GWT", [D, 8], F32)
    WDT = ctx.din("WDT", [8, D, 512], F32)
    WUT = ctx.din("WUT", [8, 512, D], F32)
    BDC = ctx.din("BDC", [128, 8 * 4], F32)
    BU = ctx.din("BU", [8, D], F32)
    WSDT = ctx.din("WSDT", [D, 512], F32)
    BSDC = ctx.din("BSDC", [128, 4], F32)
    WSUT = ctx.din("WSUT", [512, D], F32)
    BSUC = ctx.din("BSUC", [128, 16], F32)
    BOUTC = ctx.din("BOUTC", [128, 16], F32)
    ONESC = ctx.din("ONESC", [128, 1], F32)
    ONESR = ctx.din("ONESR", [1, 128], F32)
    IDENTB = ctx.din("IDENTB", [128, 128], BF16)
    ONESRB = ctx.din("ONESRB", [1, 128], BF16)
    OUT = ctx.dout("out_s", [D, T], F32)
    X132 = ctx.dmid("xn132_i", [D, T], F32) if ctx.fused else nc.dram_tensor("xn132_i", [D, T], F32, kind="Internal").ap()

    st = contextlib.ExitStack()
    sb = lambda name, shape, dt=F32: st.enter_context(nc.sbuf_tensor(pfx + name, shape, dt))
    ps = lambda name, shape, dt=F32: st.enter_context(nc.psum_tensor(pfx + name, shape, dt))
    xnt1 = sb("xnt1", [128, NT * T], BF16)
    tmp = sb("tmp", [128, 512])
    epst = sb("epst", [128, 1])
    onc = sb("onc", [128, 1]); onr = sb("onr", [1, 128]); onrb = sb("onrb", [1, 128], BF16)
    idb = sb("idb", [128, 128], BF16)
    ba1 = sb("ba1", [64, 1]); bb1 = sb("bb1", [64, 1])
    hsb = sb("hsb", [64, T], BF16)
    lg = sb("lg", [128, 8]); ex = sb("ex", [128, 8]); mh = sb("mh", [128, 8])
    mh2 = sb("mh2", [128, 8]); pp = sb("pp", [128, 8]); colsb = sb("colsb", [128, 10])
    cmb = sb("cmb", [128, 8], BF16)
    cRow = sb("cRow", [1, 8 * T], BF16)
    cT8 = sb("cT8", [8, T], BF16)
    bdc = sb("bdc", [128, 8 * 4]); bsdc = sb("bsdc", [128, 4])
    bub = sb("bub", [8, D], BF16)
    bcol = sb("bcol", [128, 16])
    outt = [sb(f"outt{i}", [128, 512]) for i in range(2)]
    x1t = [sb(f"x1t{i}", [128, 512]) for i in range(2)]
    gsb = [sb(f"gsb{i}", [128, 512]) for i in range(2)]
    psA = ps("psA", [128, 512]); psB = ps("psB", [128, 512])
    psC = ps("psC", [128, 512]); psD = ps("psD", [128, 512])
    psE = ps("psE", [128, 512]); psF = ps("psF", [128, 512])
    ps_tb = ps("ps_tb", [128, 128], BF16)
    m = MSer(st, nc, ["ld", "pld", "mm", "ev"])
    m.paranoid = paranoid
    stA = contextlib.ExitStack()
    sba = lambda name, shape, dt=F32: stA.enter_context(nc.sbuf_tensor(pfx + name, shape, dt))
    x1w = sba("x1w", [128, NT * 512])
    rows = sba("rows", [1, 6 * 512])
    watp = sba("watp", [128, NT * 64], BF16); wbtp = sba("wbtp", [128, NT * 64], BF16)
    aa = sba("aa", [64, T]); ab = sba("ab", [64, T])
    gwtp = sba("gwtp", [128, NT * 8]); x32w = sba("x32w", [128, 4 * 128])

    with st:
        @block.sync
        def _(sync):
            m.dma("ld", sync, onc[:], ONESC); m.dma("ld", sync, onr[:], ONESR)
            m.dma("ld", sync, onrb[:], ONESRB); m.dma("ld", sync, idb[:], IDENTB)
            m.dma("ld", sync, ba1[:], BA1); m.dma("ld", sync, bb1[:], BB1)
            m.dma("ld", sync, bdc[:], BDC); m.dma("ld", sync, bsdc[:], BSDC)
            m.dma("ld", sync, gwtp[:].rearrange("p (k e) -> p k e", k=NT),
                  GWT.rearrange("(k p) e -> p k e", p=128))
            m.dma("ld", sync, tmp[:, 0:16], BSUC[:, 0:16])
            m.dma("ld", sync, x1w[:, 0:16], BOUTC[:, 0:16])

        @block.gpsimd
        def _(gp):
            m.dma("pld", gp, watp[:].rearrange("p (k c) -> p k c", k=NT),
                  WAT.rearrange("(k p) c -> p k c", p=128))
            m.dma("pld", gp, wbtp[:].rearrange("p (k c) -> p k c", k=NT),
                  WBT.rearrange("(k p) c -> p k c", p=128))
            m.dma("pld", gp, bub[:], BU)

        @block.vector
        def _(v):
            m.op("ev", v, v.memset, epst[:], EPS, joins=[m.mark("ld")])
            m.op("ev", v, v.tensor_tensor, out=bcol[:], in0=tmp[:, 0:16], in1=x1w[:, 0:16], op=Op.add)

        m.paranoid = paranoid or ("ln" in par)
        apply_done = ("none", 0)
        for cb in range(4):
            csl = slice(cb * 512, (cb + 1) * 512)

            @block.sync
            def _(sync, cb=cb):
                m.dma("ld", sync, x1w[:].rearrange("p (kd t) -> p kd t", kd=NT),
                      X1.rearrange("(kd p) t -> p kd t", p=128)[:, :, csl],
                      joins=[apply_done, m.mark("ev")])
            ldm = m.mark("ld")

            for k in range(NT):
                @block.tensor
                def _(t, k=k, ldm=ldm):
                    m.op("mm", t, t.matmul, psA[0:1, :], onc[:], x1w[:, k * 512:(k + 1) * 512],
                         start=(k == 0), stop=(k == NT - 1), joins=[ldm] if k == 0 else ())

            @block.scalar
            def _(s, ldm=ldm):
                for k in range(NT):
                    joins = [ldm] if k < 2 else [m.mark("mm")]
                    m.op("ev", s, s.activation, out=gsb[k % 2][:], in_=x1w[:, k * 512:(k + 1) * 512],
                         func=AF.Square, joins=joins)
                    mq = m.mark("ev")

                    @block.tensor
                    def _(t, k=k, mq=mq):
                        m.op("mm", t, t.matmul, psB[0:1, :], onc[:], gsb[k % 2][:],
                             start=(k == 0), stop=(k == NT - 1), joins=[mq])

            mmst = m.mark("mm")

            @block.vector
            def _(v, mmst=mmst):
                mr, msq, m2, var, rstd, mneg = (rows[0:1, i * 512:(i + 1) * 512] for i in range(6))
                m.op("ev", v, v.tensor_scalar, out=mr, in0=psA[0:1, :], scalar1=1.0 / D,
                     scalar2=None, op0=Op.mult, joins=[mmst])
                m.op("ev", v, v.tensor_scalar, out=msq, in0=psB[0:1, :], scalar1=1.0 / D,
                     scalar2=None, op0=Op.mult)
                m.op("ev", v, v.tensor_tensor, out=m2, in0=mr, in1=mr, op=Op.mult)
                m.op("ev", v, v.tensor_tensor, out=var, in0=msq, in1=m2, op=Op.subtract)

            @block.scalar
            def _(s):
                var = rows[0:1, 3 * 512:4 * 512]
                m.op("ev", s, s.activation, out=var, in_=var, func=AF.Sqrt, bias=epst[0:1, :])

            @block.vector
            def _(v):
                mr, msq, m2, var, rstd, mneg = (rows[0:1, i * 512:(i + 1) * 512] for i in range(6))
                m.op("ev", v, v.reciprocal, out=rstd, in_=var)
                m.op("ev", v, v.tensor_tensor, out=mneg, in0=mr, in1=rstd, op=Op.mult)
                m.op("ev", v, v.tensor_scalar, out=mneg, in0=mneg, scalar1=-1.0, scalar2=None, op0=Op.mult)
            rowm = m.mark("ev")

            @block.tensor
            def _(t, rowm=rowm):
                m.op("mm", t, t.matmul, psC[:], onr[:], rows[0:1, 4 * 512:5 * 512],
                     start=True, stop=True, joins=[rowm])
                m.op("mm", t, t.matmul, psD[:], onr[:], rows[0:1, 5 * 512:6 * 512], start=True, stop=True)
            bcm = m.mark("mm")

            for k in range(NT):
                @block.vector
                def _(v, k=k, cb=cb, bcm=bcm):
                    buf = gsb[k % 2]
                    m.op("ev", v, v.tensor_tensor, out=buf[:], in0=x1w[:, k * 512:(k + 1) * 512],
                         in1=psC[:], op=Op.mult, joins=[bcm, m.mark("ld")] if k == 0 else [m.mark("ld")])
                    m.op("ev", v, v.tensor_tensor, out=buf[:], in0=buf[:], in1=psD[:], op=Op.add)
                    m.op("ev", v, v.tensor_copy, xnt1[:, k * T + cb * 512:k * T + (cb + 1) * 512], buf[:])

                @block.sync
                def _(sync, k=k, cb=cb):
                    m.dma("ld", sync, X132[k * 128:(k + 1) * 128, csl], gsb[k % 2][:], joins=[m.mark("ev")])
            apply_done = m.mark("ev")

        m.paranoid = paranoid or ("attn" in par)
        for which, wp, bias, dst in ((0, watp, ba1, aa), (1, wbtp, bb1, ab)):
            for tb in range(4):
                @block.tensor
                def _(t, wp=wp, tb=tb, which=which):
                    for kd in range(NT):
                        m.op("mm", t, t.matmul, psE[0:64, :], wp[:, kd * 64:(kd + 1) * 64],
                             xnt1[:, kd * T + tb * 512:kd * T + (tb + 1) * 512],
                             start=(kd == 0), stop=(kd == NT - 1),
                             joins=[apply_done, m.mark("pld"), m.mark("ev")] if (tb == 0 and kd == 0 and which == 0)
                             else ([m.mark("ev")] if kd == 0 else ()))
                pm = m.mark("mm")

                @block.scalar
                def _(s, bias=bias, dst=dst, tb=tb, pm=pm):
                    m.op("ev", s, s.activation, out=dst[:, tb * 512:(tb + 1) * 512],
                         in_=psE[0:64, :], func=AF.Tanh, bias=bias[:], joins=[pm])

        @block.vector
        def _(v):
            m.op("ev", v, v.tensor_tensor_scan, out=hsb[:], data0=aa[:], data1=ab[:],
                 initial=0.0, op0=Op.mult, op1=Op.add)

        m.paranoid = paranoid or ("rt" in par)
        lgf = {}
        for tt in range(NT):
            for qt in range(4):
                @block.sync
                def _(sync, tt=tt, qt=qt):
                    m.dma("ld", sync, x32w[:].rearrange("p (kd t) -> p kd t", kd=4),
                          X132.rearrange("(kd p) t -> p kd t", p=128)[:, qt * 4:(qt + 1) * 4,
                                                                     tt * 128:(tt + 1) * 128],
                          joins=[m.mark("mm")])
                ldm = m.mark("ld")

                @block.tensor
                def _(t, tt=tt, qt=qt, ldm=ldm):
                    for k4 in range(4):
                        kd = qt * 4 + k4
                        joins = []
                        if k4 == 0:
                            joins = [ldm]
                            if qt == 0 and tt >= 2:
                                joins.append(lgf[tt - 2])
                        m.op("mm", t, t.matmul, (psA if tt % 2 == 0 else psB)[:, 0:8],
                             x32w[:, k4 * 128:(k4 + 1) * 128], gwtp[:, kd * 8:(kd + 1) * 8],
                             start=(kd == 0), stop=(kd == NT - 1), joins=joins)
            mmm = m.mark("mm")

            @block.vector
            def _(v, tt=tt, mmm=mmm):
                m.op("ev", v, v.tensor_copy, lg[:], (psA if tt % 2 == 0 else psB)[:, 0:8], joins=[mmm])
            lgf[tt] = m.mark("ev")

            @block.vector
            def _(v):
                m.op("ev", v, v.tensor_reduce, out=colsb[:, 0:1], in_=lg[:], axis=mybir.AxisListType.X, op=Op.max)
                m.op("ev", v, v.tensor_scalar, out=colsb[:, 1:2], in0=colsb[:, 0:1],
                     scalar1=-1.0, scalar2=None, op0=Op.mult)

            @block.scalar
            def _(s):
                m.op("ev", s, s.activation, out=ex[:], in_=lg[:], func=AF.Exp, bias=colsb[:, 1:2])

            @block.vector
            def _(v):
                m.op("ev", v, v.tensor_reduce, out=colsb[:, 2:3], in_=ex[:], axis=mybir.AxisListType.X, op=Op.add)
                m.op("ev", v, v.reciprocal, out=colsb[:, 3:4], in_=colsb[:, 2:3])
                m.op("ev", v, v.tensor_scalar, out=pp[:], in0=ex[:], scalar1=colsb[:, 3:4],
                     scalar2=None, op0=Op.mult)
                m.op("ev", v, v.tensor_reduce, out=colsb[:, 4:5], in_=pp[:], axis=mybir.AxisListType.X, op=Op.max)
                m.op("ev", v, v.tensor_tensor, out=mh[:], in0=pp[:],
                     in1=colsb[:, 4:5].broadcast_to([128, 8]), op=Op.is_equal)
                m.op("ev", v, v.tensor_copy, colsb[:, 6:7], mh[:, 0:1])
                for e in range(1, 8):
                    m.op("ev", v, v.tensor_scalar, out=colsb[:, 7:8], in0=colsb[:, 6:7],
                         scalar1=-1.0, scalar2=1.0, op0=Op.mult, op1=Op.add)
                    m.op("ev", v, v.tensor_tensor, out=mh[:, e:e + 1], in0=mh[:, e:e + 1],
                         in1=colsb[:, 7:8], op=Op.mult)
                    m.op("ev", v, v.tensor_tensor, out=colsb[:, 6:7], in0=colsb[:, 6:7],
                         in1=mh[:, e:e + 1], op=Op.max)
                m.op("ev", v, v.tensor_scalar, out=mh2[:], in0=mh[:], scalar1=-1.0, scalar2=1.0,
                     op0=Op.mult, op1=Op.add)
                m.op("ev", v, v.tensor_tensor, out=mh2[:], in0=pp[:], in1=mh2[:], op=Op.mult)
                m.op("ev", v, v.tensor_reduce, out=colsb[:, 5:6], in_=mh2[:], axis=mybir.AxisListType.X, op=Op.max)
                m.op("ev", v, v.tensor_tensor, out=mh2[:], in0=mh2[:],
                     in1=colsb[:, 5:6].broadcast_to([128, 8]), op=Op.is_equal)
                m.op("ev", v, v.tensor_copy, colsb[:, 8:9], mh2[:, 0:1])
                for e in range(1, 8):
                    m.op("ev", v, v.tensor_scalar, out=colsb[:, 9:10], in0=colsb[:, 8:9],
                         scalar1=-1.0, scalar2=1.0, op0=Op.mult, op1=Op.add)
                    m.op("ev", v, v.tensor_tensor, out=mh2[:, e:e + 1], in0=mh2[:, e:e + 1],
                         in1=colsb[:, 9:10], op=Op.mult)
                    m.op("ev", v, v.tensor_tensor, out=colsb[:, 8:9], in0=colsb[:, 8:9],
                         in1=mh2[:, e:e + 1], op=Op.max)
                m.op("ev", v, v.tensor_tensor, out=colsb[:, 6:7], in0=colsb[:, 4:5],
                     in1=colsb[:, 5:6], op=Op.add)
                m.op("ev", v, v.reciprocal, out=colsb[:, 7:8], in_=colsb[:, 6:7])
                m.op("ev", v, v.tensor_tensor, out=colsb[:, 8:9], in0=colsb[:, 4:5],
                     in1=colsb[:, 7:8], op=Op.mult)
                m.op("ev", v, v.tensor_tensor, out=colsb[:, 9:10], in0=colsb[:, 5:6],
                     in1=colsb[:, 7:8], op=Op.mult)
                m.op("ev", v, v.tensor_scalar, out=mh[:], in0=mh[:], scalar1=colsb[:, 8:9],
                     scalar2=None, op0=Op.mult)
                m.op("ev", v, v.scalar_tensor_tensor, out=pp[:], in0=mh2[:], scalar=colsb[:, 9:10],
                     in1=mh[:], op0=Op.mult, op1=Op.add)
                m.op("ev", v, v.tensor_copy, cmb[:], pp[:])

            for e in range(8):
                @block.tensor
                def _(t, e=e):
                    m.op("ev", t, t.transpose, ps_tb[0:1, :], cmb[:, e:e + 1], idb[:])

                @block.vector
                def _(v, e=e, tt=tt):
                    m.op("ev", v, v.tensor_copy, cRow[0:1, e * T + tt * 128:e * T + (tt + 1) * 128],
                         ps_tb[0:1, 0:128])

        @block.sync
        def _(sync):
            for e in range(8):
                m.dma("ld", sync, cT8[e:e + 1, :], cRow[0:1, e * T:(e + 1) * T],
                      joins=[m.mark("ev")])

        m.paranoid = paranoid or ("ffn" in par)
        m.noskip = "noskip" in par
        m.peonly = "peonly" in par
        stA.close()
        wdtp = [sb(f"wdtp{i}", [128, NT * 128], BF16) for i in range(2)]
        wutp = [sb(f"wutp{i}", [128, 32 * 128], BF16) for i in range(2)]
        wsutp = sb("wsutp", [128, 4 * 128], BF16)
        wotp = sb("wotp", [64, 128], BF16)
        he = sb("he", [128, 8 * 4 * 512], BF16)
        hsh = sb("hsh", [128, 4 * 512], BF16)

        dn_free = {0: m.mark("ev"), 1: m.mark("ev")}
        ev_all = m.mark("ev")
        pld_free = {0: ev_all, 1: ev_all}
        up_evac = {}
        up_free = {0: ev_all, 1: ev_all}
        for tb in range(4):
            tsl = slice(tb * 512, (tb + 1) * 512)
            ups_prev = m.mark("mm")
            for e in range(9):
                for bt in range(4):
                    pi = (e * 4 + bt) % 2

                    @block.gpsimd
                    def _(gp, e=e, bt=bt, pi=pi):
                        src = (WSDT if e == 8 else WDT[e])
                        m.dma("pld", gp, wdtp[pi][:].rearrange("p (kd c) -> p kd c", kd=NT),
                              src.rearrange("(kd p) c -> p kd c", p=128)[:, :, bt * 128:(bt + 1) * 128],
                              joins=[pld_free[pi]])
                    pldm = m.mark("pld")

                    @block.tensor
                    def _(t, e=e, bt=bt, tb=tb, pi=pi, pldm=pldm):
                        bank = psA if pi == 0 else psB
                        for kd in range(NT):
                            joins = []
                            if kd == 0:
                                joins = [pldm, dn_free[pi]]
                                if tb == 0 and e == 0 and bt == 0:
                                    joins.append(m.mark("ld"))
                            m.op("mm", t, t.matmul, bank[:], wdtp[pi][:, kd * 128:(kd + 1) * 128],
                                 xnt1[:, kd * T + tb * 512:kd * T + (tb + 1) * 512],
                                 start=(kd == 0), stop=(kd == NT - 1), joins=joins)
                    pld_free[pi] = m.mark("mm")
                    dmm = m.mark("mm")

                    if e == 8:
                        @block.scalar
                        def _(s, bt=bt, pi=pi, dmm=dmm):
                            bank = psA if pi == 0 else psB
                            m.op("ev", s, s.activation, out=hsh[:, bt * 512:(bt + 1) * 512],
                                 in_=bank[:], func=AF.Gelu, bias=bsdc[:, bt:bt + 1], joins=[dmm])
                        dn_free[pi] = m.mark("ev")
                    else:
                        @block.scalar
                        def _(s, e=e, bt=bt, pi=pi, dmm=dmm):
                            bank = psA if pi == 0 else psB
                            m.op("ev", s, s.activation, out=gsb[pi][:], in_=bank[:],
                                 func=AF.Gelu, bias=bdc[:, e * 4 + bt:e * 4 + bt + 1], joins=[dmm])
                        dn_free[pi] = m.mark("ev")

                        @block.tensor
                        def _(t, e=e, tb=tb):
                            m.op("mm", t, t.matmul, psC[:], onrb[:],
                                 cRow[0:1, e * T + tb * 512:e * T + (tb + 1) * 512],
                                 start=True, stop=True, joins=[m.mark("ev")])
                        gbm = m.mark("mm")

                        @block.vector
                        def _(v, e=e, bt=bt, pi=pi, gbm=gbm, tb=tb):
                            joins = [gbm] + ([ups_prev] if (e == 0 and bt == 0 and tb > 0) else [])
                            m.op("ev", v, v.tensor_tensor,
                                 out=he[:, (e * 4 + bt) * 512:(e * 4 + bt + 1) * 512],
                                 in0=gsb[pi][:], in1=psC[:], op=Op.mult, joins=joins)
            heready = m.mark("ev")
            for k in range(NT):
                kp = k % 2

                @block.gpsimd
                def _(gp, k=k, kp=kp):
                    for e in range(8):
                        m.dma("pld", gp,
                              wutp[kp][:, e * 4 * 128:(e + 1) * 4 * 128].rearrange("p (bt c) -> p bt c", bt=4),
                              WUT[e].rearrange("(bt p) c -> p bt c", p=128)[:, :, k * 128:(k + 1) * 128],
                              joins=[up_free[kp]] if e == 0 else ())
                    m.dma("pld", gp, wsutp[:].rearrange("p (bt c) -> p bt c", bt=4),
                          WSUT.rearrange("(bt p) c -> p bt c", p=128)[:, :, k * 128:(k + 1) * 128])
                    m.dma("pld", gp, wotp[:], WOT[:, k * 128:(k + 1) * 128])
                updm = m.mark("pld")

                @block.sync
                def _(sync, k=k, tb=tb, kp=kp):
                    m.dma("ld", sync, x1t[kp][:], X1[k * 128:(k + 1) * 128, tsl],
                          joins=[m.mark("ev")])
                xldm = m.mark("ld")

                @block.tensor
                def _(t, k=k, tb=tb, kp=kp, updm=updm):
                    bank = psD if kp == 0 else psE
                    first = True
                    for e in range(8):
                        for bt in range(4):
                            joins = []
                            if first:
                                joins = [updm, heready]
                                if (tb, k - 2) in up_evac:
                                    joins.append(up_evac[(tb, k - 2)])
                                elif tb > 0:
                                    joins.append(up_evac[(tb - 1, NT - 2 + k)])
                                first = False
                            m.op("mm", t, t.matmul, bank[:],
                                 wutp[kp][:, (e * 4 + bt) * 128:(e * 4 + bt + 1) * 128],
                                 he[:, (e * 4 + bt) * 512:(e * 4 + bt + 1) * 512],
                                 start=(e == 0 and bt == 0), stop=False, joins=joins)
                    for bt in range(4):
                        m.op("mm", t, t.matmul, bank[:], wsutp[:, bt * 128:(bt + 1) * 128],
                             hsh[:, bt * 512:(bt + 1) * 512], start=False, stop=False)
                    m.op("mm", t, t.matmul, bank[:], bub[:, k * 128:(k + 1) * 128], cT8[:, tsl],
                         start=False, stop=False)
                    m.op("mm", t, t.matmul, bank[:], wotp[:], hsb[:, tsl], start=False, stop=True)
                up_free[kp] = m.mark("mm")
                umm = m.mark("mm")

                @block.vector
                def _(v, k=k, tb=tb, kp=kp, umm=umm, xldm=xldm):
                    bank = psD if kp == 0 else psE
                    m.op("ev", v, v.tensor_tensor, out=outt[kp][:], in0=x1t[kp][:],
                         in1=bank[:], op=Op.add, joins=[umm, xldm])
                    m.op("ev", v, v.tensor_tensor, out=outt[kp][:], in0=outt[kp][:],
                         in1=xnt1[:, k * T + tb * 512:k * T + (tb + 1) * 512], op=Op.add)
                    m.op("ev", v, v.tensor_scalar, out=outt[kp][:], in0=outt[kp][:],
                         scalar1=bcol[:, k:k + 1], scalar2=None, op0=Op.add)
                up_evac[(tb, k)] = m.mark("ev")

                @block.sync
                def _(sync, k=k, tb=tb, kp=kp):
                    m.dma("ld", sync, OUT[k * 128:(k + 1) * 128, tsl], outt[kp][:],
                          joins=[up_evac[(tb, k)]])

        for name in ("sync", "tensor", "vector", "scalar", "gpsimd"):
            @getattr(block, name)
            def _(eng):
                for sn in ("ld", "pld", "mm", "ev"):
                    sm = m.streams[sn]
                    eng.wait_ge(sm.sem, sm.count)


def build_l12(paranoid=False, par=()):
    ctx = Ctx(fused=False).begin()
    emit_l12(ctx, paranoid=paranoid, par=par)
    ctx.finish()
    return ctx.nc


def emit_delta3(ctx, pfx="d3"):
    nc, block = ctx.nc, ctx.block
    q_s = ctx.dmid("q_s", [2048, T], BF16)
    k_s = ctx.dmid("k_s", [2048, T], BF16)
    g_s = ctx.dmid("g_s", [2048, T], BF16)
    p1_s = ctx.dmid("p1_s", [2048, T], F32)
    af_s = ctx.dmid("af_s", [2048, T], F32)
    ia_s = ctx.dmid("ia_s", [2048, T], F32)
    bta_s = ctx.dmid("bta_s", [16, T], F32)
    eom_s = ctx.dmid("eom_s", [128, NCH * NH], F32)
    decrows_s = ctx.dmid("decrows_s", [NCH, 2048], F32)
    elcrows_s = ctx.dmid("elcrows_s", [NCH, 2048], F32)
    MBD4 = ctx.din("MBD4", [128, 512], BF16)
    MSL4 = ctx.din("MSL4", [128, 512], BF16)
    MTL4 = ctx.din("MTL4", [128, 512], BF16)
    IDENTB = ctx.din("IDENTB", [128, 128], BF16)
    IDENT = ctx.din("IDENT", [128, 128], F32)
    NONESR = ctx.din("NONESR", [1, 128], F32)
    ONESRB = ctx.din("ONESRB", [1, 128], BF16)
    LNW = ctx.din("LNW", [128, 1], F32)
    LNB = ctx.din("LNB", [128, 1], F32)
    of_s = ctx.dmid("of_s", [2048, T], BF16, producer=True)
    y_scr = nc.dram_tensor(pfx + "y_scr", [T, 2048], BF16, kind="Internal").ap()

    st = contextlib.ExitStack()
    sb = lambda name, shape, dt=F32: st.enter_context(nc.sbuf_tensor(pfx + name, shape, dt))
    ps = lambda name, shape, dt=F32: st.enter_context(nc.psum_tensor(pfx + name, shape, dt))
    Kc2 = [sb(f"Kc{p}", [128, 1024], BF16) for p in range(2)]
    Qc2 = [sb(f"Qc{p}", [128, 1024], BF16) for p in range(2)]
    Gc2 = [sb(f"Gc{p}", [128, 1024], BF16) for p in range(2)]
    P1c2 = [sb(f"P1c{p}", [128, 1024]) for p in range(2)]
    Afc2 = [sb(f"Afc{p}", [128, 1024]) for p in range(2)]
    IAc2 = [sb(f"IAc{p}", [128, 1024]) for p in range(2)]
    kT2 = [sb(f"kT{p}", [128, 2048], BF16) for p in range(2)]
    y2b = sb("y2b", [128, 2048], BF16)
    brow2 = [sb(f"brow{p}", [1, 1024]) for p in range(2)]
    drobb2 = [sb(f"drobb{p}", [1, 2048], BF16) for p in range(2)]
    erobb2 = [sb(f"erobb{p}", [1, 2048], BF16) for p in range(2)]
    gms2 = [sb(f"gms{p}", [128, 36 * 128], BF16) for p in range(2)]
    hms2 = [sb(f"hms{p}", [128, 36 * 128], BF16) for p in range(2)]
    nETs2 = [sb(f"nETs{p}", [128, 1024]) for p in range(2)]
    abTs2 = [sb(f"abTs{p}", [128, 1024]) for p in range(2)]
    Kord = sb("Kord", [128, 1024], BF16); Qord = sb("Qord", [128, 1024], BF16)
    nEc = sb("nEc", [128, 1024]); nEord = sb("nEord", [128, 1024])
    afod = sb("afod", [128, 1024])
    ptmp = [sb(f"ptmp{i}", [128, 512]) for i in range(2)]
    mbd4 = sb("mbd4", [128, 512], BF16); msl4 = sb("msl4", [128, 512], BF16)
    mtl4 = sb("mtl4", [128, 512], BF16)
    btmp = sb("btmp", [128, 1024])
    nrot = sb("nrot", [128, 1024]); dtot = sb("dtot", [128, 1024]); tmpw = sb("tmpw", [128, 1024])
    ysb = sb("ysb", [128, 1024], BF16)
    obs = sb("obs", [128, 128]); obsq = sb("obsq", [128, 128])
    ons2 = [sb(f"ons{i}", [128, 128]) for i in range(2)]
    besb = sb("besb", [128, 128], BF16)
    cols = sb("cols", [128, 8])
    ofchunk = sb("ofchunk", [128, 1024]); ofb = sb("ofb", [128, 1024], BF16)
    S_B = sb("S_B", [128, 2048]); S_Bb = sb("S_Bb", [128, 2048], BF16)
    eomres = sb("eomres", [128, NCH * NH])
    idb = sb("idb", [128, 128], BF16)
    idf = sb("idf", [128, 128]); nonr = sb("nonr", [1, 128]); onrb = sb("onrb", [1, 128], BF16)
    lnw = sb("lnw", [128, 1]); lnb = sb("lnb", [128, 1]); epst = sb("epst", [128, 1])
    ps_big = ps("ps_big", [128, 1024])
    ps_pair = ps("ps_pair", [128, 512])
    ps_pair2 = ps("ps_pair2", [128, 512])
    ps_sol = ps("ps_sol", [128, 128])
    ps_o2 = [ps(f"ps_o2{p}", [128, 128]) for p in range(2)]
    ps_tr = ps("ps_tr", [128, 128])
    m = MSer(st, nc, ["ld", "prep", "prepg", "main", "out", "ys"])
    hview = lambda ap: ap.rearrange("(h p) t -> p h t", p=128)

    gidx = {}
    for J in range(NSB):
        for i in range(J):
            gidx[(i, J)] = len(gidx)
    gdiag = {J: 28 + J for J in range(NSB)}
    hidx = dict(gidx)
    hdiag = dict(gdiag)

    with st:
        @block.sync
        def _(sync):
            m.dma("ld", sync, idb[:], IDENTB)
            m.dma("ld", sync, idf[:], IDENT); m.dma("ld", sync, nonr[:], NONESR)
            m.dma("ld", sync, onrb[:], ONESRB)
            m.dma("ld", sync, lnw[:], LNW); m.dma("ld", sync, lnb[:], LNB)
            m.dma("ld", sync, eomres[:], eom_s)
            m.dma("ld", sync, mbd4[:], MBD4); m.dma("ld", sync, msl4[:], MSL4)
            m.dma("ld", sync, mtl4[:], MTL4)

        @block.vector
        def _(v):
            m.op("main", v, v.memset, epst[:], EPS, joins=[m.mark("ld")])
            m.op("main", v, v.memset, S_B[:], 0.0)
            m.op("main", v, v.memset, S_Bb[:], 0.0)

        ld_done = [None] * NCH
        prep_done = [None] * NCH
        cons_done = [None] * (NCH + 2)
        ys_done = [None] * (NCH + 1)
        out_evac = {0: ("none", 0), 1: ("none", 0)}
        bank_free = {0: ("none", 0), 1: ("none", 0)}

        def emit_ld(n):
            p = n % 2
            csl = slice(n * CH, (n + 1) * CH)
            joins = [cons_done[n - 2]] if n >= 2 else []

            @block.gpsimd
            def _(gp):
                m.dma("ld", gp, drobb2[p][:], decrows_s[n:n + 1, :], joins=joins)
                m.dma("ld", gp, erobb2[p][:], elcrows_s[n:n + 1, :])

            @block.sync
            def _(sync):
                m.dma("ld", sync, Kc2[p][:].rearrange("p (h t) -> p h t", h=NH),
                      hview(k_s)[:, :, csl])
                m.dma("ld", sync, Qc2[p][:].rearrange("p (h t) -> p h t", h=NH), hview(q_s)[:, :, csl])
                m.dma("ld", sync, Gc2[p][:].rearrange("p (h t) -> p h t", h=NH), hview(g_s)[:, :, csl])
                m.dma("ld", sync, P1c2[p][:].rearrange("p (h t) -> p h t", h=NH), hview(p1_s)[:, :, csl])
                m.dma("ld", sync, Afc2[p][:].rearrange("p (h t) -> p h t", h=NH), hview(af_s)[:, :, csl])
                m.dma("ld", sync, IAc2[p][:].rearrange("p (h t) -> p h t", h=NH), hview(ia_s)[:, :, csl])
                if n % 2 == 0:
                    m.dma("ld", sync, kT2[(n // 2) % 2][:], k_s[:, n * CH:(n + 2) * CH], transpose=True)
                m.dma("ld", sync, brow2[p][:].rearrange("o (h t) -> o h t", h=NH),
                      bta_s.rearrange("(o h) t -> o h t", o=1)[0:1, :, csl])
            @block.gpsimd
            def _(gp):
                m.dma("ld", gp, drobb2[p][:], decrows_s[n:n + 1, :])
                m.dma("ld", gp, erobb2[p][:], elcrows_s[n:n + 1, :])
            ld_done[n] = m.mark("ld")

        def prep_items(n):
            p = n % 2
            items = []
            marks = {}

            def mk_reord(src, dst, hf, first):
                def it():
                    @block.gpsimd
                    def _(gp):
                        m.op("prepg", gp, gp.tensor_copy,
                             dst[:].rearrange("p (s h t) -> p h s t", s=NSB, h=NH)[:, hf * 8:(hf + 1) * 8],
                             src[:].rearrange("p (h s t) -> p h s t", h=NH, s=NSB)[:, hf * 8:(hf + 1) * 8],
                             joins=[ld_done[n]] if first else ())
                    marks["reord"] = m.mark("prepg")
                return it
            for src, dst in ((Kc2[p], Kord), (Qc2[p], Qord), (Afc2[p], afod)):
                for hf in range(2):
                    items.append(mk_reord(src, dst, hf, src is Kc2[p] and hf == 0))

            def it_brow():
                @block.tensor
                def _(t):
                    m.op("prep", t, t.matmul, ps_pair[:], nonr[:], brow2[p][0:1, 0:512],
                         start=True, stop=True, joins=[ld_done[n], bank_free[0]])
                mmk = m.mark("prep")

                @block.vector
                def _(v):
                    m.op("prep", v, v.tensor_copy, btmp[:, 0:512], ps_pair[:], joins=[mmk])
                bank_free[0] = m.mark("prep")

                @block.tensor
                def _(t):
                    m.op("prep", t, t.matmul, ps_pair2[:], nonr[:], brow2[p][0:1, 512:1024],
                         start=True, stop=True, joins=[bank_free[1]])
                mmk2 = m.mark("prep")

                @block.vector
                def _(v):
                    m.op("prep", v, v.tensor_copy, btmp[:, 512:1024], ps_pair2[:], joins=[mmk2])
                bank_free[1] = m.mark("prep")
                marks["btmp"] = bank_free[1]
            items.append(it_brow)

            def mk_nE(hf):
                def it():
                    @block.gpsimd
                    def _(gp):
                        hs = slice(hf * 512, (hf + 1) * 512)
                        m.op("prepg", gp, gp.tensor_tensor, out=nEc[:, hs], in0=IAc2[p][:, hs],
                             in1=btmp[:, hs], op=Op.mult,
                             joins=[marks["btmp"]] if hf == 0 else ())
                        m.op("prepg", gp, gp.tensor_copy,
                             nEord[:].rearrange("p (s h t) -> p h s t", s=NSB, h=NH)[:, hf * 8:(hf + 1) * 8],
                             nEc[:].rearrange("p (h s t) -> p h s t", h=NH, s=NSB)[:, hf * 8:(hf + 1) * 8])
                    marks["nEord"] = m.mark("prepg")
                return it
            items.append(mk_nE(0))
            items.append(mk_nE(1))

            tgroups = [("nE", nETs2[p], 0), ("nE", nETs2[p], 1), ("af", abTs2[p], 0), ("af", abTs2[p], 1)]
            for tg, (which, dst, hf) in enumerate(tgroups):
                def it_tg(which=which, dst=dst, hf=hf, tg=tg):
                    src = nEord if which == "nE" else afod
                    srcmark = marks["nEord"] if which == "nE" else marks["reord"]
                    bank = ps_pair if tg % 2 == 0 else ps_pair2

                    @block.tensor
                    def _(t):
                        for j in range(4):
                            s_ = hf * 4 + j
                            m.op("prep", t, t.transpose, bank[:, j * 128:(j + 1) * 128],
                                 src[:, s_ * 128:(s_ + 1) * 128], idf[:],
                                 joins=[srcmark, bank_free[tg % 2]] if j == 0 else ())
                    tmm = m.mark("prep")

                    @block.vector
                    def _(v):
                        m.op("prep", v, v.tensor_copy, dst[:, hf * 512:(hf + 1) * 512], bank[:], joins=[tmm])
                    bank_free[tg % 2] = m.mark("prep")
                items.append(it_tg)

            pairL = []
            for J in range(NSB):
                for i in range(J):
                    pairL.append((Kord, i, Kord, J, gms2[p], gidx[(i, J)]))
            for J in range(NSB):
                pairL.append((Kord, J, Kord, J, gms2[p], gdiag[J]))
            for J in range(NSB):
                for i in range(J):
                    pairL.append((Kord, i, Qord, J, hms2[p], hidx[(i, J)]))
            for J in range(NSB):
                pairL.append((Kord, J, Qord, J, hms2[p], hdiag[J]))
            mask4s = [mbd4] * 7 + [msl4] * 2 + [mbd4] * 7 + [mtl4] * 2
            for g in range(18):
                def it_pair(g=g):
                    bank = ps_pair if g % 2 == 0 else ps_pair2
                    quad = pairL[g * 4:(g + 1) * 4]

                    @block.tensor
                    def _(t):
                        for j, (sa, si, ma, mi, dstt, di) in enumerate(quad):
                            jn = []
                            if j == 0:
                                jn = [bank_free[g % 2]]
                                if g == 0:
                                    jn.append(marks["reord"])
                            m.op("prep", t, t.matmul, bank[:, j * 128:(j + 1) * 128],
                                 sa[:, si * 128:(si + 1) * 128], ma[:, mi * 128:(mi + 1) * 128],
                                 start=(j == 0), stop=(j == 3), joins=jn)
                    mm_mark = m.mark("prep")

                    @block.vector
                    def _(v):
                        m.op("prep", v, v.tensor_copy, ptmp[g % 2][:], bank[:], joins=[mm_mark])
                    bank_free[g % 2] = m.mark("prep")
                    ptm = bank_free[g % 2]

                    @block.gpsimd
                    def _(gp):
                        msk = mask4s[g]
                        base = pairL[g * 4][5]
                        m.op("prepg", gp, gp.tensor_tensor,
                             out=pairL[g * 4][4][:, base * 128:(base + 4) * 128],
                             in0=ptmp[g % 2][:], in1=msk[:], op=Op.mult, joins=[ptm])
                items.append(it_pair)
            return items

        def emit_main_out(n, feeder):
            p = n % 2
            fi = [0]

            def feed(k=1):
                while k > 0 and feeder and fi[0] < len(feeder):
                    feeder[fi[0]]()
                    fi[0] += 1
                    k -= 1

            for h in range(NH):
                @block.tensor
                def _(t, h=h):
                    m.op("main", t, t.matmul, ps_big[:, h * 64:(h + 1) * 64],
                         S_Bb[:, h * 128:(h + 1) * 128], Kc2[p][:, h * 64:(h + 1) * 64],
                         start=(h % 8 == 0), stop=(h % 8 == 7),
                         joins=[ld_done[n]] if h == 0 else ())

            @block.vector
            def _(v):
                for b in range(2):
                    bs = slice(b * 512, (b + 1) * 512)
                    eosl = eomres[:, n * 16 + b * 8: n * 16 + (b + 1) * 8]
                    m.op("main", v, v.tensor_tensor,
                         out=tmpw[:, bs].rearrange("p (h t) -> p h t", h=8),
                         in0=ps_big[:, bs].rearrange("p (h t) -> p h t", h=8),
                         in1=eosl.broadcast_to([128, 8, 64]), op=Op.mult)
                    m.op("main", v, v.tensor_tensor,
                         out=nrot[:].rearrange("p (s h t) -> p h s t", s=NSB, h=NH)[:, b * 8:(b + 1) * 8],
                         in0=tmpw[:, bs].rearrange("p (h s t) -> p h s t", h=8, s=NSB),
                         in1=P1c2[p][:].rearrange("p (h s t) -> p h s t", h=NH, s=NSB)[:, b * 8:(b + 1) * 8],
                         op=Op.subtract)

            for h in range(NH):
                @block.tensor
                def _(t, h=h):
                    m.op("main", t, t.matmul, ps_big[:, h * 64:(h + 1) * 64],
                         S_Bb[:, h * 128:(h + 1) * 128], Qc2[p][:, h * 64:(h + 1) * 64],
                         start=(h % 8 == 0), stop=(h % 8 == 7))

            @block.vector
            def _(v):
                for b in range(2):
                    bs = slice(b * 512, (b + 1) * 512)
                    eosl = eomres[:, n * 16 + b * 8: n * 16 + (b + 1) * 8]
                    m.op("main", v, v.tensor_tensor,
                         out=dtot[:].rearrange("p (s h t) -> p h s t", s=NSB, h=NH)[:, b * 8:(b + 1) * 8],
                         in0=ps_big[:, bs].rearrange("p (h s t) -> p h s t", h=8, s=NSB),
                         in1=eosl.broadcast_to([128, 8, 64]).rearrange("p h (s t) -> p h s t", s=NSB),
                         op=Op.mult)

            deferred = []

            def emit_deferred():
                if deferred:
                    deferred.pop(0)()

            for J in range(NSB):
                jsl = slice(J * 128, (J + 1) * 128)

                @block.tensor
                def _(t, J=J):
                    jn = list(prep_done[n]) if J == 0 and prep_done[n] else []
                    m.op("main", t, t.transpose, ps_sol[:], nrot[:, jsl], idf[:], joins=jn)
                    for i in range(J):
                        gi = gidx[(i, J)]
                        m.op("main", t, t.matmul, ps_sol[:], gms2[p][:, gi * 128:(gi + 1) * 128],
                             ysb[:, i * 128:(i + 1) * 128], start=False, stop=(i == J - 1))

                @block.vector
                def _(v, J=J):
                    jn = [ys_done[n - 1]] if J == 0 and n >= 1 else []
                    m.op("main", v, v.tensor_tensor, out=ysb[:, jsl], in0=ps_sol[:],
                         in1=nETs2[p][:, jsl], op=Op.mult, joins=jn)

                @block.vector
                def _(v):
                    m.op("main", v, v.tensor_copy, besb[:], ps_sol[:])
                if J >= 2:
                    feed(1)

                gd = gdiag[J]
                for r in range(SBL - 1):
                    @block.tensor
                    def _(t, J=J, gd=gd):
                        m.op("main", t, t.matmul, ps_sol[:], idb[:], besb[:], start=True, stop=False)
                        m.op("main", t, t.matmul, ps_sol[:], gms2[p][:, gd * 128:(gd + 1) * 128],
                             ysb[:, jsl], start=False, stop=True)

                    @block.vector
                    def _(v, J=J):
                        m.op("main", v, v.tensor_tensor, out=ysb[:, jsl], in0=ps_sol[:],
                             in1=nETs2[p][:, jsl], op=Op.mult)
                    if J >= 2:
                        feed(1)
                    elif J == 0 and r == 4 and fi[0] == 0:
                        feed(1)

                @block.sync
                def _(sync, J=J, n=n):
                    rnd = m.mark("main")
                    yout = AP(y_scr.tensor, (n * CH + J * 8) * 2048, [[128, 16], [2048, 8], [1, 128]])
                    m.dma("ys", sync, yout, ysb[:, jsl], joins=[rnd])

                @block.tensor
                def _(t, J=J):
                    m.op("main", t, t.transpose, ps_o2[J % 2][:], dtot[:, jsl], idf[:],
                         joins=[out_evac[J % 2]])
                    for i in range(J + 1):
                        hi = hidx[(i, J)] if i < J else hdiag[J]
                        m.op("main", t, t.matmul, ps_o2[J % 2][:], hms2[p][:, hi * 128:(hi + 1) * 128],
                             ysb[:, i * 128:(i + 1) * 128], start=False, stop=(i == J))
                main_o2 = m.mark("main")

                @block.vector
                def _(v, J=J):
                    m.op("out", v, v.tensor_tensor, out=obs[:], in0=ps_o2[J % 2][:],
                         in1=abTs2[p][:, jsl], op=Op.mult, joins=[main_o2])
                out_evac[J % 2] = m.mark("out")

                @block.gpsimd
                def _(gp):
                    m.op("out", gp, gp.tensor_tensor, out=obsq[:], in0=obs[:], in1=obs[:], op=Op.mult)

                @block.vector
                def _(v):
                    m.op("out", v, v.tensor_reduce, out=cols[:, 0:1], in_=obs[:],
                         axis=mybir.AxisListType.X, op=Op.add)
                    m.op("out", v, v.tensor_reduce, out=cols[:, 1:2], in_=obsq[:],
                         axis=mybir.AxisListType.X, op=Op.add)

                @block.gpsimd
                def _(gp):
                    m.op("out", gp, gp.tensor_scalar, out=cols[:, 0:1], in0=cols[:, 0:1],
                         scalar1=1.0 / DV, scalar2=None, op0=Op.mult)
                    m.op("out", gp, gp.tensor_scalar, out=cols[:, 1:2], in0=cols[:, 1:2],
                         scalar1=1.0 / DV, scalar2=None, op0=Op.mult)
                    m.op("out", gp, gp.tensor_tensor, out=cols[:, 2:3], in0=cols[:, 0:1],
                         in1=cols[:, 0:1], op=Op.mult)
                    m.op("out", gp, gp.tensor_tensor, out=cols[:, 3:4], in0=cols[:, 1:2],
                         in1=cols[:, 2:3], op=Op.subtract)

                @block.scalar
                def _(sc):
                    m.op("out", sc, sc.activation, out=cols[:, 4:5], in_=cols[:, 3:4],
                         func=AF.Sqrt, bias=epst[:])

                @block.vector
                def _(v, J=J):
                    m.op("out", v, v.reciprocal, out=cols[:, 5:6], in_=cols[:, 4:5])
                    m.op("out", v, v.tensor_scalar, out=ons2[J % 2][:], in0=obs[:],
                         scalar1=cols[:, 0:1], scalar2=cols[:, 5:6], op0=Op.subtract, op1=Op.mult)

                def part2(J=J):
                    @block.tensor
                    def _(t):
                        m.op("out", t, t.transpose, ps_tr[:], ons2[J % 2][:], idf[:])

                    @block.vector
                    def _(v):
                        m.op("out", v, v.tensor_scalar,
                             out=ofchunk[:].rearrange("p (h t) -> p h t", h=NH)[:, :, J * 8:(J + 1) * 8],
                             in0=ps_tr[:].rearrange("p (h t) -> p h t", h=NH),
                             scalar1=lnw[:], scalar2=lnb[:], op0=Op.mult, op1=Op.add)
                deferred.append(part2)
                emit_deferred() if J > 0 else None
                if J >= 2:
                    feed(1)

            while deferred:
                emit_deferred()
            ys_done[n] = m.mark("ys")

            @block.vector
            def _(v):
                m.op("out", v, v.tensor_tensor, out=ofb[:], in0=ofchunk[:], in1=Gc2[p][:], op=Op.mult)

            @block.sync
            def _(sync):
                m.dma("out", sync, hview(of_s)[:, :, slice(n * CH, (n + 1) * CH)],
                      ofb[:].rearrange("p (h t) -> p h t", h=NH))

            rb = (n % 2) * 64
            kTq = kT2[(n // 2) % 2]
            gath = [None] * 4

            @block.gpsimd
            def _(gp):
                for g4 in range(4):
                    m.dma("ys", gp, y2b[rb:rb + 64, g4 * 512:(g4 + 1) * 512],
                          y_scr[n * CH:(n + 1) * CH, g4 * 512:(g4 + 1) * 512],
                          joins=[ys_done[n]] if g4 == 0 else ())
                    gath[g4] = m.mark("ys")

            for g4 in range(4):
                gsl = slice(g4 * 512, (g4 + 1) * 512)

                @block.tensor
                def _(t, g4=g4, gsl=gsl):
                    m.op("main", t, t.matmul, ps_big[:, 512:1024], onrb[:],
                         erobb2[p][0:1, gsl], start=True, stop=True)
                bce = m.mark("main")

                @block.vector
                def _(v, g4=g4, gsl=gsl, bce=bce):
                    m.op("main", v, v.tensor_tensor, out=y2b[rb:rb + 64, gsl],
                         in0=y2b[rb:rb + 64, gsl], in1=ps_big[rb:rb + 64, 512:1024],
                         op=Op.mult, joins=[gath[g4], bce])

                @block.tensor
                def _(t, g4=g4, gsl=gsl):
                    for hh in range(4):
                        h = g4 * 4 + hh
                        m.op("main", t, t.matmul, ps_big[:, hh * 128:(hh + 1) * 128],
                             kTq[rb:rb + 64, h * 128:(h + 1) * 128],
                             y2b[rb:rb + 64, h * 128:(h + 1) * 128],
                             start=(hh == 0), stop=(hh == 3))
                    m.op("main", t, t.matmul, ps_big[:, 512:1024], onrb[:],
                         drobb2[p][0:1, gsl], start=True, stop=True)

                @block.vector
                def _(v, g4=g4, gsl=gsl):
                    m.op("main", v, v.tensor_tensor, out=S_B[:, gsl], in0=S_B[:, gsl],
                         in1=ps_big[:, 512:1024], op=Op.mult)
                    m.op("main", v, v.tensor_tensor, out=S_B[:, gsl], in0=S_B[:, gsl],
                         in1=ps_big[:, 0:512], op=Op.add)

            @block.vector
            def _(v):
                m.op("main", v, v.tensor_copy, S_Bb[:], S_B[:])
            cons_done[n] = m.mark("main")
            feed(99)

        emit_ld(0)
        items0 = prep_items(0)
        for it in items0:
            it()
        prep_done[0] = [m.mark("prep"), m.mark("prepg")]
        for n in range(NCH):
            feeder = None
            if n + 1 < NCH:
                emit_ld(n + 1)
                feeder = prep_items(n + 1)
            emit_main_out(n, feeder)
            if n + 1 < NCH:
                prep_done[n + 1] = [m.mark("prep"), m.mark("prepg")]

        for name in ("sync", "tensor", "vector", "scalar", "gpsimd"):
            @getattr(block, name)
            def _(eng):
                for sn in ("ld", "prep", "prepg", "main", "out", "ys"):
                    sm = m.streams[sn]
                    eng.wait_ge(sm.sem, sm.count)
